# version 14
# baseline (speedup 1.0000x reference)
import contextlib
import numpy as np
import concourse.bass as bass
import concourse.mybir as mybir
from concourse.bass_utils import run_bass_kernel_spmd

F32 = mybir.dt.float32
BF16 = mybir.dt.bfloat16
I32 = mybir.dt.int32
I16 = mybir.dt.int16
Alu = mybir.AluOpType
Act = mybir.ActivationFunctionType
AX = mybir.AxisListType

NCORES = 8
QMAX = 32767.0
SHIFT = 32768.0
RSH = 32768.5
WRSH = 8.5
WSHIFT = 8.0
WQMAX = 7.0
WBLK = 32
EPS = 1e-5
ENGS = ("sync", "vector", "scalar", "tensor", "gpsimd")



def _flat2(v):
    if len(v.shape) == 3:
        return v.rearrange("p a b -> p (a b)")
    return v


class Sched:
    def __init__(self):
        self.ops = {e: [] for e in ENGS}
        self.counts = {}
        self.lastwait = {e: {} for e in ENGS}

    def _wait(self, eng, semh, val):
        if val <= 0:
            return
        if self.lastwait[eng].get(semh.name, 0) >= val:
            return
        self.lastwait[eng][semh.name] = val
        self.ops[eng].append(lambda e, s=semh, v=val: e.wait_ge(s, v))

    def run(self, eng, semh, fn, deps=(), inc=1):
        for d in deps:
            if d is not None:
                self._wait(eng, d[0], d[1])
        v = self.counts.get(semh.name, 0) + inc
        self.counts[semh.name] = v
        self.ops[eng].append(lambda e, fn=fn, s=semh, i=inc: fn(e).then_inc(s, i))
        return (semh, v)

    def wait_only(self, eng, evt):
        if evt is not None:
            self._wait(eng, evt[0], evt[1])


def emax(*evts):
    best = {}
    for e in evts:
        if e is None:
            continue
        s, v = e
        if s.name not in best or best[s.name][1] < v:
            best[s.name] = e
    return list(best.values())


def build_program(T, D, NI, NS, debug=False):
    DCH = D // 128
    ICH = NI // 128
    TT = T // 128
    SEG = D // 512
    TB = T // 512
    NI_SH = NI // NCORES
    D_SH = D // NCORES
    assert D % 512 == 0 and T % 512 == 0 and NI % 128 == 0
    assert NI_SH % 128 == 0 and D_SH % 128 == 0 and NS <= 64 and TT % 4 == 0

    nc = bass.Bass("TRN2", target_bir_lowering=False)

    u_in = nc.dram_tensor("u", [T, D], F32, kind="ExternalInput").ap()
    lam_in = nc.dram_tensor("lam", [NS, 1], F32, kind="ExternalInput").ap()
    ldt_in = nc.dram_tensor("ldt", [1, 1], F32, kind="ExternalInput").ap()
    B_in = nc.dram_tensor("Bm", [NS, D], F32, kind="ExternalInput").ap()
    C_in = nc.dram_tensor("Cm", [D, NS], F32, kind="ExternalInput").ap()
    D_in = nc.dram_tensor("Dv", [1, D], F32, kind="ExternalInput").ap()
    sel_in = nc.dram_tensor("sel", [NCORES, 1], F32, kind="ExternalInput").ap()
    wg_in = nc.dram_tensor("wg", [NI_SH, D], F32, kind="ExternalInput").ap()
    wu_in = nc.dram_tensor("wu", [NI_SH, D], F32, kind="ExternalInput").ap()
    wd_in = nc.dram_tensor("wd", [D_SH, NI], F32, kind="ExternalInput").ap()
    out = nc.dram_tensor("out", [T, D], F32, kind="ExternalOutput").ap()

    cc_w_i = nc.dram_tensor("cc_w_i", [1, 8], F32, kind="Internal").ap()
    cc_w_o = nc.dram_tensor("cc_w_o", [1, 8], F32, kind="Internal").ap()
    st_i = nc.dram_tensor("st_i", [1, NS], F32, kind="Internal").ap()
    st_o = nc.dram_tensor("st_o", [NCORES, NS], F32, kind="Internal").ap()
    arb = [(nc.dram_tensor(f"ar{k}_i", [1, 8], F32, kind="Internal").ap(),
            nc.dram_tensor(f"ar{k}_o", [1, 8], F32, kind="Internal").ap())
           for k in range(5)]
    SH_UG = (NI_SH // 128) * DCH * 128 * 128
    SH_D = (D_SH // 128) * ICH * 128 * 128
    dbg_kind = "ExternalOutput" if debug else "Internal"
    wg_sh_d = nc.dram_tensor("wg_sh_d", [1, SH_UG], BF16, kind="Internal").ap()
    wu_sh_d = nc.dram_tensor("wu_sh_d", [1, SH_UG], BF16, kind="Internal").ap()
    wd_sh_d = nc.dram_tensor("wd_sh_d", [1, SH_D], BF16, kind="Internal").ap()
    wg_all = nc.dram_tensor("wg_all", [NCORES, SH_UG], BF16, kind="Internal").ap()
    wu_all = nc.dram_tensor("wu_all", [NCORES, SH_UG], BF16, kind="Internal").ap()
    wd_all = nc.dram_tensor("wd_all", [NCORES, SH_D], BF16, kind="Internal").ap()
    dbg_wu = nc.dram_tensor("dbg_wu", [NCORES, SH_UG], BF16, kind="ExternalOutput").ap() if debug else None
    dbg_wg = nc.dram_tensor("dbg_wg", [NCORES, SH_UG], BF16, kind="ExternalOutput").ap() if debug else None
    dbg_gmm = nc.dram_tensor("dbg_gmm", [128, 4096], BF16, kind="ExternalOutput").ap() if debug else None
    dbg_amx = nc.dram_tensor("dbg_amx", [1, 66], F32, kind="ExternalOutput").ap() if debug else None
    dbg_ar = nc.dram_tensor("dbg_ar", [1, 16], F32, kind="ExternalOutput").ap() if debug else None
    dbg_p3 = nc.dram_tensor("dbg_p3", [128, 24 * 512], BF16, kind="ExternalOutput").ap() if debug else None
    dbg_wd = nc.dram_tensor("dbg_wd", [NCORES, SH_D], BF16, kind="ExternalOutput").ap() if debug else None
    dbg_o = nc.dram_tensor("dbg_o", [128, 1024], F32, kind="ExternalOutput").ap() if debug else None
    u_spill = nc.dram_tensor("u_spill", [ICH, 128, T], BF16, kind=dbg_kind).ap()
    x2_spill = nc.dram_tensor("x2_spill", [T, D], F32, kind=dbg_kind).ap()
    dbg_t = nc.dram_tensor("dbg_t", [1, 32], F32, kind="ExternalOutput").ap() if debug else None
    dbg_c = nc.dram_tensor("dbg_c", [128, 24], F32, kind="ExternalOutput").ap() if debug else None
    dbg_bu = nc.dram_tensor("dbg_bu", [NS, T], F32, kind="ExternalOutput").ap() if debug else None
    dbg_xs = nc.dram_tensor("dbg_xs", [NS, T], F32, kind="ExternalOutput").ap() if debug else None
    dbg_a = nc.dram_tensor("dbg_a", [NS, 4], F32, kind="ExternalOutput").ap() if debug else None
    dbg_z = nc.dram_tensor("dbg_z", [128, 2048], BF16, kind="ExternalOutput").ap() if debug else None
    dbg_x1 = nc.dram_tensor("dbg_x1", [128, 2048], BF16, kind="ExternalOutput").ap() if debug else None
    dbg_g = nc.dram_tensor("dbg_g", [128, T], F32, kind="ExternalOutput").ap() if debug else None
    dbg_xq = nc.dram_tensor("dbg_xq", [128, T], BF16, kind="ExternalOutput").ap() if debug else None
    dbg_xq2 = nc.dram_tensor("dbg_xq2", [128, 1024], BF16, kind="ExternalOutput").ap() if debug else None

    rg = [list(range(NCORES))]
    stack = contextlib.ExitStack()
    sbuf = lambda n, s, d: stack.enter_context(nc.sbuf_tensor(n, s, d))
    psum = lambda n, s: stack.enter_context(nc.psum_tensor(n, s, F32))
    sem = lambda n: stack.enter_context(nc.semaphore(n))

    with stack:
        AUW = max(TT * D, 7 * D + 1024, (DCH * T) // 2 + T + 2 * DCH * 64)
        A_u = sbuf("A_u", [128, AUW], F32)
        FZB = (TT * D) // 2 + (DCH * T) // 2
        AZW = max((ICH * T) // 2, FZB + 6 * D, FZB + D + (TT * D) // 2,
                  (TT * D) // 2 + 3 * T + T // 2)
        A_z = sbuf("A_z", [128, AZW], F32)
        AWW = max(8192, 4 * T)
        A_w = sbuf("A_w", [128, AWW], F32)

        def z_bf(j):
            return _flat2(A_z[:, (j * D) // 2:((j + 1) * D) // 2].bitcast(BF16))

        X1F_W0 = (TT * D) // 2
        def x1f_bf(c):
            w0 = X1F_W0 + (c * T) // 2
            return _flat2(A_z[:, w0:w0 + T // 2].bitcast(BF16))
        Bu_v = A_z[:, X1F_W0:X1F_W0 + T][0:NS, :]
        pw_v = A_z[:, X1F_W0 + T:X1F_W0 + 2 * T][0:NS, :]
        xs_v = A_z[:, X1F_W0 + 2 * T:X1F_W0 + 3 * T][0:NS, :]
        xsb_v =_flat2( A_z[:, X1F_W0 + 3 * T:X1F_W0 + 3 * T + T // 2][0:NS, :].bitcast(BF16))
        FZ = X1F_W0 + (DCH * T) // 2
        x1t = [_flat2(A_z[:, FZ + i * (D // 2):FZ + (i + 1) * (D // 2)].bitcast(BF16)) for i in range(2)]
        FZ2 = FZ + D
        dt_t = [_flat2(A_z[:, FZ2 + i * (D // 2):FZ2 + (i + 1) * (D // 2)].bitcast(BF16)) for i in range(2)]
        FZ3 = FZ2 + D
        zt_f = [A_z[:, FZ3 + i * D:FZ3 + (i + 1) * D] for i in range(2)]
        xqt_v = _flat2(A_z[:, FZ:FZ + D // 2].bitcast(I16))
        XQB0 = FZ + D
        def xqb_full(j):
            return _flat2(A_z[:, XQB0 + (j * D) // 2:XQB0 + ((j + 1) * D) // 2].bitcast(BF16))
        assert XQB0 + (TT * D) // 2 <= AZW

        def gslot_bf(c):
            return _flat2(A_z[:, (c * T) // 2:((c + 1) * T) // 2].bitcast(BF16))

        def gslot_i16(c):
            return _flat2(A_z[:, (c * T) // 2:((c + 1) * T) // 2].bitcast(I16))

        XQF_W = (DCH * T) // 2
        def xqf_bf(c):
            return _flat2(A_u[:, (c * T) // 2:((c + 1) * T) // 2].bitcast(BF16))
        ustg_bf = [_flat2(A_u[:, XQF_W + i * (T // 2):XQF_W + (i + 1) * (T // 2)].bitcast(BF16)) for i in range(2)]
        WST_W0 = XQF_W + T
        wstr_bf = [_flat2(A_u[:, WST_W0 + i * (DCH * 64):WST_W0 + (i + 1) * (DCH * 64)].bitcast(BF16)) for i in range(2)]
        wdstr_bf = [_flat2(A_u[:, i * (D // 2):(i + 1) * (D // 2)].bitcast(BF16)) for i in range(2)]
        RB4_W0 = WST_W0 + 2 * (DCH * 64)
        rb4 = [_flat2(A_u[:, RB4_W0 + i * (T // 2):RB4_W0 + (i + 1) * (T // 2)].bitcast(BF16)) for i in range(4)]
        oqbf_v = [_flat2(A_u[:, D + i * 256:D + (i + 1) * 256].bitcast(BF16)) for i in range(2)]
        oq32_p = [A_u[:, D + 512:D + 1024].bitcast(I32),
                  A_u[:, RB4_W0 + 4 * (T // 2):RB4_W0 + 4 * (T // 2) + 512].bitcast(I32)]
        outst_v = A_u[:, D + 1024:D + 1024 + 4 * D]
        X2RB_W0 = D + 1024 + 4 * D
        x2rb = [A_u[:, X2RB_W0 + i * D:X2RB_W0 + (i + 1) * D] for i in range(2)]
        assert X2RB_W0 + 2 * D <= AUW

        wsh_t = [A_w[:, i * 1536:(i + 1) * 1536] for i in range(2)]
        scr_f = A_w[:, 3072:4608]
        scr_i = A_w[:, 3072:4608].bitcast(I32)
        wq_v =_flat2( A_w[:, 4608:6144].bitcast(BF16))
        wtr_v =_flat2( A_w[:, 6144:7680].bitcast(BF16))
        bmax_v = A_w[:, 7680:7776]
        bscl_v = A_w[:, 7776:7872]
        brcp_v = A_w[:, 7872:7968]
        assert AWW >= 7968
        gq32_p = [A_w[:, 0:T].bitcast(I32), A_w[:, T:2 * T].bitcast(I32)]
        scf0_p = [A_w[:, 2 * T:3 * T], A_w[:, 3 * T:4 * T]]
        scf0i_p = [A_w[:, 2 * T:3 * T].bitcast(I32), A_w[:, 3 * T:4 * T].bitcast(I32)]

        ident_f = sbuf("ident_f", [128, 128], F32)
        ident_b = sbuf("ident_b", [128, 128], BF16)
        iota_t = sbuf("iota_t", [128, 128], F32)
        a_col = sbuf("a_col", [NS, 1], F32)
        e_col = sbuf("e_col", [NS, 1], F32)
        lam_c = sbuf("lam_c", [NS, 1], F32)
        ldt_c = sbuf("ldt_c", [NS, 1], F32)
        zer_c = sbuf("zer_c", [NS, 1], F32)
        BT_bf = sbuf("BT_bf", [128, DCH * NS], BF16)
        CT_bf = sbuf("CT_bf", [NS, D], BF16)
        D_bc = sbuf("D_bc", [128, D], BF16)
        Drow = A_w[:, 0:D]
        selc = sbuf("selc", [NCORES, 1], F32)
        stg = sbuf("stg", [NCORES, NS], F32)
        st_in = sbuf("st_in", [NS, 1], F32)
        stats = sbuf("stats", [128, TT * SEG * 6], F32)
        mvar = sbuf("mvar", [128, TT * 2], F32)
        m1c = sbuf("m1c", [128, TT], F32)
        r1c = sbuf("r1c", [128, TT], F32)
        b1c = sbuf("b1c", [128, TT], F32)
        m2c = sbuf("m2c", [128, TT], F32)
        r2c = sbuf("r2c", [128, TT], F32)
        b2c = sbuf("b2c", [128, TT], F32)
        m3c = sbuf("m3c", [128, TT], F32)
        r3c = sbuf("r3c", [128, TT], F32)
        b3c = sbuf("b3c", [128, TT], F32)
        sqc = sbuf("sqc", [128, TT], F32)
        bqc = sbuf("bqc", [128, TT], F32)
        mx2 = sbuf("mx2", [128, TT], F32)
        mn2 = sbuf("mn2", [128, TT], F32)
        tiny = sbuf("tiny", [128, 24], F32)
        t1r = sbuf("t1r", [1, 32], F32)
        acol = sbuf("acol", [128, 64], F32)
        amx = acol[0:1, 0:64]
        ccsb = sbuf("ccsb", [1, 8], F32)
        ones_r = sbuf("ones_r", [1, 128], F32)
        ln4s = sbuf("ln4s", [128, SEG * 6], F32)
        ln4m = sbuf("ln4m", [128, 8], F32)

        TX_GX, TX_SX, TX_RQ = 0, 1, 2
        TX_AG, TX_SG, TX_FG, TX_SGB = 3, 4, 5, 6
        TX_AG2, TX_SG2, TX_F2 = 7, 8, 9
        TX_AU, TX_SU, TX_FU = 10, 11, 12
        TX_AO, TX_SO, TX_FO, TX_CO = 13, 14, 15, 16
        TX_T0, TX_T1 = 17, 18

        def tc(i):
            return tiny[:, i:i + 1]

        p_tr = [psum(f"p_tr{i}", [128, 512]) for i in range(2)]
        def ptrb(i):
            return _flat2(p_tr[i][:, 0:64].bitcast(BF16))
        ptr_ring = []

        def ring_slot():
            return len(ptr_ring) % 2

        def ring_dep():
            return [ptr_ring[-2]] if len(ptr_ring) >= 2 else []
        p_mm = [psum(f"p_mm{i}", [128, 512]) for i in range(2)]
        p_dn = [psum(f"p_dn{i}", [128, 512]) for i in range(4)]

        sdS = sem("sdS")
        sdG = sem("sdG")
        sdC = sem("sdC")
        sdU = sem("sdU")
        sdSm = sem("sdSm")
        sdW = sem("sdW")
        sdStrm = sem("sdStrm")
        sdRB = sem("sdRB")
        sdOut = sem("sdOut")
        sdSp = sem("sdSp")
        sdStA = sem("sdStA")
        sdStB = sem("sdStB")
        sdStC = sem("sdStC")
        sdWA = sem("sdWA")
        sdWB = sem("sdWB")
        sdWS = sem("sdWS")
        sdRBA = sem("sdRBA")
        sdRBB = sem("sdRBB")
        sdRBC = sem("sdRBC")
        sdRBD = sem("sdRBD")
        sdXR = sem("sdXR")
        sv = sem("sv")
        ss = sem("ss")
        stn = sem("stn")
        sg = sem("sg")
        scc = sem("scc")

        S = Sched()
        DMAS = dict(sync=sdS, gpsimd=sdG, scalar=sdC)

        def dma(q, o, i, deps=(), semq=None, **kw):
            return S.run(q, semq or DMAS[q], lambda e, o=o, i=i, kw=kw: e.dma_start(out=o, in_=i, **kw),
                         deps=deps, inc=16)

        def vop(fn, deps=()):
            return S.run("vector", sv, fn, deps)

        def sop(fn, deps=()):
            return S.run("scalar", ss, fn, deps)

        def top(fn, deps=()):
            return S.run("tensor", stn, fn, deps)

        def gop(fn, deps=()):
            return S.run("gpsimd", sg, fn, deps)

        def cop(fn, deps=()):
            return S.run("gpsimd", scc, fn, deps)

        ev_u = dma("sync", A_u[:, 0:TT * D].rearrange("p (n f) -> p n f", n=TT),
                   u_in.rearrange("(n p) f -> p n f", p=128), semq=sdU)
        ev_lam = dma("sync", lam_c[:], lam_in, semq=sdSm)
        ev_ldt = dma("sync", t1r[0:1, 0:1], ldt_in, semq=sdSm)
        ev_B = dma("sync", Drow[0:NS, :], B_in, semq=sdSm)
        ev_sel = dma("sync", selc[:], sel_in, semq=sdSm)


        ev_iota = gop(lambda e: e.iota(iota_t[:], [[1, 128]], base=0, channel_multiplier=-1,
                                       allow_small_or_imprecise_dtypes=True))
        ev_idf = vop(lambda e: e.tensor_scalar(ident_f[:], iota_t[:], 0.0, None, Alu.is_equal, Alu.bypass),
                     deps=[ev_iota])
        ev_idb = vop(lambda e: e.tensor_scalar(ident_b[:], iota_t[:], 0.0, None, Alu.is_equal, Alu.bypass),
                     deps=[ev_iota])
        ev_z0 = vop(lambda e: e.memset(zer_c[:], 0.0))
        ev_o1 = vop(lambda e: e.memset(ones_r[:], 1.0))
        ev_zr = vop(lambda e: e.memset(ccsb[:], 0.0))
        ev_ccw = dma("sync", cc_w_i[:], ccsb[:], deps=[ev_zr], semq=sdSm)
        ev_warm = cop(lambda e: e.collective_compute("AllReduce", Alu.max, replica_groups=rg,
                                                     ins=[cc_w_i[:]], outs=[cc_w_o[:]]),
                      deps=[ev_ccw])
        ar_zero = [dma("sync", arb[k][0][:], ccsb[:], deps=[ev_zr], semq=sdSm) for k in range(5)]

        def smbar():
            return (sdSm, S.counts.get(sdSm.name, 0))

        ev_ldtm = top(lambda e: e.matmul(p_dn[1][0:NS, 0:1], ones_r[0:1, 0:NS], t1r[0:1, 0:1],
                                         start=True, stop=True), deps=[smbar(), ev_o1])
        ev_ldtb = sop(lambda e: e.activation(ldt_c[:], p_dn[1][0:NS, 0:1], Act.Copy),
                      deps=[ev_ldtm])
        ev_e = sop(lambda e: e.activation(e_col[:], lam_c[:], Act.Exp, bias=ldt_c[:], scale=1.0),
                   deps=[smbar(), ev_ldtb])
        ev_a = sop(lambda e: e.activation(a_col[:], e_col[:], Act.Exp, scale=-1.0), deps=[ev_e])

        ev_bt_e = []
        for c in range(DCH):
            sl = ring_slot()
            evt = top(lambda e, c=c, sl=sl: e.transpose(p_tr[sl][0:128, 0:NS],
                                                        Drow[0:NS, c * 128:(c + 1) * 128],
                                                        ident_f[0:NS, 0:NS]),
                      deps=[smbar(), ev_idf] + ring_dep())
            eve = sop(lambda e, c=c, sl=sl: e.activation(BT_bf[:, c * NS:(c + 1) * NS],
                                                         p_tr[sl][0:128, 0:NS], Act.Copy),
                      deps=[evt])
            ptr_ring.append(eve)
            ev_bt_e.append(eve)
        ev_BT = ev_bt_e[-1]
        ev_C = dma("sync", Drow[:, 0:DCH * NS].rearrange("p (n f) -> p n f", n=DCH),
                   C_in.rearrange("(n p) f -> p n f", p=128), deps=[ev_BT], semq=sdSm)
        ev_ct_e = []
        for c in range(DCH):
            sl = ring_slot()
            evt = top(lambda e, c=c, sl=sl: e.transpose(p_tr[sl][0:NS, 0:128],
                                                        Drow[:, c * NS:(c + 1) * NS], ident_f[:]),
                      deps=[smbar(), ev_idf] + ring_dep())
            eve = sop(lambda e, c=c, sl=sl: e.activation(CT_bf[0:NS, c * 128:(c + 1) * 128],
                                                         p_tr[sl][0:NS, 0:128], Act.Copy),
                      deps=[evt])
            ptr_ring.append(eve)
            ev_ct_e.append(eve)
        ev_CT = ev_ct_e[-1]
        ev_D = dma("sync", Drow[0:1, :], D_in, deps=[ev_CT], semq=sdSm)
        ev_dbc = None
        for hh in range(D // 512):
            ev_dm = top(lambda e, hh=hh: e.matmul(
                p_dn[1][:, 0:512], ones_r[0:1, 0:128], Drow[0:1, hh * 512:(hh + 1) * 512],
                start=True, stop=True),
                deps=[smbar(), ev_o1, ev_ldtb] + ([ev_dbc] if ev_dbc else []))
            ev_dbc = sop(lambda e, hh=hh: e.activation(
                D_bc[:, hh * 512:(hh + 1) * 512], p_dn[1][:, 0:512], Act.Copy), deps=[ev_dm])

        def dequant_and_share(w_inp, rows, cols, sh_dram, all_dram, dep0, dep_wq_free):
            ntile = rows // 128
            CU = min(cols, 1536)
            nsub = cols // CU
            nbu = CU // WBLK
            ev_ld = {}
            ev_q = []
            ev_tr = []
            ev_sd_hist = {}
            last_stt = dep_wq_free
            last_red = None
            k = 0
            for it in range(ntile):
                for su in range(nsub):
                    uidx = it * nsub + su
                    bi = uidx % 2
                    dep_l = [dep0]
                    if uidx >= 2:
                        dep_l.append(ev_q[uidx - 2])
                    ev_ld[uidx] = dma("sync",
                                      wsh_t[bi][:, 0:CU],
                                      w_inp[it * 128:(it + 1) * 128, su * CU:(su + 1) * CU],
                                      deps=dep_l, semq=(sdWA if bi == 0 else sdWB))
                    d_r = [ev_ld[uidx]]
                    if last_red is not None:
                        d_r.append(last_red)
                    e_r = vop(lambda e, bi=bi: e.tensor_reduce(
                        bmax_v[:, 0:nbu], wsh_t[bi][:, 0:CU].rearrange("p (n w) -> p n w", w=WBLK),
                        AX.X, Alu.max, apply_absolute_value=True), deps=d_r)
                    e_s = vop(lambda e: e.tensor_scalar(bscl_v[:, 0:nbu], bmax_v[:, 0:nbu],
                                                        1.0 / WQMAX, 1e-12, Alu.mult, Alu.max),
                              deps=[e_r])
                    e_c = vop(lambda e: e.reciprocal(brcp_v[:, 0:nbu], bscl_v[:, 0:nbu]), deps=[e_s])
                    rv = brcp_v[:, 0:nbu].unsqueeze(2).broadcast_to((128, nbu, WBLK))
                    sv_b = bscl_v[:, 0:nbu].unsqueeze(2).broadcast_to((128, nbu, WBLK))
                    e_i = vop(lambda e, bi=bi, rv=rv: e.tensor_tensor(
                        scr_i[:, 0:CU].rearrange("p (n w) -> p n w", w=WBLK),
                        wsh_t[bi][:, 0:CU].rearrange("p (n w) -> p n w", w=WBLK), rv, Alu.mult),
                        deps=[e_c])
                    d_q = [e_i]
                    if last_stt is not None:
                        d_q.append(last_stt)
                    e_q = vop(lambda e, sv_b=sv_b: e.scalar_tensor_tensor(
                        wq_v[:, 0:CU].rearrange("p (n w) -> p n w", w=WBLK),
                        scr_i[:, 0:CU].rearrange("p (n w) -> p n w", w=WBLK), 1.0,
                        sv_b, Alu.mult, Alu.mult), deps=d_q)
                    ev_q.append(e_q)
                    last_red = e_q
                    unit_tr = []
                    ncc = CU // 128
                    ring0 = (uidx % 2) * 12
                    for cc in range(ncc):
                        sl = ring_slot()
                        dd = [e_q, ev_idb] + ring_dep()
                        if uidx >= 2 and cc == 0:
                            dd.append(ev_sd_hist[uidx - 2])
                        evt = top(lambda e, cc=cc, sl=sl: e.transpose(
                            ptrb(sl), wq_v[:, cc * 128:(cc + 1) * 128], ident_b[:]),
                            deps=dd)
                        eve = sop(lambda e, ring0=ring0, cc=cc, sl=sl: e.activation(
                            wtr_v[:, (ring0 + cc) * 128:(ring0 + cc) * 128 + 128],
                            ptrb(sl), Act.Copy),
                            deps=[evt] + ([ev_sd_hist[uidx - 2]] if uidx >= 2 else []))
                        ptr_ring.append(eve)
                        ev_tr.append((evt, eve))
                        unit_tr.append(eve)
                        k += 1
                    last_stt = unit_tr[-1]
                    dview = sh_dram.rearrange("a (gc it p f) -> p (a gc) it f",
                                              it=ntile, p=128, f=128)
                    ev_sd = dma("sync",
                                dview[:, su * ncc:(su + 1) * ncc, it:it + 1, :],
                                wtr_v[:, ring0 * 128:(ring0 + ncc) * 128].rearrange(
                                    "p (n f) -> p n f", f=128).unsqueeze(2),
                                deps=[unit_tr[-1]], semq=sdWS)
                    ev_sd_hist[uidx] = ev_sd
                    ev_tr_done = ev_sd
            box = {}

            def emit_ag():
                full = (sdWS, S.counts.get(sdWS.name, 0))
                box["ag"] = cop(lambda e: e.collective_compute(
                    "AllGather", Alu.bypass, replica_groups=rg,
                    ins=[sh_dram[:]], outs=[all_dram[:]]),
                    deps=[full, ev_warm])
                return box["ag"]
            return emit_ag, ev_q[-1], last_stt

        st1 = []
        for j in range(TT):
            for s_ in range(SEG):
                st1.append(vop(lambda e, j=j, s_=s_: e.bn_stats(
                    stats[:, (j * SEG + s_) * 6:(j * SEG + s_) * 6 + 6],
                    A_u[:, j * D + s_ * 512:j * D + s_ * 512 + 512]), deps=[ev_u]))
        ev_bs1 = None
        for j in range(TT):
            ev_bs1 = vop(lambda e, j=j: e.bn_aggr(mvar[:, 2 * j:2 * j + 2],
                                                  stats[:, j * SEG * 6:(j + 1) * SEG * 6]),
                         deps=[st1[-1]] + ([ev_bs1] if ev_bs1 else []))

        def ln_coeffs(mc, rc, bc, dep):
            mview = mvar[:].rearrange("p (n two) -> p n two", two=2)
            e1 = vop(lambda e, mc=mc: e.tensor_copy(mc[:, 0:TT].unsqueeze(2), mview[:, :, 0:1]),
                     deps=[dep])
            e2 = vop(lambda e, rc=rc: e.tensor_scalar(
                rc[:, 0:TT].unsqueeze(2), mview[:, :, 1:2],
                EPS, None, Alu.add, Alu.bypass), deps=[dep])
            e3 = sop(lambda e, rc=rc: e.activation(rc[:, 0:TT], rc[:, 0:TT], Act.Sqrt), deps=[e2])
            e4 = vop(lambda e, rc=rc: e.reciprocal(rc[:, 0:TT], rc[:, 0:TT]), deps=[e3])
            e5 = vop(lambda e, mc=mc, rc=rc, bc=bc: e.scalar_tensor_tensor(
                bc[:, 0:TT], mc[:, 0:TT], -1.0, rc[:, 0:TT], Alu.mult, Alu.mult), deps=[e1, e4])
            return e5

        ev_c1 = ln_coeffs(m1c, r1c, b1c, ev_bs1)

        ev_trx = []
        last_x1 = [None, None]
        k = 0
        for j in range(TT):
            dd = [ev_c1, ev_u]
            if last_x1[j % 2] is not None:
                dd.append(last_x1[j % 2])
            e_x1 = vop(lambda e, j=j: e.tensor_scalar(
                x1t[j % 2][:, :], A_u[:, j * D:(j + 1) * D],
                r1c[:, j:j + 1], b1c[:, j:j + 1], Alu.mult, Alu.add), deps=dd)
            grp_last = None
            for c in range(DCH):
                sl = ring_slot()
                dd = [e_x1, ev_idb] + ring_dep()
                evt = top(lambda e, j=j, c=c, sl=sl: e.transpose(
                    ptrb(sl), x1t[j % 2][:, c * 128:(c + 1) * 128], ident_b[:]),
                    deps=dd)
                eve = sop(lambda e, j=j, c=c, sl=sl: e.activation(
                    x1f_bf(c)[:, j * 128:(j + 1) * 128], ptrb(sl), Act.Copy), deps=[evt])
                ptr_ring.append(eve)
                ev_trx.append(eve)
                grp_last = eve
                k += 1
            last_x1[j % 2] = grp_last
        ev_x1f = ev_trx[-1]

        agf_g, ev_dqg, ev_trg_l = dequant_and_share(wg_in, NI_SH, D, wg_sh_d, wg_all, ev_dbc, None)
        agf_u, ev_dqu, ev_tru_l = dequant_and_share(wu_in, NI_SH, D, wu_sh_d, wu_all, ev_dqg, ev_trg_l)

        assert TB <= 4
        ev_bu_m = []
        for b in range(TB):
            dd = emax(ev_x1f, ev_BT)
            ev_m = None
            for c in range(DCH):
                ev_m = top(lambda e, b=b, c=c: e.matmul(
                    p_dn[b][0:NS, 0:512], BT_bf[:, c * NS:(c + 1) * NS],
                    x1f_bf(c)[:, b * 512:(b + 1) * 512],
                    start=(c == 0), stop=(c == DCH - 1)), deps=dd)
                dd = []
            ev_bu_m.append(ev_m)
        ev_bu_e = []
        for b in range(TB):
            eve = sop(lambda e, b=b: e.activation(Bu_v[:, b * 512:(b + 1) * 512],
                                                  p_dn[b][0:NS, 0:512], Act.Copy),
                      deps=[ev_bu_m[-1]])
            ev_bu_e.append(eve)
        ev_bu = ev_bu_e[-1]

        ev_scan = vop(lambda e: e.tensor_tensor_scan(
            xs_v[:, :], a_col[:].broadcast_to((NS, T)), Bu_v[:, :], 0.0, Alu.mult, Alu.add),
            deps=emax(ev_bu, ev_a, ev_x1f))
        ev_pw = vop(lambda e: e.tensor_tensor_scan(
            pw_v[:, :], a_col[:].broadcast_to((NS, T)), zer_c[:].broadcast_to((NS, T)), 1.0,
            Alu.mult, Alu.add), deps=[ev_scan, ev_z0])
        ev_sts = dma("gpsimd", st_i.rearrange("a n -> n a"), xs_v[:, T - 1:T], deps=[ev_scan])
        ev_ag1 = cop(lambda e: e.collective_compute("AllGather", Alu.bypass, replica_groups=rg,
                                                    ins=[st_i[:]], outs=[st_o[:]]),
                     deps=[ev_sts, ev_warm])
        ev_ag_g = agf_g()
        ev_ag_u = agf_u()
        ev_stl = dma("gpsimd", stg[:], st_o[:], deps=[ev_ag1])
        ev_smm = top(lambda e: e.matmul(p_dn[0][0:NS, 0:1], stg[:], selc[:], start=True, stop=True),
                     deps=emax(ev_stl, ev_bu) + [smbar()])
        ev_sti = sop(lambda e: e.activation(st_in[:], p_dn[0][0:NS, 0:1], Act.Copy), deps=[ev_smm])
        ev_xs = vop(lambda e: e.scalar_tensor_tensor(
            xs_v[:, :], pw_v[:, :], st_in[:], xs_v[:, :], Alu.mult, Alu.add),
            deps=[ev_pw, ev_sti])
        ev_xsb = vop(lambda e: e.tensor_copy(xsb_v[:, :], xs_v[:, :]), deps=[ev_xs])
        ev_dxs = dma("sync", dbg_xs[:], xs_v[:, :], deps=[ev_xs]) if debug else None

        if debug:
            ev_d1 = dma("sync", dbg_bu[:], Bu_v[:, :], deps=[ev_bu, ev_scan])
            ev_d2 = dma("sync", dbg_a[:, 0:1], a_col[:], deps=[ev_a], allow_slow_non_contiguous=True)
            ev_d2b = dma("sync", dbg_a[:, 1:2], e_col[:], deps=[ev_a], allow_slow_non_contiguous=True)
            ev_d2c = dma("sync", dbg_a[:, 2:3], ldt_c[:], deps=[ev_a], allow_slow_non_contiguous=True)
            ev_d2d = dma("sync", dbg_a[:, 3:4], lam_c[:], deps=[ev_a], allow_slow_non_contiguous=True)
            ev_d3 = dma("sync", dbg_x1[:, 0:T // 1], x1f_bf(0)[:, :], deps=[ev_x1f])
            for _ev in (ev_d1, ev_d2, ev_d3):
                pass
            dbg_pre_gate = [ev_d1, ev_d3]
        else:
            dbg_pre_gate = []

        ev_zt = []
        last_dt = [None, None]
        for j in range(TT):
            dd = [ev_c1, ev_u]
            if last_dt[j % 2] is not None:
                dd.append(last_dt[j % 2])
            e_x1r = vop(lambda e, j=j: e.tensor_scalar(
                x1t[j % 2][:, :], A_u[:, j * D:(j + 1) * D],
                r1c[:, j:j + 1], b1c[:, j:j + 1], Alu.mult, Alu.add),
                deps=dd + [ev_x1f])
            e_dt = vop(lambda e, j=j: e.tensor_tensor(
                dt_t[j % 2][:, :], x1t[j % 2][:, :], D_bc[:], Alu.mult),
                deps=[e_x1r, ev_dbc])
            for h in range(SEG):
                hb = (j * SEG + h) % 2
                dd2 = emax(ev_xsb, ev_CT) + ([ev_zt[-2]] if len(ev_zt) >= 2 else [])
                e_m = top(lambda e, j=j, h=h, hb=hb: e.matmul(
                    p_mm[hb][:, 0:512], xsb_v[:, j * 128:(j + 1) * 128],
                    CT_bf[0:NS, h * 512:(h + 1) * 512], start=True, stop=True), deps=dd2)
                e_e = vop(lambda e, j=j, h=h, hb=hb: e.tensor_tensor(
                    z_bf(j)[:, h * 512:(h + 1) * 512], p_mm[hb][:, 0:512],
                    dt_t[j % 2][:, h * 512:(h + 1) * 512], Alu.add), deps=[e_m, e_dt])
                ev_zt.append(e_e)
            last_dt[j % 2] = ev_zt[-1]
        ev_z = ev_zt[-1]

        st2 = []
        for j in range(TT):
            for s_ in range(SEG):
                st2.append(vop(lambda e, j=j, s_=s_: e.bn_stats(
                    stats[:, (j * SEG + s_) * 6:(j * SEG + s_) * 6 + 6],
                    z_bf(j)[:, s_ * 512:s_ * 512 + 512]), deps=[ev_z, ev_bs1]))
        ev_bs2 = None
        for j in range(TT):
            ev_bs2 = vop(lambda e, j=j: e.bn_aggr(mvar[:, 2 * j:2 * j + 2],
                                                  stats[:, j * SEG * 6:(j + 1) * SEG * 6]),
                         deps=[st2[-1], ev_c1] + ([ev_bs2] if ev_bs2 else []))
        ev_c2 = ln_coeffs(m2c, r2c, b2c, ev_bs2)

        ev_x2t = []
        last_ztf = [None, None]
        for j in range(TT):
            dd = [ev_c2, ev_z]
            if last_ztf[j % 2] is not None:
                dd.append(last_ztf[j % 2])
            e1 = vop(lambda e, j=j: e.tensor_scalar(
                zt_f[j % 2][:, :], z_bf(j)[:, :], r2c[:, j:j + 1], b2c[:, j:j + 1],
                Alu.mult, Alu.add), deps=dd)
            e2 = vop(lambda e, j=j: e.tensor_tensor(
                A_u[:, j * D:(j + 1) * D], zt_f[j % 2][:, :], A_u[:, j * D:(j + 1) * D],
                Alu.add), deps=[e1])
            e2b = vop(lambda e, j=j: e.tensor_reduce(
                mx2[:, j:j + 1], A_u[:, j * D:(j + 1) * D], AX.X, Alu.max), deps=[e2])
            e3 = vop(lambda e, j=j: e.tensor_reduce(
                mn2[:, j:j + 1], A_u[:, j * D:(j + 1) * D], AX.X, Alu.min), deps=[e2b])
            last_ztf[j % 2] = e3
            ev_x2t.append(e3)
        ev_x2 = ev_x2t[-1]

        st3 = []
        for j in range(TT):
            for s_ in range(SEG):
                st3.append(vop(lambda e, j=j, s_=s_: e.bn_stats(
                    stats[:, (j * SEG + s_) * 6:(j * SEG + s_) * 6 + 6],
                    A_u[:, j * D + s_ * 512:j * D + s_ * 512 + 512]),
                    deps=[ev_x2t[j], ev_bs2]))
        ev_bs3 = None
        for j in range(TT):
            ev_bs3 = vop(lambda e, j=j: e.bn_aggr(mvar[:, 2 * j:2 * j + 2],
                                                  stats[:, j * SEG * 6:(j + 1) * SEG * 6]),
                         deps=[st3[-1], ev_c2] + ([ev_bs3] if ev_bs3 else []))
        ev_c3 = ln_coeffs(m3c, r3c, b3c, ev_bs3)
        agf_d, ev_dqd, ev_trd_l = dequant_and_share(wd_in, D_SH, NI, wd_sh_d, wd_all, ev_dqu, ev_tru_l)
        ev_ag_d = agf_d()

        e1 = vop(lambda e: e.tensor_tensor(mx2[:], mx2[:], m3c[:], Alu.subtract), deps=[ev_c3, ev_x2])
        e2 = vop(lambda e: e.tensor_tensor(mn2[:], m3c[:], mn2[:], Alu.subtract), deps=[e1])
        e3 = vop(lambda e: e.tensor_tensor(mx2[:], mx2[:], mn2[:], Alu.max), deps=[e2])
        e4 = vop(lambda e: e.tensor_tensor(mx2[:], mx2[:], r3c[:], Alu.mult), deps=[e3])
        e5 = vop(lambda e: e.tensor_reduce(tc(TX_GX), mx2[:], AX.X, Alu.max), deps=[e4])
        ev_axl = gop(lambda e: e.tensor_reduce(t1r[0:1, 1:2], tc(TX_GX), AX.XYZWC, Alu.max),
                     deps=[e5])

        _bc_prev = [ev_dbc]

        def allreduce_scalar(kk, src11, dst_col, deps):
            evi = dma("gpsimd", arb[kk][0][0:1, 0:1], src11, deps=list(deps) + [ar_zero[kk]])
            eva = cop(lambda e, kk=kk: e.collective_compute(
                "AllReduce", Alu.max, replica_groups=rg,
                ins=[arb[kk][0][:]], outs=[arb[kk][1][:]]), deps=[evi, ev_warm])
            evo = dma("gpsimd", t1r[0:1, 4 + kk:5 + kk], arb[kk][1][0:1, 0:1], deps=[eva])
            evm = top(lambda e, kk=kk: e.matmul(
                p_dn[1][:, 0:1], ones_r[0:1, 0:128], t1r[0:1, 4 + kk:5 + kk],
                start=True, stop=True), deps=[evo, ev_o1] + list(_bc_prev))
            evb = sop(lambda e, dst_col=dst_col: e.activation(dst_col, p_dn[1][:, 0:1], Act.Copy),
                      deps=[evm])
            _bc_prev.clear()
            _bc_prev.append(evb)
            return evb, evi

        ev_gx, evi0 = allreduce_scalar(0, t1r[0:1, 1:2], tc(TX_GX), [ev_axl, e5])
        e6 = vop(lambda e: e.tensor_scalar(tc(TX_SX), tc(TX_GX), 1.0 / QMAX, 1e-12, Alu.mult, Alu.max),
                 deps=[ev_gx])
        ev_rq = vop(lambda e: e.reciprocal(tc(TX_RQ), tc(TX_SX)), deps=[e6])
        e7 = vop(lambda e: e.tensor_tensor(sqc[:], r3c[:], tc(TX_RQ).broadcast_to((128, TT)), Alu.mult),
                 deps=[ev_rq, ev_c3])
        e8 = vop(lambda e: e.scalar_tensor_tensor(bqc[:], m3c[:], -1.0, sqc[:], Alu.mult, Alu.mult),
                 deps=[e7])
        ev_qc = e8

        ev_xq_e = []
        ev_xq_e_dump = []
        ev_sp = []
        last_q = None
        for j in range(TT):
            dd = [ev_qc, ev_x2t[j]]
            if last_q is not None:
                dd.append(last_q)
            eq1 = vop(lambda e, j=j: e.tensor_scalar(
                xqt_v[:, :], A_u[:, j * D:(j + 1) * D], sqc[:, j:j + 1], bqc[:, j:j + 1],
                Alu.mult, Alu.add), deps=dd)
            dd2 = [eq1] + ev_xq_e_dump
            if j >= 2:
                dd2.append(ev_xq_e[j - 2][1])
            eq2 = sop(lambda e, j=j: e.activation(
                xqb_full(j)[:, :], xqt_v[:, :], Act.Copy), deps=dd2)
            if debug and j == 0:
                ev_dq0 = dma("sync", dbg_xq2[:, 0:D], xqb_full(0)[:, :], deps=[eq2])
                eq2 = emax(eq2)[0]
                ev_xq_e_dump.append(ev_dq0)
            last_q = eq2
            ev_xq_e.append((eq1, eq2))
            ev_sp.append(dma("scalar",
                             x2_spill.rearrange("(n p) f -> p n f", p=128)[:, j:j + 1, :],
                             A_u[:, j * D:(j + 1) * D].unsqueeze(1), deps=[eq1]))
        ev_allq = ev_xq_e[-1][0]
        ev_allsp = ev_sp[-1]
        ev_xtr = []
        for j in range(TT):
            for c in range(DCH):
                sl = ring_slot()
                dd = [ev_xq_e[j][1], ev_idb, ev_allq, ev_allsp] + ring_dep()
                evt = top(lambda e, j=j, c=c, sl=sl: e.transpose(
                    ptrb(sl), xqb_full(j)[:, c * 128:(c + 1) * 128], ident_b[:]),
                    deps=dd)
                eve = sop(lambda e, j=j, c=c, sl=sl: e.activation(
                    xqf_bf(c)[:, j * 128:(j + 1) * 128], ptrb(sl), Act.Copy),
                    deps=[evt, ev_allsp])
                ptr_ring.append(eve)
                ev_xtr.append(eve)
        ev_xqf = ev_xtr[-1]
        if debug:
            ev_dxq = dma("sync", dbg_xq[:], xqf_bf(0)[:, :], deps=[ev_xqf])
            ev_dwu = dma("sync", dbg_wu[:], wu_all[:], deps=[ev_ag_u])
            ev_dwg = dma("sync", dbg_wg[:], wg_all[:], deps=[ev_ag_g])
            ev_dwd = dma("sync", dbg_wd[:], wd_all[:], deps=[ev_ag_d])

        def wstream(all_dram, i, buf_idx, deps):
            ntr = NI_SH // 128
            r = i // ntr
            rc = i % ntr
            src = all_dram[r:r + 1, :].rearrange("a (cc rc p f) -> p a cc rc f",
                                                 cc=DCH, rc=ntr, p=128, f=128)
            return dma("sync",
                       wstr_bf[buf_idx].rearrange("p (cc f) -> p cc f", cc=DCH, f=128).unsqueeze(1).unsqueeze(3),
                       src[:, :, :, rc:rc + 1, :], deps=deps,
                       semq=(sdStA if buf_idx == 0 else sdStB))

        def big_mm(all_dram, ev_ag, evac_fn, extra_first_dep, extra_emit=None):
            ev_ws = [None, None]
            ev_evac = {}
            ev_mm_chunk_last = {}
            prev_chunk_mm = None
            ev_ws[0] = wstream(all_dram, 0, 0, deps=[ev_ag] + extra_first_dep)
            for i in range(ICH):
                if i + 1 < ICH:
                    dd = [ev_ag]
                    if prev_chunk_mm is not None:
                        dd.append(prev_chunk_mm)
                    ev_ws[(i + 1) % 2] = wstream(all_dram, i + 1, (i + 1) % 2, deps=dd)
                ev_m = None
                for b in range(TB):
                    gb = i * TB + b
                    pm = p_mm[gb % 2]
                    dd = emax(ev_ws[i % 2], ev_xqf, ev_evac.get(gb - 2))
                    for h in range(DCH):
                        ev_m = top(lambda e, i=i, b=b, h=h, pm=pm: e.matmul(
                            pm[:, 0:512], wstr_bf[i % 2][:, h * 128:(h + 1) * 128],
                            xqf_bf(h)[:, b * 512:(b + 1) * 512],
                            start=(h == 0), stop=(h == DCH - 1)), deps=dd)
                        dd = []
                    ev_evac[gb] = evac_fn(i, b, pm, ev_m)
                prev_chunk_mm = ev_m
                ev_mm_chunk_last[i] = ev_m
                if extra_emit is not None:
                    extra_emit(i)
            return ev_evac, ev_mm_chunk_last

        ev_dz = dma("sync", dbg_z[:, 0:2048], _flat2(A_z[:, 0:1024].bitcast(BF16)), deps=[ev_z]) if debug else None

        def g_guard(i):
            return [ev_xqf] + dbg_pre_gate + ([ev_dz, ev_dxs] if debug else [])

        ev_gevac = {}

        def gate_evac(i, b, pm, ev_m):
            ev = sop(lambda e, i=i, b=b, pm=pm: e.activation(
                gslot_bf(i)[:, b * 512:(b + 1) * 512], pm[:, 0:512], Act.Copy),
                deps=[ev_m] + g_guard(i))
            ev_gevac[(i, b)] = ev
            return ev

        gevac, gmm_last = big_mm(wg_all, ev_ag_g, gate_evac, [ev_xqf])
        ev_gmm_all = gmm_last[ICH - 1]
        if debug:
            ev_dgmm = dma("sync", dbg_gmm[:, 0:T], gslot_bf(0)[:, :],
                          deps=[gevac[b] for b in range(TB)])
        ev_gam = []
        for i in range(ICH):
            dd = [gevac[i * TB + b] for b in range(TB)]
            ev_gam.append(vop(lambda e, i=i: e.tensor_reduce(
                acol[:, i:i + 1], gslot_bf(i)[:, :], AX.X, Alu.max,
                apply_absolute_value=True), deps=dd))
        ev_ga = gop(lambda e: e.tensor_reduce(t1r[0:1, 2:3], acol[:, 0:ICH], AX.XYZWC, Alu.max),
                    deps=[ev_gam[-1]] + ev_gam + [ev_axl])
        if debug:
            ev_damx = dma("sync", dbg_amx[:, 0:64], amx[:], deps=[ev_ga])
            ev_damx2 = dma("sync", dbg_amx[:, 64:66], t1r[0:1, 2:4], deps=[ev_ga])
        ev_agc, evi1 = allreduce_scalar(1, t1r[0:1, 2:3], tc(TX_AG), [ev_ga] + ([ev_damx2, ev_damx] if debug else []))

        e1 = vop(lambda e: e.tensor_tensor(tc(TX_T0), tc(TX_AG), tc(TX_SX), Alu.mult),
                 deps=[ev_agc, ev_rq])
        e2 = vop(lambda e: e.tensor_scalar(tc(TX_SG), tc(TX_T0), 1.0 / QMAX, 1e-12, Alu.mult, Alu.max),
                 deps=[e1])
        e3 = vop(lambda e: e.reciprocal(tc(TX_T1), tc(TX_SG)), deps=[e2])
        e4 = vop(lambda e: e.tensor_tensor(tc(TX_FG), tc(TX_SX), tc(TX_T1), Alu.mult), deps=[e3])
        ev_sgb = vop(lambda e: e.tensor_scalar(tc(TX_SGB), tc(TX_SG), -SHIFT, None, Alu.mult, Alu.bypass),
                     deps=[e4])

        ev_uevac = {}
        ev_uspill = {}
        ev_uam = []

        def up_evac(i, b, pm, ev_m):
            dd = [ev_m]
            if i >= 2:
                dd.append(ev_uspill[i - 2])
            ev = sop(lambda e, i=i, b=b, pm=pm: e.activation(
                ustg_bf[i % 2][:, b * 512:(b + 1) * 512], pm[:, 0:512], Act.Copy), deps=dd)
            ev_uevac[(i, b)] = ev
            if b == TB - 1:
                ev_uspill[i] = dma("gpsimd", u_spill[i:i + 1, :, :].rearrange("a p f -> p a f"),
                                   ustg_bf[i % 2].unsqueeze(1),
                                   deps=[ev] + [ev_uevac[(i, bb)] for bb in range(TB)], semq=sdSp)
                dd2 = [ev]
                ev_uam.append(vop(lambda e, i=i: e.tensor_reduce(
                    acol[:, 32 + i:32 + i + 1], ustg_bf[i % 2][:, :], AX.X, Alu.max,
                    apply_absolute_value=True), deps=dd2))
            return ev

        ev_p2 = []
        ev_p2am = []
        ev_g2 = []
        _box = {}

        def emit_chain(c):
            p = c % 2
            d0 = [ev_sgb] + [gevac[c * TB + b] for b in range(TB)] + [ev_gam[c]]
            if debug and c == 0:
                d0.append(ev_dgmm)
            d0.append(ev_dqd)
            if len(ev_p2) >= 2:
                d0.append(ev_p2[-2])
            ec1 = sop(lambda e, c=c, p=p: e.activation(
                gq32_p[p][:, :], gslot_bf(c)[:, :], Act.Copy, scale=tc(TX_FG)), deps=d0)
            ec2 = sop(lambda e, p=p: e.activation(
                scf0_p[p][:, :], gq32_p[p][:, :], Act.Sigmoid, scale=tc(TX_SG)),
                deps=[ec1, ev_sgb, ev_trd_l])
            ec3 = vop(lambda e, p=p: e.tensor_scalar(
                scf0i_p[p][:, :], scf0_p[p][:, :], 65536.0, 65535.0, Alu.mult, Alu.min),
                deps=[ec2])
            ec5 = vop(lambda e, c=c, p=p: e.scalar_tensor_tensor(
                gslot_bf(c)[:, :], gq32_p[p][:, :], 1.0, scf0i_p[p][:, :], Alu.mult, Alu.mult),
                deps=[ec3])
            ev_p2.append(ec5)
            dd = [ec5] + ([ev_ga] if c == 0 else [])
            ev_p2am.append(vop(lambda e, c=c: e.tensor_reduce(
                acol[:, c:c + 1], gslot_bf(c)[:, :], AX.X, Alu.max,
                apply_absolute_value=True), deps=dd))

        def emit_ar4():
            ev_p2a = gop(lambda e: e.tensor_reduce(t1r[0:1, 2:3], acol[:, 0:ICH], AX.XYZWC, Alu.max),
                         deps=[ev_p2am[-1]] + ev_p2am + [ev_ga, evi1])
            ev_ag2, _ = allreduce_scalar(2, t1r[0:1, 2:3], tc(TX_AG2), [ev_p2a])
            e1 = vop(lambda e: e.tensor_scalar(tc(TX_T0), tc(TX_SG), 1.0 / 65536.0, None,
                                               Alu.mult, Alu.bypass), deps=[ev_ag2, ev_sgb])
            e2 = vop(lambda e: e.tensor_tensor(tc(TX_T1), tc(TX_AG2), tc(TX_T0), Alu.mult), deps=[e1])
            e3 = vop(lambda e: e.tensor_scalar(tc(TX_SG2), tc(TX_T1), 1.0 / QMAX, 1e-12,
                                               Alu.mult, Alu.max), deps=[e2])
            e4 = vop(lambda e: e.reciprocal(tc(TX_T1), tc(TX_SG2)), deps=[e3])
            _box["f2"] = vop(lambda e: e.tensor_tensor(tc(TX_F2), tc(TX_T0), tc(TX_T1), Alu.mult),
                             deps=[e4])

        def emit_g2(c):
            d0 = [_box["f2"], ev_p2[c], ev_p2am[c]]
            eg2 = sop(lambda e, c=c: e.activation(
                gslot_i16(c)[:, :], gslot_bf(c)[:, :], Act.Copy, scale=tc(TX_F2)),
                deps=d0)
            ev_g2.append(eg2)

        work = ([(emit_chain, c) for c in range(ICH)] + [(emit_ar4, None)] +
                [(emit_g2, c) for c in range(ICH)])

        def up_extra(i):
            budget = 3
            while work and budget > 0:
                fn, arg = work.pop(0)
                fn(arg) if arg is not None else fn()
                budget -= 1

        uevac, umm_last = big_mm(wu_all, ev_ag_u, up_evac, [ev_gmm_all], extra_emit=up_extra)
        while work:
            fn, arg = work.pop(0)
            fn(arg) if arg is not None else fn()
        ev_umm_all = umm_last[ICH - 1]
        ev_ua = gop(lambda e: e.tensor_reduce(t1r[0:1, 3:4], acol[:, 32:32 + ICH], AX.XYZWC, Alu.max),
                    deps=[ev_uam[-1]] + ev_uam + [ev_ga])


        ev_auc, evi3 = allreduce_scalar(3, t1r[0:1, 3:4], tc(TX_AU), [ev_ua])
        e1 = vop(lambda e: e.tensor_tensor(tc(TX_T0), tc(TX_AU), tc(TX_SX), Alu.mult), deps=[ev_auc])
        e2 = vop(lambda e: e.tensor_scalar(tc(TX_SU), tc(TX_T0), 1.0 / QMAX, 1e-12, Alu.mult, Alu.max),
                 deps=[e1])
        e3 = vop(lambda e: e.reciprocal(tc(TX_T1), tc(TX_SU)), deps=[e2])
        ev_fu = vop(lambda e: e.tensor_tensor(tc(TX_FU), tc(TX_SX), tc(TX_T1), Alu.mult), deps=[e3])

        ev_p3 = []
        ev_p3am = []
        sdsp_full = (sdSp, S.counts.get(sdSp.name, 0))
        for c in range(ICH):
            dd = [sdsp_full, ev_umm_all, ev_uam[-1]]
            if c >= 4:
                dd.append(ev_p3[c - 4])
            e_rd = dma("sync", rb4[c % 4].unsqueeze(1),
                       u_spill[c:c + 1, :, :].rearrange("a p f -> p a f"), deps=dd,
                       semq=[sdRBA, sdRBB, sdRBC, sdRBD][c % 4])
            d0 = [ev_fu, e_rd, ev_g2[c]]
            if len(ev_p3) >= 2:
                d0.append(ev_p3[-2])
            ep1 = sop(lambda e, c=c: e.activation(
                gq32_p[c % 2][:, :], rb4[c % 4][:, :], Act.Copy, scale=tc(TX_FU)),
                deps=d0)
            ep3 = vop(lambda e, c=c: e.tensor_tensor(
                gslot_bf(c)[:, :], gq32_p[c % 2][:, :], gslot_i16(c)[:, :], Alu.mult), deps=[ep1])
            ev_p3.append(ep3)
            dd = [ep3] + ([ev_ua] if c == 0 else [])
            ev_p3am.append(vop(lambda e, c=c: e.tensor_reduce(
                acol[:, 32 + c:32 + c + 1], gslot_bf(c)[:, :], AX.X, Alu.max,
                apply_absolute_value=True), deps=dd))
        ev_oa = gop(lambda e: e.tensor_reduce(t1r[0:1, 3:4], acol[:, 32:32 + ICH], AX.XYZWC, Alu.max),
                    deps=[ev_p3am[-1]] + ev_p3am + [ev_ua, evi3])
        if debug:
            ev_dam3 = dma("sync", dbg_amx[:, 32:56] if False else dbg_ar[:, 0:8], amx[:, 32:40], deps=[ev_oa])
            ev_dam4 = dma("sync", dbg_ar[:, 8:16], amx[:, 40:48], deps=[ev_oa])
            ev_dp3 = dma("sync", dbg_p3[:, 0:(ICH * T)], _flat2(A_z[:, 0:(ICH * T) // 2].bitcast(BF16)), deps=[ev_oa])
            ev_dam3 = emax(ev_dam3, ev_dp3)[0]
        ev_aoc, evi4 = allreduce_scalar(4, t1r[0:1, 3:4], tc(TX_AO), [ev_oa] + ([ev_dam3, ev_dam4] if debug else []))
        e1 = vop(lambda e: e.tensor_tensor(tc(TX_CO), tc(TX_SG2), tc(TX_SU), Alu.mult), deps=[ev_aoc])
        e2 = vop(lambda e: e.tensor_tensor(tc(TX_T0), tc(TX_AO), tc(TX_CO), Alu.mult), deps=[e1])
        e3 = vop(lambda e: e.tensor_scalar(tc(TX_SO), tc(TX_T0), 1.0 / QMAX, 1e-12, Alu.mult, Alu.max),
                 deps=[e2])
        e4 = vop(lambda e: e.reciprocal(tc(TX_T1), tc(TX_SO)), deps=[e3])
        ev_fo = vop(lambda e: e.tensor_tensor(tc(TX_FO), tc(TX_CO), tc(TX_T1), Alu.mult), deps=[e4])

        NTG = TT // 4
        bankB = [p_mm[0], p_mm[1], p_tr[0], p_tr[1]]
        ev_outd = []
        ev_wdld = {}
        ev_mm_hist = {}
        prev_oq = None
        prev_oq2 = None
        prev_evacs = None
        for tg in range(NTG):
            ev_dn_last = None
            for i in range(ICH):
                gidx = tg * ICH + i
                if gidx == 0:
                    ev_wdld[0] = dma(
                        "sync", wdstr_bf[0].rearrange("p (r f) -> p r f", r=NCORES),
                        wd_all.rearrange("r (i p f) -> p i r f", p=128, f=128)[:, 0:1, :, :].rearrange("p i r f -> p (i r) f"),
                        deps=[ev_ag_d, ev_umm_all, ev_xqf], semq=sdStA)
                if gidx + 1 < NTG * ICH:
                    i2 = (gidx + 1) % ICH
                    dd = [ev_ag_d]
                    if gidx >= 1:
                        dd.append(ev_wd_prev_mm)
                    ev_wdld[gidx + 1] = dma(
                        "sync" if (gidx + 1) % 2 == 0 else "scalar",
                        wdstr_bf[(gidx + 1) % 2].rearrange("p (r f) -> p r f", r=NCORES),
                        wd_all.rearrange("r (i p f) -> p i r f", p=128, f=128)[:, i2:i2 + 1, :, :].rearrange("p i r f -> p (i r) f"),
                        deps=dd, semq=(sdStA if (gidx + 1) % 2 == 0 else sdStC))
                d0 = [ev_fo, ev_p3[i]]
                if prev_oq2 is not None:
                    d0.append(prev_oq2)
                eo1 = vop(lambda e, i=i, tg=tg, gidx=gidx: e.tensor_scalar(
                    oq32_p[gidx % 2][:, :], gslot_bf(i)[:, tg * 512:(tg + 1) * 512],
                    tc(TX_FO), None, Alu.mult, Alu.bypass), deps=d0 + [ev_p3am[i]])
                d1 = [eo1]
                if gidx >= 2:
                    d1.append(ev_mm_hist[gidx - 2])
                eo2 = sop(lambda e, gidx=gidx: e.activation(
                    oqbf_v[gidx % 2][:, :], oq32_p[gidx % 2][:, :], Act.Copy),
                    deps=d1)
                prev_oq2 = prev_oq
                prev_oq = eo2
                ev_m = None
                for jj in range(4):
                    for hh in range(SEG):
                        pbank = p_dn[jj] if hh == 0 else bankB[jj]
                        dd2 = emax(eo2, ev_wdld[gidx])
                        if i == 0 and prev_evacs is not None:
                            dd2 = emax(eo2, ev_wdld[gidx], prev_evacs)
                        ev_m = top(lambda e, jj=jj, hh=hh, i=i, gidx=gidx, pbank=pbank: e.matmul(
                            pbank[:, 0:512], oqbf_v[gidx % 2][:, jj * 128:(jj + 1) * 128],
                            wdstr_bf[gidx % 2][:, hh * 512:(hh + 1) * 512],
                            start=(i == 0), stop=(i == ICH - 1)), deps=dd2)
                ev_wd_prev_mm = ev_m
                ev_mm_hist[gidx] = ev_m
                ev_dn_last = ev_m
            last_evac = None
            prev_ea = None
            for jj in range(4):
                j = tg * 4 + jj
                dd = [ev_sp[j]]
                if len(ev_outd) >= 2:
                    dd.append(ev_outd[-2])
                e_rb = dma("scalar", x2rb[jj % 2][:, :],
                           x2_spill.rearrange("(n p) f -> p n f", p=128)[:, j:j + 1, :].rearrange("p a f -> p (a f)"),
                           deps=dd + [ev_umm_all, ev_allsp], semq=sdXR)
                evh = []
                for hh in range(SEG):
                    src = p_dn[jj] if hh == 0 else bankB[jj]
                    evh.append(sop(lambda e, jj=jj, hh=hh, src=src: e.activation(
                        outst_v[:, jj * D + hh * 512:jj * D + hh * 512 + 512],
                        src[:, 0:512], Act.Copy, scale=tc(TX_SO)), deps=[ev_dn_last, ev_fo]))
                sevs = []
                for s_ in range(SEG):
                    dds = [evh[-1]]
                    if prev_ea is not None:
                        dds.append(prev_ea)
                    sevs.append(vop(lambda e, jj=jj, s_=s_: e.bn_stats(
                        ln4s[:, s_ * 6:s_ * 6 + 6],
                        outst_v[:, jj * D + s_ * 512:jj * D + s_ * 512 + 512]), deps=dds))
                ea = vop(lambda e: e.bn_aggr(ln4m[:, 0:2], ln4s[:]),
                         deps=[sevs[-1]])
                eb = vop(lambda e: e.tensor_scalar(ln4m[:, 2:3], ln4m[:, 1:2], EPS, None,
                                                   Alu.add, Alu.bypass), deps=[ea])
                ec = sop(lambda e: e.activation(ln4m[:, 2:3], ln4m[:, 2:3], Act.Sqrt), deps=[eb])
                ed = vop(lambda e: e.reciprocal(ln4m[:, 2:3], ln4m[:, 2:3]), deps=[ec])
                ee = vop(lambda e: e.scalar_tensor_tensor(ln4m[:, 3:4], ln4m[:, 0:1], -1.0,
                                                          ln4m[:, 2:3], Alu.mult, Alu.mult),
                         deps=[ed])
                if debug and tg == 0 and jj == 0:
                    ev_do = dma("sync", dbg_o[:, 0:D], outst_v[:, 0:D], deps=[evh[-1]])
                else:
                    ev_do = None
                ef = vop(lambda e, jj=jj: e.tensor_scalar(
                    outst_v[:, jj * D:(jj + 1) * D], outst_v[:, jj * D:(jj + 1) * D],
                    ln4m[:, 2:3], ln4m[:, 3:4], Alu.mult, Alu.add),
                    deps=[ee, evh[-1]] + ([ev_do] if ev_do else []))
                eg = vop(lambda e, jj=jj: e.tensor_tensor(
                    outst_v[:, jj * D:(jj + 1) * D], outst_v[:, jj * D:(jj + 1) * D],
                    x2rb[jj % 2][:, :], Alu.add), deps=[ef, e_rb])
                ev_od = dma("sync", out.rearrange("(n p) f -> p n f", p=128)[:, j:j + 1, :],
                            outst_v[:, jj * D:(jj + 1) * D].unsqueeze(1), deps=[eg], semq=sdOut)
                ev_outd.append(ev_od)
                last_evac = evh[-1]
                prev_ea = ea
            prev_evacs = last_evac

        if debug:
            dd = [ev_outd[-1], ev_fo, ev_xs, ev_bu]
            dma("sync", dbg_t[:], t1r[:], deps=dd)
            dma("sync", dbg_c[:], tiny[:], deps=dd)
            ev_dbg = dma("sync", dbg_g[:, 0:T // 2].bitcast(BF16), gslot_bf(0)[:, :].rearrange("p (a b) -> p a b", b=2) if False else gslot_bf(0)[:, :], deps=dd)
            S.wait_only("sync", ev_dbg)
        S.wait_only("sync", ev_outd[-1])

        with nc.Block() as block:
            @block.sync
            def _(e):
                for fn in S.ops["sync"]:
                    fn(e)

            @block.vector
            def _(e):
                for fn in S.ops["vector"]:
                    fn(e)

            @block.scalar
            def _(e):
                for fn in S.ops["scalar"]:
                    fn(e)

            @block.tensor
            def _(e):
                for fn in S.ops["tensor"]:
                    fn(e)

            @block.gpsimd
            def _(e):
                for fn in S.ops["gpsimd"]:
                    fn(e)

    return nc


_CACHE = {}


def kernel(L0_attn_u, L0_attn_Lambda, L0_attn_log_dt, L0_attn_B, L0_attn_C,
           L0_attn_D, L0_ffn_w_gate, L0_ffn_w_up, L0_ffn_w_down):
    u = np.ascontiguousarray(np.asarray(L0_attn_u, dtype=np.float32))
    Bsz, Sq, Dm = u.shape
    NI = int(L0_ffn_w_gate.shape[0])
    NS = int(L0_attn_Lambda.shape[0])
    T = (Bsz * Sq) // NCORES
    uf = u.reshape(Bsz * Sq, Dm)

    key = (T, Dm, NI, NS)
    if key not in _CACHE:
        _CACHE[key] = build_program(T, Dm, NI, NS)
    nc = _CACHE[key]

    lam = np.ascontiguousarray(np.asarray(L0_attn_Lambda, np.float32).reshape(NS, 1))
    ldt = np.ascontiguousarray(np.asarray(L0_attn_log_dt, np.float32).reshape(1, 1))
    Bm = np.ascontiguousarray(np.asarray(L0_attn_B, np.float32))
    Cm = np.ascontiguousarray(np.asarray(L0_attn_C, np.float32))
    Dv = np.ascontiguousarray(np.asarray(L0_attn_D, np.float32).reshape(1, Dm))
    wg = np.ascontiguousarray(np.asarray(L0_ffn_w_gate, np.float32))
    wu = np.ascontiguousarray(np.asarray(L0_ffn_w_up, np.float32))
    wd = np.ascontiguousarray(np.asarray(L0_ffn_w_down, np.float32))
    NI_SH = NI // NCORES
    D_SH = Dm // NCORES

    in_maps = []
    for c in range(NCORES):
        sel = np.zeros((NCORES, 1), np.float32)
        if c % 2 == 1:
            sel[c - 1, 0] = 1.0
        in_maps.append({
            "u": np.ascontiguousarray(uf[c * T:(c + 1) * T]),
            "lam": lam, "ldt": ldt, "Bm": Bm, "Cm": Cm, "Dv": Dv, "sel": sel,
            "wg": np.ascontiguousarray(wg[c * NI_SH:(c + 1) * NI_SH]),
            "wu": np.ascontiguousarray(wu[c * NI_SH:(c + 1) * NI_SH]),
            "wd": np.ascontiguousarray(wd[c * D_SH:(c + 1) * D_SH]),
        })
    res = run_bass_kernel_spmd(nc, in_maps, core_ids=list(range(NCORES)))
    full = np.concatenate([np.asarray(res.results[c]["out"]) for c in range(NCORES)], axis=0)
    return full.reshape(Bsz, Sq, Dm)


# version 16 (final)
# speedup vs baseline: 1.0270x; 1.0270x over previous
import contextlib
import numpy as np
import concourse.bass as bass
import concourse.mybir as mybir
from concourse.bass_utils import run_bass_kernel_spmd

F32 = mybir.dt.float32
BF16 = mybir.dt.bfloat16
I32 = mybir.dt.int32
I16 = mybir.dt.int16
Alu = mybir.AluOpType
Act = mybir.ActivationFunctionType
AX = mybir.AxisListType

NCORES = 8
QMAX = 32767.0
SHIFT = 32768.0
RSH = 32768.5
WRSH = 8.5
WSHIFT = 8.0
WQMAX = 7.0
WBLK = 32
EPS = 1e-5
ENGS = ("sync", "vector", "scalar", "tensor", "gpsimd")



def _flat2(v):
    if len(v.shape) == 3:
        return v.rearrange("p a b -> p (a b)")
    return v


class Sched:
    def __init__(self):
        self.ops = {e: [] for e in ENGS}
        self.counts = {}
        self.lastwait = {e: {} for e in ENGS}

    def _wait(self, eng, semh, val):
        if val <= 0:
            return
        if self.lastwait[eng].get(semh.name, 0) >= val:
            return
        self.lastwait[eng][semh.name] = val
        self.ops[eng].append(lambda e, s=semh, v=val: e.wait_ge(s, v))

    def run(self, eng, semh, fn, deps=(), inc=1):
        for d in deps:
            if d is not None:
                self._wait(eng, d[0], d[1])
        v = self.counts.get(semh.name, 0) + inc
        self.counts[semh.name] = v
        self.ops[eng].append(lambda e, fn=fn, s=semh, i=inc: fn(e).then_inc(s, i))
        return (semh, v)

    def wait_only(self, eng, evt):
        if evt is not None:
            self._wait(eng, evt[0], evt[1])


def emax(*evts):
    best = {}
    for e in evts:
        if e is None:
            continue
        s, v = e
        if s.name not in best or best[s.name][1] < v:
            best[s.name] = e
    return list(best.values())


def build_program(T, D, NI, NS, debug=False):
    DCH = D // 128
    ICH = NI // 128
    TT = T // 128
    SEG = D // 512
    TB = T // 512
    NI_SH = NI // NCORES
    D_SH = D // NCORES
    assert D % 512 == 0 and T % 512 == 0 and NI % 128 == 0
    assert NI_SH % 128 == 0 and D_SH % 128 == 0 and NS <= 64 and TT % 4 == 0

    nc = bass.Bass("TRN2", target_bir_lowering=False)

    u_in = nc.dram_tensor("u", [T, D], F32, kind="ExternalInput").ap()
    lam_in = nc.dram_tensor("lam", [NS, 1], F32, kind="ExternalInput").ap()
    ldt_in = nc.dram_tensor("ldt", [1, 1], F32, kind="ExternalInput").ap()
    B_in = nc.dram_tensor("Bm", [NS, D], F32, kind="ExternalInput").ap()
    C_in = nc.dram_tensor("Cm", [D, NS], F32, kind="ExternalInput").ap()
    D_in = nc.dram_tensor("Dv", [1, D], F32, kind="ExternalInput").ap()
    sel_in = nc.dram_tensor("sel", [NCORES, 1], F32, kind="ExternalInput").ap()
    wg_in = nc.dram_tensor("wg", [NI_SH, D], F32, kind="ExternalInput").ap()
    wu_in = nc.dram_tensor("wu", [NI_SH, D], F32, kind="ExternalInput").ap()
    wd_in = nc.dram_tensor("wd", [D_SH, NI], F32, kind="ExternalInput").ap()
    out = nc.dram_tensor("out", [T, D], F32, kind="ExternalOutput").ap()

    cc_w_i = nc.dram_tensor("cc_w_i", [1, 8], F32, kind="Internal").ap()
    cc_w_o = nc.dram_tensor("cc_w_o", [1, 8], F32, kind="Internal").ap()
    st_i = nc.dram_tensor("st_i", [1, NS], F32, kind="Internal").ap()
    st_o = nc.dram_tensor("st_o", [NCORES, NS], F32, kind="Internal").ap()
    arb = [(nc.dram_tensor(f"ar{k}_i", [1, 8], F32, kind="Internal").ap(),
            nc.dram_tensor(f"ar{k}_o", [1, 8], F32, kind="Internal").ap())
           for k in range(5)]
    SH_UG = (NI_SH // 128) * DCH * 128 * 128
    SH_D = (D_SH // 128) * ICH * 128 * 128
    dbg_kind = "ExternalOutput" if debug else "Internal"
    wg_sh_d = nc.dram_tensor("wg_sh_d", [1, SH_UG], BF16, kind="Internal").ap()
    wu_sh_d = nc.dram_tensor("wu_sh_d", [1, SH_UG], BF16, kind="Internal").ap()
    wd_sh_d = nc.dram_tensor("wd_sh_d", [1, SH_D], BF16, kind="Internal").ap()
    wg_all = nc.dram_tensor("wg_all", [NCORES, SH_UG], BF16, kind="Internal").ap()
    wu_all = nc.dram_tensor("wu_all", [NCORES, SH_UG], BF16, kind="Internal").ap()
    wd_all = nc.dram_tensor("wd_all", [NCORES, SH_D], BF16, kind="Internal").ap()
    dbg_wu = nc.dram_tensor("dbg_wu", [NCORES, SH_UG], BF16, kind="ExternalOutput").ap() if debug else None
    dbg_wg = nc.dram_tensor("dbg_wg", [NCORES, SH_UG], BF16, kind="ExternalOutput").ap() if debug else None
    dbg_gmm = nc.dram_tensor("dbg_gmm", [128, 4096], BF16, kind="ExternalOutput").ap() if debug else None
    dbg_amx = nc.dram_tensor("dbg_amx", [1, 66], F32, kind="ExternalOutput").ap() if debug else None
    dbg_ar = nc.dram_tensor("dbg_ar", [1, 16], F32, kind="ExternalOutput").ap() if debug else None
    dbg_p3 = nc.dram_tensor("dbg_p3", [128, 24 * 512], BF16, kind="ExternalOutput").ap() if debug else None
    dbg_wd = nc.dram_tensor("dbg_wd", [NCORES, SH_D], BF16, kind="ExternalOutput").ap() if debug else None
    dbg_o = nc.dram_tensor("dbg_o", [128, 1024], F32, kind="ExternalOutput").ap() if debug else None
    u_spill = nc.dram_tensor("u_spill", [ICH, 128, T], BF16, kind=dbg_kind).ap()
    x2_spill = nc.dram_tensor("x2_spill", [T, D], F32, kind=dbg_kind).ap()
    dbg_t = nc.dram_tensor("dbg_t", [1, 32], F32, kind="ExternalOutput").ap() if debug else None
    dbg_c = nc.dram_tensor("dbg_c", [128, 24], F32, kind="ExternalOutput").ap() if debug else None
    dbg_bu = nc.dram_tensor("dbg_bu", [NS, T], F32, kind="ExternalOutput").ap() if debug else None
    dbg_xs = nc.dram_tensor("dbg_xs", [NS, T], F32, kind="ExternalOutput").ap() if debug else None
    dbg_a = nc.dram_tensor("dbg_a", [NS, 4], F32, kind="ExternalOutput").ap() if debug else None
    dbg_z = nc.dram_tensor("dbg_z", [128, 2048], BF16, kind="ExternalOutput").ap() if debug else None
    dbg_x1 = nc.dram_tensor("dbg_x1", [128, 2048], BF16, kind="ExternalOutput").ap() if debug else None
    dbg_g = nc.dram_tensor("dbg_g", [128, T], F32, kind="ExternalOutput").ap() if debug else None
    dbg_xq = nc.dram_tensor("dbg_xq", [128, T], BF16, kind="ExternalOutput").ap() if debug else None
    dbg_xq2 = nc.dram_tensor("dbg_xq2", [128, 1024], BF16, kind="ExternalOutput").ap() if debug else None

    rg = [list(range(NCORES))]
    stack = contextlib.ExitStack()
    sbuf = lambda n, s, d: stack.enter_context(nc.sbuf_tensor(n, s, d))
    psum = lambda n, s: stack.enter_context(nc.psum_tensor(n, s, F32))
    sem = lambda n: stack.enter_context(nc.semaphore(n))

    with stack:
        AUW = max(TT * D, 7 * D + 1024, (DCH * T) // 2 + T + 2 * DCH * 64)
        A_u = sbuf("A_u", [128, AUW], F32)
        FZB = (TT * D) // 2 + (DCH * T) // 2
        AZW = max((ICH * T) // 2, FZB + 6 * D, FZB + D + (TT * D) // 2,
                  (TT * D) // 2 + 3 * T + T // 2)
        A_z = sbuf("A_z", [128, AZW], F32)
        AWW = max(8192, 4 * T)
        A_w = sbuf("A_w", [128, AWW], F32)

        def z_bf(j):
            return _flat2(A_z[:, (j * D) // 2:((j + 1) * D) // 2].bitcast(BF16))

        X1F_W0 = (TT * D) // 2
        def x1f_bf(c):
            w0 = X1F_W0 + (c * T) // 2
            return _flat2(A_z[:, w0:w0 + T // 2].bitcast(BF16))
        Bu_v = A_z[:, X1F_W0:X1F_W0 + T][0:NS, :]
        pw_v = A_z[:, X1F_W0 + T:X1F_W0 + 2 * T][0:NS, :]
        xs_v = A_z[:, X1F_W0 + 2 * T:X1F_W0 + 3 * T][0:NS, :]
        xsb_v =_flat2( A_z[:, X1F_W0 + 3 * T:X1F_W0 + 3 * T + T // 2][0:NS, :].bitcast(BF16))
        FZ = X1F_W0 + (DCH * T) // 2
        x1t = [_flat2(A_z[:, FZ + i * (D // 2):FZ + (i + 1) * (D // 2)].bitcast(BF16)) for i in range(2)]
        FZ2 = FZ + D
        dt_t = [_flat2(A_z[:, FZ2 + i * (D // 2):FZ2 + (i + 1) * (D // 2)].bitcast(BF16)) for i in range(2)]
        FZ3 = FZ2 + D
        zt_f = [A_z[:, FZ3 + i * D:FZ3 + (i + 1) * D] for i in range(2)]
        xqt_v = _flat2(A_z[:, FZ:FZ + D // 2].bitcast(I16))
        XQB0 = FZ + D
        def xqb_full(j):
            return _flat2(A_z[:, XQB0 + (j * D) // 2:XQB0 + ((j + 1) * D) // 2].bitcast(BF16))
        assert XQB0 + (TT * D) // 2 <= AZW

        def gslot_bf(c):
            return _flat2(A_z[:, (c * T) // 2:((c + 1) * T) // 2].bitcast(BF16))

        def gslot_i16(c):
            return _flat2(A_z[:, (c * T) // 2:((c + 1) * T) // 2].bitcast(I16))

        XQF_W = (DCH * T) // 2
        def xqf_bf(c):
            return _flat2(A_u[:, (c * T) // 2:((c + 1) * T) // 2].bitcast(BF16))
        ustg_bf = [_flat2(A_u[:, XQF_W + i * (T // 2):XQF_W + (i + 1) * (T // 2)].bitcast(BF16)) for i in range(2)]
        WST_W0 = XQF_W + T
        wstr_bf = [_flat2(A_u[:, WST_W0 + i * (DCH * 64):WST_W0 + (i + 1) * (DCH * 64)].bitcast(BF16)) for i in range(2)]
        wdstr_bf = [_flat2(A_u[:, i * (D // 2):(i + 1) * (D // 2)].bitcast(BF16)) for i in range(2)]
        RB4_W0 = WST_W0 + 2 * (DCH * 64)
        rb4 = [_flat2(A_u[:, RB4_W0 + i * (T // 2):RB4_W0 + (i + 1) * (T // 2)].bitcast(BF16)) for i in range(4)]
        oqbf_v = [_flat2(A_u[:, D + i * 256:D + (i + 1) * 256].bitcast(BF16)) for i in range(2)]
        oq32_p = [A_u[:, D + 512:D + 1024].bitcast(I32),
                  A_u[:, RB4_W0 + 4 * (T // 2):RB4_W0 + 4 * (T // 2) + 512].bitcast(I32)]
        outst_v = A_u[:, D + 1024:D + 1024 + 4 * D]
        X2RB_W0 = D + 1024 + 4 * D
        x2rb = [A_u[:, X2RB_W0 + i * D:X2RB_W0 + (i + 1) * D] for i in range(2)]
        assert X2RB_W0 + 2 * D <= AUW

        wsh_t = [A_w[:, i * 1536:(i + 1) * 1536] for i in range(2)]
        scr_f = A_w[:, 3072:4608]
        scr_i = A_w[:, 3072:4608].bitcast(I32)
        wq_v =_flat2( A_w[:, 4608:6144].bitcast(BF16))
        wtr_v =_flat2( A_w[:, 6144:7680].bitcast(BF16))
        bmax_v = A_w[:, 7680:7776]
        bscl_v = A_w[:, 7776:7872]
        brcp_v = A_w[:, 7872:7968]
        assert AWW >= 7968
        gq32_p = [A_w[:, 0:T].bitcast(I32), A_w[:, T:2 * T].bitcast(I32)]
        scf0_p = [A_w[:, 2 * T:3 * T], A_w[:, 3 * T:4 * T]]
        scf0i_p = [A_w[:, 2 * T:3 * T].bitcast(I32), A_w[:, 3 * T:4 * T].bitcast(I32)]

        ident_f = sbuf("ident_f", [128, 128], F32)
        ident_b = sbuf("ident_b", [128, 128], BF16)
        iota_t = sbuf("iota_t", [128, 128], F32)
        a_col = sbuf("a_col", [NS, 1], F32)
        e_col = sbuf("e_col", [NS, 1], F32)
        lam_c = sbuf("lam_c", [NS, 1], F32)
        ldt_c = sbuf("ldt_c", [NS, 1], F32)
        zer_c = sbuf("zer_c", [NS, 1], F32)
        BT_bf = sbuf("BT_bf", [128, DCH * NS], BF16)
        CT_bf = sbuf("CT_bf", [NS, D], BF16)
        D_bc = sbuf("D_bc", [128, D], BF16)
        Drow = A_w[:, 0:D]
        selc = sbuf("selc", [NCORES, 1], F32)
        stg = sbuf("stg", [NCORES, NS], F32)
        st_in = sbuf("st_in", [NS, 1], F32)
        stats = sbuf("stats", [128, TT * SEG * 6], F32)
        mvar = sbuf("mvar", [128, TT * 2], F32)
        m1c = sbuf("m1c", [128, TT], F32)
        r1c = sbuf("r1c", [128, TT], F32)
        b1c = sbuf("b1c", [128, TT], F32)
        m2c = sbuf("m2c", [128, TT], F32)
        r2c = sbuf("r2c", [128, TT], F32)
        b2c = sbuf("b2c", [128, TT], F32)
        m3c = sbuf("m3c", [128, TT], F32)
        r3c = sbuf("r3c", [128, TT], F32)
        b3c = sbuf("b3c", [128, TT], F32)
        sqc = sbuf("sqc", [128, TT], F32)
        bqc = sbuf("bqc", [128, TT], F32)
        mx2 = sbuf("mx2", [128, TT], F32)
        mn2 = sbuf("mn2", [128, TT], F32)
        tiny = sbuf("tiny", [128, 24], F32)
        t1r = sbuf("t1r", [1, 32], F32)
        acol = sbuf("acol", [128, 64], F32)
        amx = acol[0:1, 0:64]
        ccsb = sbuf("ccsb", [1, 8], F32)
        ones_r = sbuf("ones_r", [1, 128], F32)
        ln4s = sbuf("ln4s", [128, SEG * 6], F32)
        ln4m = sbuf("ln4m", [128, 8], F32)

        TX_GX, TX_SX, TX_RQ = 0, 1, 2
        TX_AG, TX_SG, TX_FG, TX_SGB = 3, 4, 5, 6
        TX_AG2, TX_SG2, TX_F2 = 7, 8, 9
        TX_AU, TX_SU, TX_FU = 10, 11, 12
        TX_AO, TX_SO, TX_FO, TX_CO = 13, 14, 15, 16
        TX_T0, TX_T1 = 17, 18

        def tc(i):
            return tiny[:, i:i + 1]

        p_tr = [psum(f"p_tr{i}", [128, 512]) for i in range(2)]
        def ptrb(i):
            return _flat2(p_tr[i][:, 0:64].bitcast(BF16))
        ptr_ring = []

        def ring_slot():
            return len(ptr_ring) % 2

        def ring_dep():
            return [ptr_ring[-2]] if len(ptr_ring) >= 2 else []
        p_mm = [psum(f"p_mm{i}", [128, 512]) for i in range(2)]
        p_dn = [psum(f"p_dn{i}", [128, 512]) for i in range(4)]

        sdS = sem("sdS")
        sdG = sem("sdG")
        sdC = sem("sdC")
        sdU = sem("sdU")
        sdSm = sem("sdSm")
        sdW = sem("sdW")
        sdStrm = sem("sdStrm")
        sdRB = sem("sdRB")
        sdOut = sem("sdOut")
        sdSp = sem("sdSp")
        sdStA = sem("sdStA")
        sdStB = sem("sdStB")
        sdStC = sem("sdStC")
        sdWA = sem("sdWA")
        sdWB = sem("sdWB")
        sdWS = sem("sdWS")
        sdRBA = sem("sdRBA")
        sdRBB = sem("sdRBB")
        sdRBC = sem("sdRBC")
        sdRBD = sem("sdRBD")
        sdXR = sem("sdXR")
        sv = sem("sv")
        ss = sem("ss")
        stn = sem("stn")
        sg = sem("sg")
        scc = sem("scc")

        S = Sched()
        DMAS = dict(sync=sdS, gpsimd=sdG, scalar=sdC)

        def dma(q, o, i, deps=(), semq=None, **kw):
            return S.run(q, semq or DMAS[q], lambda e, o=o, i=i, kw=kw: e.dma_start(out=o, in_=i, **kw),
                         deps=deps, inc=16)

        def vop(fn, deps=()):
            return S.run("vector", sv, fn, deps)

        def sop(fn, deps=()):
            return S.run("scalar", ss, fn, deps)

        def top(fn, deps=()):
            return S.run("tensor", stn, fn, deps)

        def gop(fn, deps=()):
            return S.run("gpsimd", sg, fn, deps)

        def cop(fn, deps=()):
            return S.run("gpsimd", scc, fn, deps)

        ev_u = dma("sync", A_u[:, 0:TT * D].rearrange("p (n f) -> p n f", n=TT),
                   u_in.rearrange("(n p) f -> p n f", p=128), semq=sdU)
        ev_lam = dma("sync", lam_c[:], lam_in, semq=sdSm)
        ev_ldt = dma("sync", t1r[0:1, 0:1], ldt_in, semq=sdSm)
        ev_B = dma("sync", Drow[0:NS, :], B_in, semq=sdSm)
        ev_sel = dma("sync", selc[:], sel_in, semq=sdSm)


        ev_iota = gop(lambda e: e.iota(iota_t[:], [[1, 128]], base=0, channel_multiplier=-1,
                                       allow_small_or_imprecise_dtypes=True))
        ev_idf = vop(lambda e: e.tensor_scalar(ident_f[:], iota_t[:], 0.0, None, Alu.is_equal, Alu.bypass),
                     deps=[ev_iota])
        ev_idb = vop(lambda e: e.tensor_scalar(ident_b[:], iota_t[:], 0.0, None, Alu.is_equal, Alu.bypass),
                     deps=[ev_iota])
        ev_z0 = vop(lambda e: e.memset(zer_c[:], 0.0))
        ev_o1 = vop(lambda e: e.memset(ones_r[:], 1.0))
        ev_zr = vop(lambda e: e.memset(ccsb[:], 0.0))
        ev_ccw = dma("sync", cc_w_i[:], ccsb[:], deps=[ev_zr], semq=sdSm)
        ev_warm = cop(lambda e: e.collective_compute("AllReduce", Alu.max, replica_groups=rg,
                                                     ins=[cc_w_i[:]], outs=[cc_w_o[:]]),
                      deps=[ev_ccw])
        ar_zero = [dma("sync", arb[k][0][:], ccsb[:], deps=[ev_zr], semq=sdSm) for k in range(5)]

        def smbar():
            return (sdSm, S.counts.get(sdSm.name, 0))

        ev_ldtm = top(lambda e: e.matmul(p_dn[1][0:NS, 0:1], ones_r[0:1, 0:NS], t1r[0:1, 0:1],
                                         start=True, stop=True), deps=[smbar(), ev_o1])
        ev_ldtb = sop(lambda e: e.activation(ldt_c[:], p_dn[1][0:NS, 0:1], Act.Copy),
                      deps=[ev_ldtm])
        ev_e = sop(lambda e: e.activation(e_col[:], lam_c[:], Act.Exp, bias=ldt_c[:], scale=1.0),
                   deps=[smbar(), ev_ldtb])
        ev_a = sop(lambda e: e.activation(a_col[:], e_col[:], Act.Exp, scale=-1.0), deps=[ev_e])

        ev_bt_e = []
        for c in range(DCH):
            sl = ring_slot()
            evt = top(lambda e, c=c, sl=sl: e.transpose(p_tr[sl][0:128, 0:NS],
                                                        Drow[0:NS, c * 128:(c + 1) * 128],
                                                        ident_f[0:NS, 0:NS]),
                      deps=[smbar(), ev_idf] + ring_dep())
            eve = sop(lambda e, c=c, sl=sl: e.activation(BT_bf[:, c * NS:(c + 1) * NS],
                                                         p_tr[sl][0:128, 0:NS], Act.Copy),
                      deps=[evt])
            ptr_ring.append(eve)
            ev_bt_e.append(eve)
        ev_BT = ev_bt_e[-1]
        ev_C = dma("sync", Drow[:, 0:DCH * NS].rearrange("p (n f) -> p n f", n=DCH),
                   C_in.rearrange("(n p) f -> p n f", p=128), deps=[ev_BT], semq=sdSm)
        ev_ct_e = []
        for c in range(DCH):
            sl = ring_slot()
            evt = top(lambda e, c=c, sl=sl: e.transpose(p_tr[sl][0:NS, 0:128],
                                                        Drow[:, c * NS:(c + 1) * NS], ident_f[:]),
                      deps=[smbar(), ev_idf] + ring_dep())
            eve = sop(lambda e, c=c, sl=sl: e.activation(CT_bf[0:NS, c * 128:(c + 1) * 128],
                                                         p_tr[sl][0:NS, 0:128], Act.Copy),
                      deps=[evt])
            ptr_ring.append(eve)
            ev_ct_e.append(eve)
        ev_CT = ev_ct_e[-1]
        ev_D = dma("sync", Drow[0:1, :], D_in, deps=[ev_CT], semq=sdSm)
        ev_dbc = None
        for hh in range(D // 512):
            ev_dm = top(lambda e, hh=hh: e.matmul(
                p_dn[1][:, 0:512], ones_r[0:1, 0:128], Drow[0:1, hh * 512:(hh + 1) * 512],
                start=True, stop=True),
                deps=[smbar(), ev_o1, ev_ldtb] + ([ev_dbc] if ev_dbc else []))
            ev_dbc = sop(lambda e, hh=hh: e.activation(
                D_bc[:, hh * 512:(hh + 1) * 512], p_dn[1][:, 0:512], Act.Copy), deps=[ev_dm])

        def dequant_and_share(w_inp, rows, cols, sh_dram, all_dram, dep0, dep_wq_free):
            ntile = rows // 128
            CU = min(cols, 1536)
            nsub = cols // CU
            nbu = CU // WBLK
            ev_ld = {}
            ev_q = []
            ev_tr = []
            ev_sd_hist = {}
            last_stt = dep_wq_free
            last_red = None
            k = 0
            for it in range(ntile):
                for su in range(nsub):
                    uidx = it * nsub + su
                    bi = uidx % 2
                    dep_l = [dep0]
                    if uidx >= 2:
                        dep_l.append(ev_q[uidx - 2])
                    ev_ld[uidx] = dma("sync",
                                      wsh_t[bi][:, 0:CU],
                                      w_inp[it * 128:(it + 1) * 128, su * CU:(su + 1) * CU],
                                      deps=dep_l, semq=(sdWA if bi == 0 else sdWB))
                    d_r = [ev_ld[uidx]]
                    if last_red is not None:
                        d_r.append(last_red)
                    e_r = vop(lambda e, bi=bi: e.tensor_reduce(
                        bmax_v[:, 0:nbu], wsh_t[bi][:, 0:CU].rearrange("p (n w) -> p n w", w=WBLK),
                        AX.X, Alu.max, apply_absolute_value=True), deps=d_r)
                    e_s = vop(lambda e: e.tensor_scalar(bscl_v[:, 0:nbu], bmax_v[:, 0:nbu],
                                                        1.0 / WQMAX, 1e-12, Alu.mult, Alu.max),
                              deps=[e_r])
                    e_c = vop(lambda e: e.reciprocal(brcp_v[:, 0:nbu], bscl_v[:, 0:nbu]), deps=[e_s])
                    rv = brcp_v[:, 0:nbu].unsqueeze(2).broadcast_to((128, nbu, WBLK))
                    sv_b = bscl_v[:, 0:nbu].unsqueeze(2).broadcast_to((128, nbu, WBLK))
                    e_i = vop(lambda e, bi=bi, rv=rv: e.tensor_tensor(
                        scr_i[:, 0:CU].rearrange("p (n w) -> p n w", w=WBLK),
                        wsh_t[bi][:, 0:CU].rearrange("p (n w) -> p n w", w=WBLK), rv, Alu.mult),
                        deps=[e_c])
                    d_q = [e_i]
                    if last_stt is not None:
                        d_q.append(last_stt)
                    e_q = vop(lambda e, sv_b=sv_b: e.scalar_tensor_tensor(
                        wq_v[:, 0:CU].rearrange("p (n w) -> p n w", w=WBLK),
                        scr_i[:, 0:CU].rearrange("p (n w) -> p n w", w=WBLK), 1.0,
                        sv_b, Alu.mult, Alu.mult), deps=d_q)
                    ev_q.append(e_q)
                    last_red = e_q
                    unit_tr = []
                    ncc = CU // 128
                    ring0 = (uidx % 2) * 12
                    for cc in range(ncc):
                        sl = ring_slot()
                        dd = [e_q, ev_idb] + ring_dep()
                        if uidx >= 2 and cc == 0:
                            dd.append(ev_sd_hist[uidx - 2])
                        evt = top(lambda e, cc=cc, sl=sl: e.transpose(
                            ptrb(sl), wq_v[:, cc * 128:(cc + 1) * 128], ident_b[:]),
                            deps=dd)
                        eve = sop(lambda e, ring0=ring0, cc=cc, sl=sl: e.activation(
                            wtr_v[:, (ring0 + cc) * 128:(ring0 + cc) * 128 + 128],
                            ptrb(sl), Act.Copy),
                            deps=[evt] + ([ev_sd_hist[uidx - 2]] if uidx >= 2 else []))
                        ptr_ring.append(eve)
                        ev_tr.append((evt, eve))
                        unit_tr.append(eve)
                        k += 1
                    last_stt = unit_tr[-1]
                    dview = sh_dram.rearrange("a (gc it p f) -> p (a gc) it f",
                                              it=ntile, p=128, f=128)
                    ev_sd = dma("sync",
                                dview[:, su * ncc:(su + 1) * ncc, it:it + 1, :],
                                wtr_v[:, ring0 * 128:(ring0 + ncc) * 128].rearrange(
                                    "p (n f) -> p n f", f=128).unsqueeze(2),
                                deps=[unit_tr[-1]], semq=sdWS)
                    ev_sd_hist[uidx] = ev_sd
                    ev_tr_done = ev_sd
            box = {}

            def emit_ag():
                full = (sdWS, S.counts.get(sdWS.name, 0))
                box["ag"] = cop(lambda e: e.collective_compute(
                    "AllGather", Alu.bypass, replica_groups=rg,
                    ins=[sh_dram[:]], outs=[all_dram[:]]),
                    deps=[full, ev_warm])
                return box["ag"]
            return emit_ag, ev_q[-1], last_stt

        st1 = []
        for j in range(TT):
            for s_ in range(SEG):
                st1.append(vop(lambda e, j=j, s_=s_: e.bn_stats(
                    stats[:, (j * SEG + s_) * 6:(j * SEG + s_) * 6 + 6],
                    A_u[:, j * D + s_ * 512:j * D + s_ * 512 + 512]), deps=[ev_u]))
        ev_bs1 = None
        for j in range(TT):
            ev_bs1 = vop(lambda e, j=j: e.bn_aggr(mvar[:, 2 * j:2 * j + 2],
                                                  stats[:, j * SEG * 6:(j + 1) * SEG * 6]),
                         deps=[st1[-1]] + ([ev_bs1] if ev_bs1 else []))

        def ln_coeffs(mc, rc, bc, dep):
            mview = mvar[:].rearrange("p (n two) -> p n two", two=2)
            e1 = vop(lambda e, mc=mc: e.tensor_copy(mc[:, 0:TT].unsqueeze(2), mview[:, :, 0:1]),
                     deps=[dep])
            e2 = vop(lambda e, rc=rc: e.tensor_scalar(
                rc[:, 0:TT].unsqueeze(2), mview[:, :, 1:2],
                EPS, None, Alu.add, Alu.bypass), deps=[dep])
            e3 = sop(lambda e, rc=rc: e.activation(rc[:, 0:TT], rc[:, 0:TT], Act.Sqrt), deps=[e2])
            e4 = vop(lambda e, rc=rc: e.reciprocal(rc[:, 0:TT], rc[:, 0:TT]), deps=[e3])
            e5 = vop(lambda e, mc=mc, rc=rc, bc=bc: e.scalar_tensor_tensor(
                bc[:, 0:TT], mc[:, 0:TT], -1.0, rc[:, 0:TT], Alu.mult, Alu.mult), deps=[e1, e4])
            return e5

        ev_c1 = ln_coeffs(m1c, r1c, b1c, ev_bs1)

        ev_trx = []
        last_x1 = [None, None]
        k = 0
        for j in range(TT):
            dd = [ev_c1, ev_u]
            if last_x1[j % 2] is not None:
                dd.append(last_x1[j % 2])
            e_x1 = vop(lambda e, j=j: e.tensor_scalar(
                x1t[j % 2][:, :], A_u[:, j * D:(j + 1) * D],
                r1c[:, j:j + 1], b1c[:, j:j + 1], Alu.mult, Alu.add), deps=dd)
            grp_last = None
            for c in range(DCH):
                sl = ring_slot()
                dd = [e_x1, ev_idb] + ring_dep()
                evt = top(lambda e, j=j, c=c, sl=sl: e.transpose(
                    ptrb(sl), x1t[j % 2][:, c * 128:(c + 1) * 128], ident_b[:]),
                    deps=dd)
                eve = sop(lambda e, j=j, c=c, sl=sl: e.activation(
                    x1f_bf(c)[:, j * 128:(j + 1) * 128], ptrb(sl), Act.Copy), deps=[evt])
                ptr_ring.append(eve)
                ev_trx.append(eve)
                grp_last = eve
                k += 1
            last_x1[j % 2] = grp_last
        ev_x1f = ev_trx[-1]

        agf_g, ev_dqg, ev_trg_l = dequant_and_share(wg_in, NI_SH, D, wg_sh_d, wg_all, ev_dbc, None)
        agf_u, ev_dqu, ev_tru_l = dequant_and_share(wu_in, NI_SH, D, wu_sh_d, wu_all, ev_dqg, ev_trg_l)
        agf_d, ev_dqd, ev_trd_l = dequant_and_share(wd_in, D_SH, NI, wd_sh_d, wd_all, ev_dqu, ev_tru_l)

        assert TB <= 4
        ev_bu_m = []
        for b in range(TB):
            dd = emax(ev_x1f, ev_BT)
            ev_m = None
            for c in range(DCH):
                ev_m = top(lambda e, b=b, c=c: e.matmul(
                    p_dn[b][0:NS, 0:512], BT_bf[:, c * NS:(c + 1) * NS],
                    x1f_bf(c)[:, b * 512:(b + 1) * 512],
                    start=(c == 0), stop=(c == DCH - 1)), deps=dd)
                dd = []
            ev_bu_m.append(ev_m)
        ev_bu_e = []
        for b in range(TB):
            eve = sop(lambda e, b=b: e.activation(Bu_v[:, b * 512:(b + 1) * 512],
                                                  p_dn[b][0:NS, 0:512], Act.Copy),
                      deps=[ev_bu_m[-1]])
            ev_bu_e.append(eve)
        ev_bu = ev_bu_e[-1]

        ev_scan = vop(lambda e: e.tensor_tensor_scan(
            xs_v[:, :], a_col[:].broadcast_to((NS, T)), Bu_v[:, :], 0.0, Alu.mult, Alu.add),
            deps=emax(ev_bu, ev_a, ev_x1f))
        ev_pw = vop(lambda e: e.tensor_tensor_scan(
            pw_v[:, :], a_col[:].broadcast_to((NS, T)), zer_c[:].broadcast_to((NS, T)), 1.0,
            Alu.mult, Alu.add), deps=[ev_scan, ev_z0])
        ev_sts = dma("gpsimd", st_i.rearrange("a n -> n a"), xs_v[:, T - 1:T], deps=[ev_scan])
        ev_ag1 = cop(lambda e: e.collective_compute("AllGather", Alu.bypass, replica_groups=rg,
                                                    ins=[st_i[:]], outs=[st_o[:]]),
                     deps=[ev_sts, ev_warm])
        ev_ag_g = agf_g()
        ev_ag_u = agf_u()
        ev_stl = dma("gpsimd", stg[:], st_o[:], deps=[ev_ag1])
        ev_smm = top(lambda e: e.matmul(p_dn[0][0:NS, 0:1], stg[:], selc[:], start=True, stop=True),
                     deps=emax(ev_stl, ev_bu) + [smbar()])
        ev_sti = sop(lambda e: e.activation(st_in[:], p_dn[0][0:NS, 0:1], Act.Copy), deps=[ev_smm])
        ev_xs = vop(lambda e: e.scalar_tensor_tensor(
            xs_v[:, :], pw_v[:, :], st_in[:], xs_v[:, :], Alu.mult, Alu.add),
            deps=[ev_pw, ev_sti])
        ev_xsb = vop(lambda e: e.tensor_copy(xsb_v[:, :], xs_v[:, :]), deps=[ev_xs])
        ev_dxs = dma("sync", dbg_xs[:], xs_v[:, :], deps=[ev_xs]) if debug else None

        if debug:
            ev_d1 = dma("sync", dbg_bu[:], Bu_v[:, :], deps=[ev_bu, ev_scan])
            ev_d2 = dma("sync", dbg_a[:, 0:1], a_col[:], deps=[ev_a], allow_slow_non_contiguous=True)
            ev_d2b = dma("sync", dbg_a[:, 1:2], e_col[:], deps=[ev_a], allow_slow_non_contiguous=True)
            ev_d2c = dma("sync", dbg_a[:, 2:3], ldt_c[:], deps=[ev_a], allow_slow_non_contiguous=True)
            ev_d2d = dma("sync", dbg_a[:, 3:4], lam_c[:], deps=[ev_a], allow_slow_non_contiguous=True)
            ev_d3 = dma("sync", dbg_x1[:, 0:T // 1], x1f_bf(0)[:, :], deps=[ev_x1f])
            for _ev in (ev_d1, ev_d2, ev_d3):
                pass
            dbg_pre_gate = [ev_d1, ev_d3]
        else:
            dbg_pre_gate = []

        ev_zt = []
        last_dt = [None, None]
        for j in range(TT):
            dd = [ev_c1, ev_u]
            if last_dt[j % 2] is not None:
                dd.append(last_dt[j % 2])
            e_x1r = vop(lambda e, j=j: e.tensor_scalar(
                x1t[j % 2][:, :], A_u[:, j * D:(j + 1) * D],
                r1c[:, j:j + 1], b1c[:, j:j + 1], Alu.mult, Alu.add),
                deps=dd + [ev_x1f])
            e_dt = vop(lambda e, j=j: e.tensor_tensor(
                dt_t[j % 2][:, :], x1t[j % 2][:, :], D_bc[:], Alu.mult),
                deps=[e_x1r, ev_dbc])
            for h in range(SEG):
                hb = (j * SEG + h) % 2
                dd2 = emax(ev_xsb, ev_CT) + ([ev_zt[-2]] if len(ev_zt) >= 2 else [])
                e_m = top(lambda e, j=j, h=h, hb=hb: e.matmul(
                    p_mm[hb][:, 0:512], xsb_v[:, j * 128:(j + 1) * 128],
                    CT_bf[0:NS, h * 512:(h + 1) * 512], start=True, stop=True), deps=dd2)
                e_e = vop(lambda e, j=j, h=h, hb=hb: e.tensor_tensor(
                    z_bf(j)[:, h * 512:(h + 1) * 512], p_mm[hb][:, 0:512],
                    dt_t[j % 2][:, h * 512:(h + 1) * 512], Alu.add), deps=[e_m, e_dt])
                ev_zt.append(e_e)
            last_dt[j % 2] = ev_zt[-1]
        ev_z = ev_zt[-1]

        st2 = []
        for j in range(TT):
            for s_ in range(SEG):
                st2.append(vop(lambda e, j=j, s_=s_: e.bn_stats(
                    stats[:, (j * SEG + s_) * 6:(j * SEG + s_) * 6 + 6],
                    z_bf(j)[:, s_ * 512:s_ * 512 + 512]), deps=[ev_z, ev_bs1]))
        ev_bs2 = None
        for j in range(TT):
            ev_bs2 = vop(lambda e, j=j: e.bn_aggr(mvar[:, 2 * j:2 * j + 2],
                                                  stats[:, j * SEG * 6:(j + 1) * SEG * 6]),
                         deps=[st2[-1], ev_c1] + ([ev_bs2] if ev_bs2 else []))
        ev_c2 = ln_coeffs(m2c, r2c, b2c, ev_bs2)

        ev_x2t = []
        last_ztf = [None, None]
        for j in range(TT):
            dd = [ev_c2, ev_z]
            if last_ztf[j % 2] is not None:
                dd.append(last_ztf[j % 2])
            e1 = vop(lambda e, j=j: e.tensor_scalar(
                zt_f[j % 2][:, :], z_bf(j)[:, :], r2c[:, j:j + 1], b2c[:, j:j + 1],
                Alu.mult, Alu.add), deps=dd)
            e2 = vop(lambda e, j=j: e.tensor_tensor(
                A_u[:, j * D:(j + 1) * D], zt_f[j % 2][:, :], A_u[:, j * D:(j + 1) * D],
                Alu.add), deps=[e1])
            e2b = vop(lambda e, j=j: e.tensor_reduce(
                mx2[:, j:j + 1], A_u[:, j * D:(j + 1) * D], AX.X, Alu.max), deps=[e2])
            e3 = vop(lambda e, j=j: e.tensor_reduce(
                mn2[:, j:j + 1], A_u[:, j * D:(j + 1) * D], AX.X, Alu.min), deps=[e2b])
            last_ztf[j % 2] = e3
            ev_x2t.append(e3)
        ev_x2 = ev_x2t[-1]

        st3 = []
        for j in range(TT):
            for s_ in range(SEG):
                st3.append(vop(lambda e, j=j, s_=s_: e.bn_stats(
                    stats[:, (j * SEG + s_) * 6:(j * SEG + s_) * 6 + 6],
                    A_u[:, j * D + s_ * 512:j * D + s_ * 512 + 512]),
                    deps=[ev_x2t[j], ev_bs2]))
        ev_bs3 = None
        for j in range(TT):
            ev_bs3 = vop(lambda e, j=j: e.bn_aggr(mvar[:, 2 * j:2 * j + 2],
                                                  stats[:, j * SEG * 6:(j + 1) * SEG * 6]),
                         deps=[st3[-1], ev_c2] + ([ev_bs3] if ev_bs3 else []))
        ev_c3 = ln_coeffs(m3c, r3c, b3c, ev_bs3)
        ev_ag_d = agf_d()

        e1 = vop(lambda e: e.tensor_tensor(mx2[:], mx2[:], m3c[:], Alu.subtract), deps=[ev_c3, ev_x2])
        e2 = vop(lambda e: e.tensor_tensor(mn2[:], m3c[:], mn2[:], Alu.subtract), deps=[e1])
        e3 = vop(lambda e: e.tensor_tensor(mx2[:], mx2[:], mn2[:], Alu.max), deps=[e2])
        e4 = vop(lambda e: e.tensor_tensor(mx2[:], mx2[:], r3c[:], Alu.mult), deps=[e3])
        e5 = vop(lambda e: e.tensor_reduce(tc(TX_GX), mx2[:], AX.X, Alu.max), deps=[e4])
        ev_axl = gop(lambda e: e.tensor_reduce(t1r[0:1, 1:2], tc(TX_GX), AX.XYZWC, Alu.max),
                     deps=[e5])

        _bc_prev = [ev_dbc]

        def allreduce_scalar(kk, src11, dst_col, deps):
            evi = dma("gpsimd", arb[kk][0][0:1, 0:1], src11, deps=list(deps) + [ar_zero[kk]])
            eva = cop(lambda e, kk=kk: e.collective_compute(
                "AllReduce", Alu.max, replica_groups=rg,
                ins=[arb[kk][0][:]], outs=[arb[kk][1][:]]), deps=[evi, ev_warm])
            evo = dma("gpsimd", t1r[0:1, 4 + kk:5 + kk], arb[kk][1][0:1, 0:1], deps=[eva])
            evm = top(lambda e, kk=kk: e.matmul(
                p_dn[1][:, 0:1], ones_r[0:1, 0:128], t1r[0:1, 4 + kk:5 + kk],
                start=True, stop=True), deps=[evo, ev_o1] + list(_bc_prev))
            evb = sop(lambda e, dst_col=dst_col: e.activation(dst_col, p_dn[1][:, 0:1], Act.Copy),
                      deps=[evm])
            _bc_prev.clear()
            _bc_prev.append(evb)
            return evb, evi

        ev_gx, evi0 = allreduce_scalar(0, t1r[0:1, 1:2], tc(TX_GX), [ev_axl, e5])
        e6 = vop(lambda e: e.tensor_scalar(tc(TX_SX), tc(TX_GX), 1.0 / QMAX, 1e-12, Alu.mult, Alu.max),
                 deps=[ev_gx])
        ev_rq = vop(lambda e: e.reciprocal(tc(TX_RQ), tc(TX_SX)), deps=[e6])
        e7 = vop(lambda e: e.tensor_tensor(sqc[:], r3c[:], tc(TX_RQ).broadcast_to((128, TT)), Alu.mult),
                 deps=[ev_rq, ev_c3])
        e8 = vop(lambda e: e.scalar_tensor_tensor(bqc[:], m3c[:], -1.0, sqc[:], Alu.mult, Alu.mult),
                 deps=[e7])
        ev_qc = e8

        ev_xq_e = []
        ev_xq_e_dump = []
        ev_sp = []
        last_q = None
        for j in range(TT):
            dd = [ev_qc, ev_x2t[j]]
            if last_q is not None:
                dd.append(last_q)
            eq1 = vop(lambda e, j=j: e.tensor_scalar(
                xqt_v[:, :], A_u[:, j * D:(j + 1) * D], sqc[:, j:j + 1], bqc[:, j:j + 1],
                Alu.mult, Alu.add), deps=dd)
            dd2 = [eq1] + ev_xq_e_dump
            if j >= 2:
                dd2.append(ev_xq_e[j - 2][1])
            eq2 = sop(lambda e, j=j: e.activation(
                xqb_full(j)[:, :], xqt_v[:, :], Act.Copy), deps=dd2)
            if debug and j == 0:
                ev_dq0 = dma("sync", dbg_xq2[:, 0:D], xqb_full(0)[:, :], deps=[eq2])
                eq2 = emax(eq2)[0]
                ev_xq_e_dump.append(ev_dq0)
            last_q = eq2
            ev_xq_e.append((eq1, eq2))
            ev_sp.append(dma("scalar",
                             x2_spill.rearrange("(n p) f -> p n f", p=128)[:, j:j + 1, :],
                             A_u[:, j * D:(j + 1) * D].unsqueeze(1), deps=[eq1]))
        ev_allq = ev_xq_e[-1][0]
        ev_allsp = ev_sp[-1]
        ev_xtr = []
        for j in range(TT):
            for c in range(DCH):
                sl = ring_slot()
                dd = [ev_xq_e[j][1], ev_idb, ev_allq, ev_allsp] + ring_dep()
                evt = top(lambda e, j=j, c=c, sl=sl: e.transpose(
                    ptrb(sl), xqb_full(j)[:, c * 128:(c + 1) * 128], ident_b[:]),
                    deps=dd)
                eve = sop(lambda e, j=j, c=c, sl=sl: e.activation(
                    xqf_bf(c)[:, j * 128:(j + 1) * 128], ptrb(sl), Act.Copy),
                    deps=[evt, ev_allsp])
                ptr_ring.append(eve)
                ev_xtr.append(eve)
        ev_xqf = ev_xtr[-1]
        if debug:
            ev_dxq = dma("sync", dbg_xq[:], xqf_bf(0)[:, :], deps=[ev_xqf])
            ev_dwu = dma("sync", dbg_wu[:], wu_all[:], deps=[ev_ag_u])
            ev_dwg = dma("sync", dbg_wg[:], wg_all[:], deps=[ev_ag_g])
            ev_dwd = dma("sync", dbg_wd[:], wd_all[:], deps=[ev_ag_d])

        def wstream(all_dram, i, buf_idx, deps):
            ntr = NI_SH // 128
            r = i // ntr
            rc = i % ntr
            src = all_dram[r:r + 1, :].rearrange("a (cc rc p f) -> p a cc rc f",
                                                 cc=DCH, rc=ntr, p=128, f=128)
            return dma("sync",
                       wstr_bf[buf_idx].rearrange("p (cc f) -> p cc f", cc=DCH, f=128).unsqueeze(1).unsqueeze(3),
                       src[:, :, :, rc:rc + 1, :], deps=deps,
                       semq=(sdStA if buf_idx == 0 else sdStB))

        def big_mm(all_dram, ev_ag, evac_fn, extra_first_dep, extra_emit=None):
            ev_ws = [None, None]
            ev_evac = {}
            ev_mm_chunk_last = {}
            prev_chunk_mm = None
            ev_ws[0] = wstream(all_dram, 0, 0, deps=[ev_ag] + extra_first_dep)
            for i in range(ICH):
                if i + 1 < ICH:
                    dd = [ev_ag]
                    if prev_chunk_mm is not None:
                        dd.append(prev_chunk_mm)
                    ev_ws[(i + 1) % 2] = wstream(all_dram, i + 1, (i + 1) % 2, deps=dd)
                ev_m = None
                for b in range(TB):
                    gb = i * TB + b
                    pm = p_mm[gb % 2]
                    dd = emax(ev_ws[i % 2], ev_xqf, ev_evac.get(gb - 2))
                    for h in range(DCH):
                        ev_m = top(lambda e, i=i, b=b, h=h, pm=pm: e.matmul(
                            pm[:, 0:512], wstr_bf[i % 2][:, h * 128:(h + 1) * 128],
                            xqf_bf(h)[:, b * 512:(b + 1) * 512],
                            start=(h == 0), stop=(h == DCH - 1)), deps=dd)
                        dd = []
                    ev_evac[gb] = evac_fn(i, b, pm, ev_m)
                prev_chunk_mm = ev_m
                ev_mm_chunk_last[i] = ev_m
                if extra_emit is not None:
                    extra_emit(i)
            return ev_evac, ev_mm_chunk_last

        ev_dz = dma("sync", dbg_z[:, 0:2048], _flat2(A_z[:, 0:1024].bitcast(BF16)), deps=[ev_z]) if debug else None

        def g_guard(i):
            return [ev_xqf] + dbg_pre_gate + ([ev_dz, ev_dxs] if debug else [])

        ev_gevac = {}

        def gate_evac(i, b, pm, ev_m):
            ev = sop(lambda e, i=i, b=b, pm=pm: e.activation(
                gslot_bf(i)[:, b * 512:(b + 1) * 512], pm[:, 0:512], Act.Copy),
                deps=[ev_m] + g_guard(i))
            ev_gevac[(i, b)] = ev
            return ev

        gevac, gmm_last = big_mm(wg_all, ev_ag_g, gate_evac, [ev_xqf])
        ev_gmm_all = gmm_last[ICH - 1]
        if debug:
            ev_dgmm = dma("sync", dbg_gmm[:, 0:T], gslot_bf(0)[:, :],
                          deps=[gevac[b] for b in range(TB)])
        ev_gam = []
        for i in range(ICH):
            dd = [gevac[i * TB + b] for b in range(TB)]
            ev_gam.append(vop(lambda e, i=i: e.tensor_reduce(
                acol[:, i:i + 1], gslot_bf(i)[:, :], AX.X, Alu.max,
                apply_absolute_value=True), deps=dd))
        ev_ga = gop(lambda e: e.tensor_reduce(t1r[0:1, 2:3], acol[:, 0:ICH], AX.XYZWC, Alu.max),
                    deps=[ev_gam[-1]] + ev_gam + [ev_axl])
        if debug:
            ev_damx = dma("sync", dbg_amx[:, 0:64], amx[:], deps=[ev_ga])
            ev_damx2 = dma("sync", dbg_amx[:, 64:66], t1r[0:1, 2:4], deps=[ev_ga])
        ev_agc, evi1 = allreduce_scalar(1, t1r[0:1, 2:3], tc(TX_AG), [ev_ga] + ([ev_damx2, ev_damx] if debug else []))

        e1 = vop(lambda e: e.tensor_tensor(tc(TX_T0), tc(TX_AG), tc(TX_SX), Alu.mult),
                 deps=[ev_agc, ev_rq])
        e2 = vop(lambda e: e.tensor_scalar(tc(TX_SG), tc(TX_T0), 1.0 / QMAX, 1e-12, Alu.mult, Alu.max),
                 deps=[e1])
        e3 = vop(lambda e: e.reciprocal(tc(TX_T1), tc(TX_SG)), deps=[e2])
        e4 = vop(lambda e: e.tensor_tensor(tc(TX_FG), tc(TX_SX), tc(TX_T1), Alu.mult), deps=[e3])
        ev_sgb = vop(lambda e: e.tensor_scalar(tc(TX_SGB), tc(TX_SG), -SHIFT, None, Alu.mult, Alu.bypass),
                     deps=[e4])

        ev_uevac = {}
        ev_uspill = {}
        ev_uam = []

        def up_evac(i, b, pm, ev_m):
            dd = [ev_m]
            if i >= 2:
                dd.append(ev_uspill[i - 2])
            ev = sop(lambda e, i=i, b=b, pm=pm: e.activation(
                ustg_bf[i % 2][:, b * 512:(b + 1) * 512], pm[:, 0:512], Act.Copy), deps=dd)
            ev_uevac[(i, b)] = ev
            if b == TB - 1:
                ev_uspill[i] = dma("gpsimd", u_spill[i:i + 1, :, :].rearrange("a p f -> p a f"),
                                   ustg_bf[i % 2].unsqueeze(1),
                                   deps=[ev] + [ev_uevac[(i, bb)] for bb in range(TB)], semq=sdSp)
                dd2 = [ev]
                ev_uam.append(vop(lambda e, i=i: e.tensor_reduce(
                    acol[:, 32 + i:32 + i + 1], ustg_bf[i % 2][:, :], AX.X, Alu.max,
                    apply_absolute_value=True), deps=dd2))
            return ev

        ev_p2 = []
        ev_p2am = []
        ev_g2 = []
        _box = {}

        def emit_chain(c):
            p = c % 2
            d0 = [ev_sgb] + [gevac[c * TB + b] for b in range(TB)] + [ev_gam[c]]
            if debug and c == 0:
                d0.append(ev_dgmm)
            d0.append(ev_dqd)
            if len(ev_p2) >= 2:
                d0.append(ev_p2[-2])
            ec1 = sop(lambda e, c=c, p=p: e.activation(
                gq32_p[p][:, :], gslot_bf(c)[:, :], Act.Copy, scale=tc(TX_FG)), deps=d0)
            ec2 = sop(lambda e, p=p: e.activation(
                scf0_p[p][:, :], gq32_p[p][:, :], Act.Sigmoid, scale=tc(TX_SG)),
                deps=[ec1, ev_sgb, ev_trd_l])
            ec3 = vop(lambda e, p=p: e.tensor_scalar(
                scf0i_p[p][:, :], scf0_p[p][:, :], 65536.0, 65535.0, Alu.mult, Alu.min),
                deps=[ec2])
            ec5 = vop(lambda e, c=c, p=p: e.scalar_tensor_tensor(
                gslot_bf(c)[:, :], gq32_p[p][:, :], 1.0, scf0i_p[p][:, :], Alu.mult, Alu.mult),
                deps=[ec3])
            ev_p2.append(ec5)
            dd = [ec5] + ([ev_ga] if c == 0 else [])
            ev_p2am.append(vop(lambda e, c=c: e.tensor_reduce(
                acol[:, c:c + 1], gslot_bf(c)[:, :], AX.X, Alu.max,
                apply_absolute_value=True), deps=dd))

        def emit_ar4():
            ev_p2a = gop(lambda e: e.tensor_reduce(t1r[0:1, 2:3], acol[:, 0:ICH], AX.XYZWC, Alu.max),
                         deps=[ev_p2am[-1]] + ev_p2am + [ev_ga, evi1])
            ev_ag2, _ = allreduce_scalar(2, t1r[0:1, 2:3], tc(TX_AG2), [ev_p2a])
            e1 = vop(lambda e: e.tensor_scalar(tc(TX_T0), tc(TX_SG), 1.0 / 65536.0, None,
                                               Alu.mult, Alu.bypass), deps=[ev_ag2, ev_sgb])
            e2 = vop(lambda e: e.tensor_tensor(tc(TX_T1), tc(TX_AG2), tc(TX_T0), Alu.mult), deps=[e1])
            e3 = vop(lambda e: e.tensor_scalar(tc(TX_SG2), tc(TX_T1), 1.0 / QMAX, 1e-12,
                                               Alu.mult, Alu.max), deps=[e2])
            e4 = vop(lambda e: e.reciprocal(tc(TX_T1), tc(TX_SG2)), deps=[e3])
            _box["f2"] = vop(lambda e: e.tensor_tensor(tc(TX_F2), tc(TX_T0), tc(TX_T1), Alu.mult),
                             deps=[e4])

        def emit_g2(c):
            d0 = [_box["f2"], ev_p2[c], ev_p2am[c]]
            eg2 = sop(lambda e, c=c: e.activation(
                gslot_i16(c)[:, :], gslot_bf(c)[:, :], Act.Copy, scale=tc(TX_F2)),
                deps=d0)
            ev_g2.append(eg2)

        work = ([(emit_chain, c) for c in range(ICH)] + [(emit_ar4, None)] +
                [(emit_g2, c) for c in range(ICH)])

        def up_extra(i):
            budget = 3
            while work and budget > 0:
                fn, arg = work.pop(0)
                fn(arg) if arg is not None else fn()
                budget -= 1

        uevac, umm_last = big_mm(wu_all, ev_ag_u, up_evac, [ev_gmm_all], extra_emit=up_extra)
        while work:
            fn, arg = work.pop(0)
            fn(arg) if arg is not None else fn()
        ev_umm_all = umm_last[ICH - 1]
        ev_ua = gop(lambda e: e.tensor_reduce(t1r[0:1, 3:4], acol[:, 32:32 + ICH], AX.XYZWC, Alu.max),
                    deps=[ev_uam[-1]] + ev_uam + [ev_ga])


        ev_auc, evi3 = allreduce_scalar(3, t1r[0:1, 3:4], tc(TX_AU), [ev_ua])
        e1 = vop(lambda e: e.tensor_tensor(tc(TX_T0), tc(TX_AU), tc(TX_SX), Alu.mult), deps=[ev_auc])
        e2 = vop(lambda e: e.tensor_scalar(tc(TX_SU), tc(TX_T0), 1.0 / QMAX, 1e-12, Alu.mult, Alu.max),
                 deps=[e1])
        e3 = vop(lambda e: e.reciprocal(tc(TX_T1), tc(TX_SU)), deps=[e2])
        ev_fu = vop(lambda e: e.tensor_tensor(tc(TX_FU), tc(TX_SX), tc(TX_T1), Alu.mult), deps=[e3])

        ev_p3 = []
        ev_p3am = []
        sdsp_full = (sdSp, S.counts.get(sdSp.name, 0))
        for c in range(ICH):
            dd = [sdsp_full, ev_umm_all, ev_uam[-1]]
            if c >= 4:
                dd.append(ev_p3[c - 4])
            e_rd = dma("sync", rb4[c % 4].unsqueeze(1),
                       u_spill[c:c + 1, :, :].rearrange("a p f -> p a f"), deps=dd,
                       semq=[sdRBA, sdRBB, sdRBC, sdRBD][c % 4])
            d0 = [ev_fu, e_rd, ev_g2[c]]
            if len(ev_p3) >= 2:
                d0.append(ev_p3[-2])
            ep1 = sop(lambda e, c=c: e.activation(
                gq32_p[c % 2][:, :], rb4[c % 4][:, :], Act.Copy, scale=tc(TX_FU)),
                deps=d0)
            ep3 = vop(lambda e, c=c: e.tensor_tensor(
                gslot_bf(c)[:, :], gq32_p[c % 2][:, :], gslot_i16(c)[:, :], Alu.mult), deps=[ep1])
            ev_p3.append(ep3)
            dd = [ep3] + ([ev_ua] if c == 0 else [])
            ev_p3am.append(vop(lambda e, c=c: e.tensor_reduce(
                acol[:, 32 + c:32 + c + 1], gslot_bf(c)[:, :], AX.X, Alu.max,
                apply_absolute_value=True), deps=dd))
        ev_oa = gop(lambda e: e.tensor_reduce(t1r[0:1, 3:4], acol[:, 32:32 + ICH], AX.XYZWC, Alu.max),
                    deps=[ev_p3am[-1]] + ev_p3am + [ev_ua, evi3])
        if debug:
            ev_dam3 = dma("sync", dbg_amx[:, 32:56] if False else dbg_ar[:, 0:8], amx[:, 32:40], deps=[ev_oa])
            ev_dam4 = dma("sync", dbg_ar[:, 8:16], amx[:, 40:48], deps=[ev_oa])
            ev_dp3 = dma("sync", dbg_p3[:, 0:(ICH * T)], _flat2(A_z[:, 0:(ICH * T) // 2].bitcast(BF16)), deps=[ev_oa])
            ev_dam3 = emax(ev_dam3, ev_dp3)[0]
        ev_aoc, evi4 = allreduce_scalar(4, t1r[0:1, 3:4], tc(TX_AO), [ev_oa] + ([ev_dam3, ev_dam4] if debug else []))
        e1 = vop(lambda e: e.tensor_tensor(tc(TX_CO), tc(TX_SG2), tc(TX_SU), Alu.mult), deps=[ev_aoc])
        e2 = vop(lambda e: e.tensor_tensor(tc(TX_T0), tc(TX_AO), tc(TX_CO), Alu.mult), deps=[e1])
        e3 = vop(lambda e: e.tensor_scalar(tc(TX_SO), tc(TX_T0), 1.0 / QMAX, 1e-12, Alu.mult, Alu.max),
                 deps=[e2])
        e4 = vop(lambda e: e.reciprocal(tc(TX_T1), tc(TX_SO)), deps=[e3])
        ev_fo = vop(lambda e: e.tensor_tensor(tc(TX_FO), tc(TX_CO), tc(TX_T1), Alu.mult), deps=[e4])

        NTG = TT // 4
        bankB = [p_mm[0], p_mm[1], p_tr[0], p_tr[1]]
        ev_outd = []
        ev_wdld = {}
        ev_mm_hist = {}
        prev_oq = None
        prev_oq2 = None
        prev_evacs = None
        for tg in range(NTG):
            ev_dn_last = None
            for i in range(ICH):
                gidx = tg * ICH + i
                if gidx == 0:
                    ev_wdld[0] = dma(
                        "sync", wdstr_bf[0].rearrange("p (r f) -> p r f", r=NCORES),
                        wd_all.rearrange("r (i p f) -> p i r f", p=128, f=128)[:, 0:1, :, :].rearrange("p i r f -> p (i r) f"),
                        deps=[ev_ag_d, ev_umm_all, ev_xqf], semq=sdStA)
                if gidx + 1 < NTG * ICH:
                    i2 = (gidx + 1) % ICH
                    dd = [ev_ag_d]
                    if gidx >= 1:
                        dd.append(ev_wd_prev_mm)
                    ev_wdld[gidx + 1] = dma(
                        "sync" if (gidx + 1) % 2 == 0 else "scalar",
                        wdstr_bf[(gidx + 1) % 2].rearrange("p (r f) -> p r f", r=NCORES),
                        wd_all.rearrange("r (i p f) -> p i r f", p=128, f=128)[:, i2:i2 + 1, :, :].rearrange("p i r f -> p (i r) f"),
                        deps=dd, semq=(sdStA if (gidx + 1) % 2 == 0 else sdStC))
                d0 = [ev_fo, ev_p3[i]]
                if prev_oq2 is not None:
                    d0.append(prev_oq2)
                if gidx >= 2:
                    d0.append(ev_mm_hist[gidx - 2])
                eo1 = vop(lambda e, i=i, tg=tg, gidx=gidx: e.tensor_scalar(
                    oq32_p[gidx % 2][:, :], gslot_bf(i)[:, tg * 512:(tg + 1) * 512],
                    tc(TX_FO), None, Alu.mult, Alu.bypass), deps=d0 + [ev_p3am[i]])
                eo2 = sop(lambda e, gidx=gidx: e.activation(
                    oqbf_v[gidx % 2][:, :], oq32_p[gidx % 2][:, :], Act.Copy),
                    deps=[eo1])
                prev_oq2 = prev_oq
                prev_oq = eo2
                ev_m = None
                for jj in range(4):
                    for hh in range(SEG):
                        pbank = p_dn[jj] if hh == 0 else bankB[jj]
                        dd2 = emax(eo2, ev_wdld[gidx])
                        if i == 0 and prev_evacs is not None:
                            dd2 = emax(eo2, ev_wdld[gidx], prev_evacs)
                        ev_m = top(lambda e, jj=jj, hh=hh, i=i, gidx=gidx, pbank=pbank: e.matmul(
                            pbank[:, 0:512], oqbf_v[gidx % 2][:, jj * 128:(jj + 1) * 128],
                            wdstr_bf[gidx % 2][:, hh * 512:(hh + 1) * 512],
                            start=(i == 0), stop=(i == ICH - 1)), deps=dd2)
                ev_wd_prev_mm = ev_m
                ev_mm_hist[gidx] = ev_m
                ev_dn_last = ev_m
            last_evac = None
            prev_ea = None
            for jj in range(4):
                j = tg * 4 + jj
                dd = [ev_sp[j]]
                if len(ev_outd) >= 2:
                    dd.append(ev_outd[-2])
                e_rb = dma("scalar", x2rb[jj % 2][:, :],
                           x2_spill.rearrange("(n p) f -> p n f", p=128)[:, j:j + 1, :].rearrange("p a f -> p (a f)"),
                           deps=dd + [ev_umm_all, ev_allsp], semq=sdXR)
                evh = []
                for hh in range(SEG):
                    src = p_dn[jj] if hh == 0 else bankB[jj]
                    evh.append(sop(lambda e, jj=jj, hh=hh, src=src: e.activation(
                        outst_v[:, jj * D + hh * 512:jj * D + hh * 512 + 512],
                        src[:, 0:512], Act.Copy, scale=tc(TX_SO)), deps=[ev_dn_last, ev_fo]))
                sevs = []
                for s_ in range(SEG):
                    dds = [evh[-1]]
                    if prev_ea is not None:
                        dds.append(prev_ea)
                    sevs.append(vop(lambda e, jj=jj, s_=s_: e.bn_stats(
                        ln4s[:, s_ * 6:s_ * 6 + 6],
                        outst_v[:, jj * D + s_ * 512:jj * D + s_ * 512 + 512]), deps=dds))
                ea = vop(lambda e: e.bn_aggr(ln4m[:, 0:2], ln4s[:]),
                         deps=[sevs[-1]])
                eb = vop(lambda e: e.tensor_scalar(ln4m[:, 2:3], ln4m[:, 1:2], EPS, None,
                                                   Alu.add, Alu.bypass), deps=[ea])
                ec = sop(lambda e: e.activation(ln4m[:, 2:3], ln4m[:, 2:3], Act.Sqrt), deps=[eb])
                ed = vop(lambda e: e.reciprocal(ln4m[:, 2:3], ln4m[:, 2:3]), deps=[ec])
                ee = vop(lambda e: e.scalar_tensor_tensor(ln4m[:, 3:4], ln4m[:, 0:1], -1.0,
                                                          ln4m[:, 2:3], Alu.mult, Alu.mult),
                         deps=[ed])
                if debug and tg == 0 and jj == 0:
                    ev_do = dma("sync", dbg_o[:, 0:D], outst_v[:, 0:D], deps=[evh[-1]])
                else:
                    ev_do = None
                ef = vop(lambda e, jj=jj: e.tensor_scalar(
                    outst_v[:, jj * D:(jj + 1) * D], outst_v[:, jj * D:(jj + 1) * D],
                    ln4m[:, 2:3], ln4m[:, 3:4], Alu.mult, Alu.add),
                    deps=[ee, evh[-1]] + ([ev_do] if ev_do else []))
                eg = vop(lambda e, jj=jj: e.tensor_tensor(
                    outst_v[:, jj * D:(jj + 1) * D], outst_v[:, jj * D:(jj + 1) * D],
                    x2rb[jj % 2][:, :], Alu.add), deps=[ef, e_rb])
                ev_od = dma("sync", out.rearrange("(n p) f -> p n f", p=128)[:, j:j + 1, :],
                            outst_v[:, jj * D:(jj + 1) * D].unsqueeze(1), deps=[eg], semq=sdOut)
                ev_outd.append(ev_od)
                last_evac = eg
                prev_ea = ea
            prev_evacs = last_evac

        if debug:
            dd = [ev_outd[-1], ev_fo, ev_xs, ev_bu]
            dma("sync", dbg_t[:], t1r[:], deps=dd)
            dma("sync", dbg_c[:], tiny[:], deps=dd)
            ev_dbg = dma("sync", dbg_g[:, 0:T // 2].bitcast(BF16), gslot_bf(0)[:, :].rearrange("p (a b) -> p a b", b=2) if False else gslot_bf(0)[:, :], deps=dd)
            S.wait_only("sync", ev_dbg)
        S.wait_only("sync", ev_outd[-1])

        with nc.Block() as block:
            @block.sync
            def _(e):
                for fn in S.ops["sync"]:
                    fn(e)

            @block.vector
            def _(e):
                for fn in S.ops["vector"]:
                    fn(e)

            @block.scalar
            def _(e):
                for fn in S.ops["scalar"]:
                    fn(e)

            @block.tensor
            def _(e):
                for fn in S.ops["tensor"]:
                    fn(e)

            @block.gpsimd
            def _(e):
                for fn in S.ops["gpsimd"]:
                    fn(e)

    return nc


_CACHE = {}


def kernel(L0_attn_u, L0_attn_Lambda, L0_attn_log_dt, L0_attn_B, L0_attn_C,
           L0_attn_D, L0_ffn_w_gate, L0_ffn_w_up, L0_ffn_w_down):
    u = np.ascontiguousarray(np.asarray(L0_attn_u, dtype=np.float32))
    Bsz, Sq, Dm = u.shape
    NI = int(L0_ffn_w_gate.shape[0])
    NS = int(L0_attn_Lambda.shape[0])
    T = (Bsz * Sq) // NCORES
    uf = u.reshape(Bsz * Sq, Dm)

    key = (T, Dm, NI, NS)
    if key not in _CACHE:
        _CACHE[key] = build_program(T, Dm, NI, NS)
    nc = _CACHE[key]

    lam = np.ascontiguousarray(np.asarray(L0_attn_Lambda, np.float32).reshape(NS, 1))
    ldt = np.ascontiguousarray(np.asarray(L0_attn_log_dt, np.float32).reshape(1, 1))
    Bm = np.ascontiguousarray(np.asarray(L0_attn_B, np.float32))
    Cm = np.ascontiguousarray(np.asarray(L0_attn_C, np.float32))
    Dv = np.ascontiguousarray(np.asarray(L0_attn_D, np.float32).reshape(1, Dm))
    wg = np.ascontiguousarray(np.asarray(L0_ffn_w_gate, np.float32))
    wu = np.ascontiguousarray(np.asarray(L0_ffn_w_up, np.float32))
    wd = np.ascontiguousarray(np.asarray(L0_ffn_w_down, np.float32))
    NI_SH = NI // NCORES
    D_SH = Dm // NCORES

    in_maps = []
    for c in range(NCORES):
        sel = np.zeros((NCORES, 1), np.float32)
        if c % 2 == 1:
            sel[c - 1, 0] = 1.0
        in_maps.append({
            "u": np.ascontiguousarray(uf[c * T:(c + 1) * T]),
            "lam": lam, "ldt": ldt, "Bm": Bm, "Cm": Cm, "Dv": Dv, "sel": sel,
            "wg": np.ascontiguousarray(wg[c * NI_SH:(c + 1) * NI_SH]),
            "wu": np.ascontiguousarray(wu[c * NI_SH:(c + 1) * NI_SH]),
            "wd": np.ascontiguousarray(wd[c * D_SH:(c + 1) * D_SH]),
        })
    res = run_bass_kernel_spmd(nc, in_maps, core_ids=list(range(NCORES)))
    full = np.concatenate([np.asarray(res.results[c]["out"]) for c in range(NCORES)], axis=0)
    return full.reshape(Bsz, Sq, Dm)


# version 17
# speedup vs baseline: 1.0316x; 1.0044x over previous
import contextlib
import numpy as np
import concourse.bass as bass
import concourse.mybir as mybir
from concourse.bass_utils import run_bass_kernel_spmd

F32 = mybir.dt.float32
BF16 = mybir.dt.bfloat16
I32 = mybir.dt.int32
I16 = mybir.dt.int16
Alu = mybir.AluOpType
Act = mybir.ActivationFunctionType
AX = mybir.AxisListType

NCORES = 8
QMAX = 32767.0
SHIFT = 32768.0
RSH = 32768.5
WRSH = 8.5
WSHIFT = 8.0
WQMAX = 7.0
WBLK = 32
EPS = 1e-5
ENGS = ("sync", "vector", "scalar", "tensor", "gpsimd")



def _flat2(v):
    if len(v.shape) == 3:
        return v.rearrange("p a b -> p (a b)")
    return v


class Sched:
    def __init__(self):
        self.ops = {e: [] for e in ENGS}
        self.counts = {}
        self.lastwait = {e: {} for e in ENGS}

    def _wait(self, eng, semh, val):
        if val <= 0:
            return
        if self.lastwait[eng].get(semh.name, 0) >= val:
            return
        self.lastwait[eng][semh.name] = val
        self.ops[eng].append(lambda e, s=semh, v=val: e.wait_ge(s, v))

    def run(self, eng, semh, fn, deps=(), inc=1):
        for d in deps:
            if d is not None:
                self._wait(eng, d[0], d[1])
        v = self.counts.get(semh.name, 0) + inc
        self.counts[semh.name] = v
        self.ops[eng].append(lambda e, fn=fn, s=semh, i=inc: fn(e).then_inc(s, i))
        return (semh, v)

    def wait_only(self, eng, evt):
        if evt is not None:
            self._wait(eng, evt[0], evt[1])


def emax(*evts):
    best = {}
    for e in evts:
        if e is None:
            continue
        s, v = e
        if s.name not in best or best[s.name][1] < v:
            best[s.name] = e
    return list(best.values())


def build_program(T, D, NI, NS, debug=False):
    DCH = D // 128
    ICH = NI // 128
    TT = T // 128
    SEG = D // 512
    TB = T // 512
    NI_SH = NI // NCORES
    D_SH = D // NCORES
    assert D % 512 == 0 and T % 512 == 0 and NI % 128 == 0
    assert NI_SH % 128 == 0 and D_SH % 128 == 0 and NS <= 64 and TT % 4 == 0

    nc = bass.Bass("TRN2", target_bir_lowering=False)

    u_in = nc.dram_tensor("u", [T, D], F32, kind="ExternalInput").ap()
    lam_in = nc.dram_tensor("lam", [NS, 1], F32, kind="ExternalInput").ap()
    ldt_in = nc.dram_tensor("ldt", [1, 1], F32, kind="ExternalInput").ap()
    B_in = nc.dram_tensor("Bm", [NS, D], F32, kind="ExternalInput").ap()
    C_in = nc.dram_tensor("Cm", [D, NS], F32, kind="ExternalInput").ap()
    D_in = nc.dram_tensor("Dv", [1, D], F32, kind="ExternalInput").ap()
    sel_in = nc.dram_tensor("sel", [NCORES, 1], F32, kind="ExternalInput").ap()
    wg_in = nc.dram_tensor("wg", [NI_SH, D], F32, kind="ExternalInput").ap()
    wu_in = nc.dram_tensor("wu", [NI_SH, D], F32, kind="ExternalInput").ap()
    wd_in = nc.dram_tensor("wd", [D_SH, NI], F32, kind="ExternalInput").ap()
    out = nc.dram_tensor("out", [T, D], F32, kind="ExternalOutput").ap()

    cc_w_i = nc.dram_tensor("cc_w_i", [1, 8], F32, kind="Internal").ap()
    cc_w_o = nc.dram_tensor("cc_w_o", [1, 8], F32, kind="Internal").ap()
    st_i = nc.dram_tensor("st_i", [1, NS], F32, kind="Internal").ap()
    st_o = nc.dram_tensor("st_o", [NCORES, NS], F32, kind="Internal").ap()
    arb = [(nc.dram_tensor(f"ar{k}_i", [1, 8], F32, kind="Internal").ap(),
            nc.dram_tensor(f"ar{k}_o", [1, 8], F32, kind="Internal").ap())
           for k in range(5)]
    SH_UG = (NI_SH // 128) * DCH * 128 * 128
    SH_D = (D_SH // 128) * ICH * 128 * 128
    dbg_kind = "ExternalOutput" if debug else "Internal"
    wg_sh_d = nc.dram_tensor("wg_sh_d", [1, SH_UG], BF16, kind="Internal").ap()
    wu_sh_d = nc.dram_tensor("wu_sh_d", [1, SH_UG], BF16, kind="Internal").ap()
    wd_sh_d = nc.dram_tensor("wd_sh_d", [1, SH_D], BF16, kind="Internal").ap()
    wg_all = nc.dram_tensor("wg_all", [NCORES, SH_UG], BF16, kind="Internal").ap()
    wu_all = nc.dram_tensor("wu_all", [NCORES, SH_UG], BF16, kind="Internal").ap()
    wd_all = nc.dram_tensor("wd_all", [NCORES, SH_D], BF16, kind="Internal").ap()
    dbg_wu = nc.dram_tensor("dbg_wu", [NCORES, SH_UG], BF16, kind="ExternalOutput").ap() if debug else None
    dbg_wg = nc.dram_tensor("dbg_wg", [NCORES, SH_UG], BF16, kind="ExternalOutput").ap() if debug else None
    dbg_gmm = nc.dram_tensor("dbg_gmm", [128, 4096], BF16, kind="ExternalOutput").ap() if debug else None
    dbg_amx = nc.dram_tensor("dbg_amx", [1, 66], F32, kind="ExternalOutput").ap() if debug else None
    dbg_ar = nc.dram_tensor("dbg_ar", [1, 16], F32, kind="ExternalOutput").ap() if debug else None
    dbg_p3 = nc.dram_tensor("dbg_p3", [128, 24 * 512], BF16, kind="ExternalOutput").ap() if debug else None
    dbg_wd = nc.dram_tensor("dbg_wd", [NCORES, SH_D], BF16, kind="ExternalOutput").ap() if debug else None
    dbg_o = nc.dram_tensor("dbg_o", [128, 1024], F32, kind="ExternalOutput").ap() if debug else None
    u_spill = nc.dram_tensor("u_spill", [ICH, 128, T], BF16, kind=dbg_kind).ap()
    x2_spill = nc.dram_tensor("x2_spill", [T, D], F32, kind=dbg_kind).ap()
    dbg_t = nc.dram_tensor("dbg_t", [1, 32], F32, kind="ExternalOutput").ap() if debug else None
    dbg_c = nc.dram_tensor("dbg_c", [128, 24], F32, kind="ExternalOutput").ap() if debug else None
    dbg_bu = nc.dram_tensor("dbg_bu", [NS, T], F32, kind="ExternalOutput").ap() if debug else None
    dbg_xs = nc.dram_tensor("dbg_xs", [NS, T], F32, kind="ExternalOutput").ap() if debug else None
    dbg_a = nc.dram_tensor("dbg_a", [NS, 4], F32, kind="ExternalOutput").ap() if debug else None
    dbg_z = nc.dram_tensor("dbg_z", [128, 2048], BF16, kind="ExternalOutput").ap() if debug else None
    dbg_x1 = nc.dram_tensor("dbg_x1", [128, 2048], BF16, kind="ExternalOutput").ap() if debug else None
    dbg_g = nc.dram_tensor("dbg_g", [128, T], F32, kind="ExternalOutput").ap() if debug else None
    dbg_xq = nc.dram_tensor("dbg_xq", [128, T], BF16, kind="ExternalOutput").ap() if debug else None
    dbg_xq2 = nc.dram_tensor("dbg_xq2", [128, 1024], BF16, kind="ExternalOutput").ap() if debug else None

    rg = [list(range(NCORES))]
    stack = contextlib.ExitStack()
    sbuf = lambda n, s, d: stack.enter_context(nc.sbuf_tensor(n, s, d))
    psum = lambda n, s: stack.enter_context(nc.psum_tensor(n, s, F32))
    sem = lambda n: stack.enter_context(nc.semaphore(n))

    with stack:
        AUW = max(TT * D, 7 * D + 1024, (DCH * T) // 2 + T + 2 * DCH * 64)
        A_u = sbuf("A_u", [128, AUW], F32)
        FZB = (TT * D) // 2 + (DCH * T) // 2
        AZW = max((ICH * T) // 2, FZB + 6 * D, FZB + D + (TT * D) // 2,
                  (TT * D) // 2 + 3 * T + T // 2)
        A_z = sbuf("A_z", [128, AZW], F32)
        AWW = max(8192, 4 * T)
        A_w = sbuf("A_w", [128, AWW], F32)

        def z_bf(j):
            return _flat2(A_z[:, (j * D) // 2:((j + 1) * D) // 2].bitcast(BF16))

        X1F_W0 = (TT * D) // 2
        def x1f_bf(c):
            w0 = X1F_W0 + (c * T) // 2
            return _flat2(A_z[:, w0:w0 + T // 2].bitcast(BF16))
        Bu_v = A_z[:, X1F_W0:X1F_W0 + T][0:NS, :]
        pw_v = A_z[:, X1F_W0 + T:X1F_W0 + 2 * T][0:NS, :]
        xs_v = A_z[:, X1F_W0 + 2 * T:X1F_W0 + 3 * T][0:NS, :]
        xsb_v =_flat2( A_z[:, X1F_W0 + 3 * T:X1F_W0 + 3 * T + T // 2][0:NS, :].bitcast(BF16))
        FZ = X1F_W0 + (DCH * T) // 2
        x1t = [_flat2(A_z[:, FZ + i * (D // 2):FZ + (i + 1) * (D // 2)].bitcast(BF16)) for i in range(2)]
        FZ2 = FZ + D
        dt_t = [_flat2(A_z[:, FZ2 + i * (D // 2):FZ2 + (i + 1) * (D // 2)].bitcast(BF16)) for i in range(2)]
        FZ3 = FZ2 + D
        zt_f = [A_z[:, FZ3 + i * D:FZ3 + (i + 1) * D] for i in range(2)]
        xqt_v = _flat2(A_z[:, FZ:FZ + D // 2].bitcast(I16))
        XQB0 = FZ + D
        def xqb_full(j):
            return _flat2(A_z[:, XQB0 + (j * D) // 2:XQB0 + ((j + 1) * D) // 2].bitcast(BF16))
        assert XQB0 + (TT * D) // 2 <= AZW

        def gslot_bf(c):
            return _flat2(A_z[:, (c * T) // 2:((c + 1) * T) // 2].bitcast(BF16))

        def gslot_i16(c):
            return _flat2(A_z[:, (c * T) // 2:((c + 1) * T) // 2].bitcast(I16))

        XQF_W = (DCH * T) // 2
        def xqf_bf(c):
            return _flat2(A_u[:, (c * T) // 2:((c + 1) * T) // 2].bitcast(BF16))
        ustg_bf = [_flat2(A_u[:, XQF_W + i * (T // 2):XQF_W + (i + 1) * (T // 2)].bitcast(BF16)) for i in range(2)]
        WST_W0 = XQF_W + T
        wstr_bf = [_flat2(A_u[:, WST_W0 + i * (DCH * 64):WST_W0 + (i + 1) * (DCH * 64)].bitcast(BF16)) for i in range(2)]
        wdstr_bf = [_flat2(A_u[:, i * (D // 2):(i + 1) * (D // 2)].bitcast(BF16)) for i in range(2)]
        RB4_W0 = WST_W0 + 2 * (DCH * 64)
        rb4 = [_flat2(A_u[:, RB4_W0 + i * (T // 2):RB4_W0 + (i + 1) * (T // 2)].bitcast(BF16)) for i in range(4)]
        oqbf_v = [_flat2(A_u[:, D + i * 256:D + (i + 1) * 256].bitcast(BF16)) for i in range(2)]
        oq32_p = [A_u[:, D + 512:D + 1024].bitcast(I32),
                  A_u[:, RB4_W0 + 4 * (T // 2):RB4_W0 + 4 * (T // 2) + 512].bitcast(I32)]
        outst_v = A_u[:, D + 1024:D + 1024 + 4 * D]
        X2RB_W0 = D + 1024 + 4 * D
        x2rb = [A_u[:, X2RB_W0 + i * D:X2RB_W0 + (i + 1) * D] for i in range(2)]
        assert X2RB_W0 + 2 * D <= AUW

        wsh_t = [A_w[:, i * 1536:(i + 1) * 1536] for i in range(2)]
        scr_f = A_w[:, 3072:4608]
        scr_i = A_w[:, 3072:4608].bitcast(I32)
        wq_v =_flat2( A_w[:, 4608:6144].bitcast(BF16))
        wtr_v =_flat2( A_w[:, 6144:7680].bitcast(BF16))
        bmax_v = A_w[:, 7680:7776]
        bscl_v = A_w[:, 7776:7872]
        brcp_v = A_w[:, 7872:7968]
        assert AWW >= 7968
        gq32_p = [A_w[:, 0:T].bitcast(I32), A_w[:, T:2 * T].bitcast(I32)]
        scf0_p = [A_w[:, 2 * T:3 * T], A_w[:, 3 * T:4 * T]]
        scf0i_p = [A_w[:, 2 * T:3 * T].bitcast(I32), A_w[:, 3 * T:4 * T].bitcast(I32)]

        ident_f = sbuf("ident_f", [128, 128], F32)
        ident_b = sbuf("ident_b", [128, 128], BF16)
        iota_t = sbuf("iota_t", [128, 128], F32)
        a_col = sbuf("a_col", [NS, 1], F32)
        e_col = sbuf("e_col", [NS, 1], F32)
        lam_c = sbuf("lam_c", [NS, 1], F32)
        ldt_c = sbuf("ldt_c", [NS, 1], F32)
        zer_c = sbuf("zer_c", [NS, 1], F32)
        BT_bf = sbuf("BT_bf", [128, DCH * NS], BF16)
        CT_bf = sbuf("CT_bf", [NS, D], BF16)
        D_bc = sbuf("D_bc", [128, D], BF16)
        Drow = A_w[:, 0:D]
        selc = sbuf("selc", [NCORES, 1], F32)
        stg = sbuf("stg", [NCORES, NS], F32)
        st_in = sbuf("st_in", [NS, 1], F32)
        stats = sbuf("stats", [128, TT * SEG * 6], F32)
        mvar = sbuf("mvar", [128, TT * 2], F32)
        m1c = sbuf("m1c", [128, TT], F32)
        r1c = sbuf("r1c", [128, TT], F32)
        b1c = sbuf("b1c", [128, TT], F32)
        m2c = sbuf("m2c", [128, TT], F32)
        r2c = sbuf("r2c", [128, TT], F32)
        b2c = sbuf("b2c", [128, TT], F32)
        m3c = sbuf("m3c", [128, TT], F32)
        r3c = sbuf("r3c", [128, TT], F32)
        b3c = sbuf("b3c", [128, TT], F32)
        sqc = sbuf("sqc", [128, TT], F32)
        bqc = sbuf("bqc", [128, TT], F32)
        mx2 = sbuf("mx2", [128, TT], F32)
        mn2 = sbuf("mn2", [128, TT], F32)
        tiny = sbuf("tiny", [128, 24], F32)
        t1r = sbuf("t1r", [1, 32], F32)
        acol = sbuf("acol", [128, 64], F32)
        amx = acol[0:1, 0:64]
        ccsb = sbuf("ccsb", [1, 8], F32)
        ones_r = sbuf("ones_r", [1, 128], F32)
        ln4s = sbuf("ln4s", [128, SEG * 6], F32)
        ln4m = sbuf("ln4m", [128, 8], F32)

        TX_GX, TX_SX, TX_RQ = 0, 1, 2
        TX_AG, TX_SG, TX_FG, TX_SGB = 3, 4, 5, 6
        TX_AG2, TX_SG2, TX_F2 = 7, 8, 9
        TX_AU, TX_SU, TX_FU = 10, 11, 12
        TX_AO, TX_SO, TX_FO, TX_CO = 13, 14, 15, 16
        TX_T0, TX_T1 = 17, 18

        def tc(i):
            return tiny[:, i:i + 1]

        p_tr = [psum(f"p_tr{i}", [128, 512]) for i in range(2)]
        def ptrb(i):
            return _flat2(p_tr[i][:, 0:64].bitcast(BF16))
        ptr_ring = []

        def ring_slot():
            return len(ptr_ring) % 2

        def ring_dep():
            return [ptr_ring[-2]] if len(ptr_ring) >= 2 else []
        p_mm = [psum(f"p_mm{i}", [128, 512]) for i in range(2)]
        p_dn = [psum(f"p_dn{i}", [128, 512]) for i in range(4)]

        sdS = sem("sdS")
        sdG = sem("sdG")
        sdC = sem("sdC")
        sdU = sem("sdU")
        sdSm = sem("sdSm")
        sdW = sem("sdW")
        sdStrm = sem("sdStrm")
        sdRB = sem("sdRB")
        sdOut = sem("sdOut")
        sdSp = sem("sdSp")
        sdSpX = sem("sdSpX")
        sdStA = sem("sdStA")
        sdStB = sem("sdStB")
        sdStC = sem("sdStC")
        sdWA = sem("sdWA")
        sdWB = sem("sdWB")
        sdWS = sem("sdWS")
        sdRBA = sem("sdRBA")
        sdRBB = sem("sdRBB")
        sdRBC = sem("sdRBC")
        sdRBD = sem("sdRBD")
        sdXR = sem("sdXR")
        sv = sem("sv")
        ss = sem("ss")
        stn = sem("stn")
        sg = sem("sg")
        scc = sem("scc")

        S = Sched()
        DMAS = dict(sync=sdS, gpsimd=sdG, scalar=sdC)

        def dma(q, o, i, deps=(), semq=None, **kw):
            return S.run(q, semq or DMAS[q], lambda e, o=o, i=i, kw=kw: e.dma_start(out=o, in_=i, **kw),
                         deps=deps, inc=16)

        def vop(fn, deps=()):
            return S.run("vector", sv, fn, deps)

        def sop(fn, deps=()):
            return S.run("scalar", ss, fn, deps)

        def top(fn, deps=()):
            return S.run("tensor", stn, fn, deps)

        def gop(fn, deps=()):
            return S.run("gpsimd", sg, fn, deps)

        def cop(fn, deps=()):
            return S.run("gpsimd", scc, fn, deps)

        ev_u = dma("sync", A_u[:, 0:TT * D].rearrange("p (n f) -> p n f", n=TT),
                   u_in.rearrange("(n p) f -> p n f", p=128), semq=sdU)
        ev_lam = dma("sync", lam_c[:], lam_in, semq=sdSm)
        ev_ldt = dma("sync", t1r[0:1, 0:1], ldt_in, semq=sdSm)
        ev_B = dma("sync", Drow[0:NS, :], B_in, semq=sdSm)
        ev_sel = dma("sync", selc[:], sel_in, semq=sdSm)


        ev_iota = gop(lambda e: e.iota(iota_t[:], [[1, 128]], base=0, channel_multiplier=-1,
                                       allow_small_or_imprecise_dtypes=True))
        ev_idf = vop(lambda e: e.tensor_scalar(ident_f[:], iota_t[:], 0.0, None, Alu.is_equal, Alu.bypass),
                     deps=[ev_iota])
        ev_idb = vop(lambda e: e.tensor_scalar(ident_b[:], iota_t[:], 0.0, None, Alu.is_equal, Alu.bypass),
                     deps=[ev_iota])
        ev_z0 = vop(lambda e: e.memset(zer_c[:], 0.0))
        ev_o1 = vop(lambda e: e.memset(ones_r[:], 1.0))
        ev_zr = vop(lambda e: e.memset(ccsb[:], 0.0))
        ev_ccw = dma("sync", cc_w_i[:], ccsb[:], deps=[ev_zr], semq=sdSm)
        ev_warm = cop(lambda e: e.collective_compute("AllReduce", Alu.max, replica_groups=rg,
                                                     ins=[cc_w_i[:]], outs=[cc_w_o[:]]),
                      deps=[ev_ccw])
        ar_zero = [dma("sync", arb[k][0][:], ccsb[:], deps=[ev_zr], semq=sdSm) for k in range(5)]

        def smbar():
            return (sdSm, S.counts.get(sdSm.name, 0))

        ev_ldtm = top(lambda e: e.matmul(p_dn[1][0:NS, 0:1], ones_r[0:1, 0:NS], t1r[0:1, 0:1],
                                         start=True, stop=True), deps=[smbar(), ev_o1])
        ev_ldtb = sop(lambda e: e.activation(ldt_c[:], p_dn[1][0:NS, 0:1], Act.Copy),
                      deps=[ev_ldtm])
        ev_e = sop(lambda e: e.activation(e_col[:], lam_c[:], Act.Exp, bias=ldt_c[:], scale=1.0),
                   deps=[smbar(), ev_ldtb])
        ev_a = sop(lambda e: e.activation(a_col[:], e_col[:], Act.Exp, scale=-1.0), deps=[ev_e])

        ev_bt_e = []
        for c in range(DCH):
            sl = ring_slot()
            evt = top(lambda e, c=c, sl=sl: e.transpose(p_tr[sl][0:128, 0:NS],
                                                        Drow[0:NS, c * 128:(c + 1) * 128],
                                                        ident_f[0:NS, 0:NS]),
                      deps=[smbar(), ev_idf] + ring_dep())
            eve = sop(lambda e, c=c, sl=sl: e.activation(BT_bf[:, c * NS:(c + 1) * NS],
                                                         p_tr[sl][0:128, 0:NS], Act.Copy),
                      deps=[evt])
            ptr_ring.append(eve)
            ev_bt_e.append(eve)
        ev_BT = ev_bt_e[-1]
        ev_C = dma("sync", Drow[:, 0:DCH * NS].rearrange("p (n f) -> p n f", n=DCH),
                   C_in.rearrange("(n p) f -> p n f", p=128), deps=[ev_BT], semq=sdSm)
        ev_ct_e = []
        for c in range(DCH):
            sl = ring_slot()
            evt = top(lambda e, c=c, sl=sl: e.transpose(p_tr[sl][0:NS, 0:128],
                                                        Drow[:, c * NS:(c + 1) * NS], ident_f[:]),
                      deps=[smbar(), ev_idf] + ring_dep())
            eve = sop(lambda e, c=c, sl=sl: e.activation(CT_bf[0:NS, c * 128:(c + 1) * 128],
                                                         p_tr[sl][0:NS, 0:128], Act.Copy),
                      deps=[evt])
            ptr_ring.append(eve)
            ev_ct_e.append(eve)
        ev_CT = ev_ct_e[-1]
        ev_D = dma("sync", Drow[0:1, :], D_in, deps=[ev_CT], semq=sdSm)
        ev_dbc = None
        for hh in range(D // 512):
            ev_dm = top(lambda e, hh=hh: e.matmul(
                p_dn[1][:, 0:512], ones_r[0:1, 0:128], Drow[0:1, hh * 512:(hh + 1) * 512],
                start=True, stop=True),
                deps=[smbar(), ev_o1, ev_ldtb] + ([ev_dbc] if ev_dbc else []))
            ev_dbc = sop(lambda e, hh=hh: e.activation(
                D_bc[:, hh * 512:(hh + 1) * 512], p_dn[1][:, 0:512], Act.Copy), deps=[ev_dm])

        def dequant_and_share(w_inp, rows, cols, sh_dram, all_dram, dep0, dep_wq_free):
            ntile = rows // 128
            CU = min(cols, 1536)
            nsub = cols // CU
            nbu = CU // WBLK
            ev_ld = {}
            ev_q = []
            ev_tr = []
            ev_sd_hist = {}
            last_stt = dep_wq_free
            last_red = None
            k = 0
            for it in range(ntile):
                for su in range(nsub):
                    uidx = it * nsub + su
                    bi = uidx % 2
                    dep_l = [dep0]
                    if uidx >= 2:
                        dep_l.append(ev_q[uidx - 2])
                    ev_ld[uidx] = dma("sync",
                                      wsh_t[bi][:, 0:CU],
                                      w_inp[it * 128:(it + 1) * 128, su * CU:(su + 1) * CU],
                                      deps=dep_l, semq=(sdWA if bi == 0 else sdWB))
                    d_r = [ev_ld[uidx]]
                    if last_red is not None:
                        d_r.append(last_red)
                    e_r = vop(lambda e, bi=bi: e.tensor_reduce(
                        bmax_v[:, 0:nbu], wsh_t[bi][:, 0:CU].rearrange("p (n w) -> p n w", w=WBLK),
                        AX.X, Alu.max, apply_absolute_value=True), deps=d_r)
                    e_s = vop(lambda e: e.tensor_scalar(bscl_v[:, 0:nbu], bmax_v[:, 0:nbu],
                                                        1.0 / WQMAX, 1e-12, Alu.mult, Alu.max),
                              deps=[e_r])
                    e_c = vop(lambda e: e.reciprocal(brcp_v[:, 0:nbu], bscl_v[:, 0:nbu]), deps=[e_s])
                    rv = brcp_v[:, 0:nbu].unsqueeze(2).broadcast_to((128, nbu, WBLK))
                    sv_b = bscl_v[:, 0:nbu].unsqueeze(2).broadcast_to((128, nbu, WBLK))
                    e_i = vop(lambda e, bi=bi, rv=rv: e.tensor_tensor(
                        scr_i[:, 0:CU].rearrange("p (n w) -> p n w", w=WBLK),
                        wsh_t[bi][:, 0:CU].rearrange("p (n w) -> p n w", w=WBLK), rv, Alu.mult),
                        deps=[e_c])
                    d_q = [e_i]
                    if last_stt is not None:
                        d_q.append(last_stt)
                    e_q = vop(lambda e, sv_b=sv_b: e.scalar_tensor_tensor(
                        wq_v[:, 0:CU].rearrange("p (n w) -> p n w", w=WBLK),
                        scr_i[:, 0:CU].rearrange("p (n w) -> p n w", w=WBLK), 1.0,
                        sv_b, Alu.mult, Alu.mult), deps=d_q)
                    ev_q.append(e_q)
                    last_red = e_q
                    unit_tr = []
                    ncc = CU // 128
                    ring0 = (uidx % 2) * 12
                    for cc in range(ncc):
                        sl = ring_slot()
                        dd = [e_q, ev_idb] + ring_dep()
                        if uidx >= 2 and cc == 0:
                            dd.append(ev_sd_hist[uidx - 2])
                        evt = top(lambda e, cc=cc, sl=sl: e.transpose(
                            ptrb(sl), wq_v[:, cc * 128:(cc + 1) * 128], ident_b[:]),
                            deps=dd)
                        eve = sop(lambda e, ring0=ring0, cc=cc, sl=sl: e.activation(
                            wtr_v[:, (ring0 + cc) * 128:(ring0 + cc) * 128 + 128],
                            ptrb(sl), Act.Copy),
                            deps=[evt] + ([ev_sd_hist[uidx - 2]] if uidx >= 2 else []))
                        ptr_ring.append(eve)
                        ev_tr.append((evt, eve))
                        unit_tr.append(eve)
                        k += 1
                    last_stt = unit_tr[-1]
                    dview = sh_dram.rearrange("a (gc it p f) -> p (a gc) it f",
                                              it=ntile, p=128, f=128)
                    ev_sd = dma("sync",
                                dview[:, su * ncc:(su + 1) * ncc, it:it + 1, :],
                                wtr_v[:, ring0 * 128:(ring0 + ncc) * 128].rearrange(
                                    "p (n f) -> p n f", f=128).unsqueeze(2),
                                deps=[unit_tr[-1]], semq=sdWS)
                    ev_sd_hist[uidx] = ev_sd
                    ev_tr_done = ev_sd
            box = {}

            def emit_ag():
                full = (sdWS, S.counts.get(sdWS.name, 0))
                box["ag"] = cop(lambda e: e.collective_compute(
                    "AllGather", Alu.bypass, replica_groups=rg,
                    ins=[sh_dram[:]], outs=[all_dram[:]]),
                    deps=[full, ev_warm])
                return box["ag"]
            return emit_ag, ev_q[-1], last_stt

        st1 = []
        for j in range(TT):
            for s_ in range(SEG):
                st1.append(vop(lambda e, j=j, s_=s_: e.bn_stats(
                    stats[:, (j * SEG + s_) * 6:(j * SEG + s_) * 6 + 6],
                    A_u[:, j * D + s_ * 512:j * D + s_ * 512 + 512]), deps=[ev_u]))
        ev_bs1 = None
        for j in range(TT):
            ev_bs1 = vop(lambda e, j=j: e.bn_aggr(mvar[:, 2 * j:2 * j + 2],
                                                  stats[:, j * SEG * 6:(j + 1) * SEG * 6]),
                         deps=[st1[-1]] + ([ev_bs1] if ev_bs1 else []))

        def ln_coeffs(mc, rc, bc, dep):
            mview = mvar[:].rearrange("p (n two) -> p n two", two=2)
            e1 = vop(lambda e, mc=mc: e.tensor_copy(mc[:, 0:TT].unsqueeze(2), mview[:, :, 0:1]),
                     deps=[dep])
            e2 = vop(lambda e, rc=rc: e.tensor_scalar(
                rc[:, 0:TT].unsqueeze(2), mview[:, :, 1:2],
                EPS, None, Alu.add, Alu.bypass), deps=[dep])
            e3 = sop(lambda e, rc=rc: e.activation(rc[:, 0:TT], rc[:, 0:TT], Act.Sqrt), deps=[e2])
            e4 = vop(lambda e, rc=rc: e.reciprocal(rc[:, 0:TT], rc[:, 0:TT]), deps=[e3])
            e5 = vop(lambda e, mc=mc, rc=rc, bc=bc: e.scalar_tensor_tensor(
                bc[:, 0:TT], mc[:, 0:TT], -1.0, rc[:, 0:TT], Alu.mult, Alu.mult), deps=[e1, e4])
            return e5

        ev_c1 = ln_coeffs(m1c, r1c, b1c, ev_bs1)

        ev_trx = []
        last_x1 = [None, None]
        k = 0
        for j in range(TT):
            dd = [ev_c1, ev_u]
            if last_x1[j % 2] is not None:
                dd.append(last_x1[j % 2])
            e_x1 = vop(lambda e, j=j: e.tensor_scalar(
                x1t[j % 2][:, :], A_u[:, j * D:(j + 1) * D],
                r1c[:, j:j + 1], b1c[:, j:j + 1], Alu.mult, Alu.add), deps=dd)
            grp_last = None
            for c in range(DCH):
                sl = ring_slot()
                dd = [e_x1, ev_idb] + ring_dep()
                evt = top(lambda e, j=j, c=c, sl=sl: e.transpose(
                    ptrb(sl), x1t[j % 2][:, c * 128:(c + 1) * 128], ident_b[:]),
                    deps=dd)
                eve = sop(lambda e, j=j, c=c, sl=sl: e.activation(
                    x1f_bf(c)[:, j * 128:(j + 1) * 128], ptrb(sl), Act.Copy), deps=[evt])
                ptr_ring.append(eve)
                ev_trx.append(eve)
                grp_last = eve
                k += 1
            last_x1[j % 2] = grp_last
        ev_x1f = ev_trx[-1]

        agf_g, ev_dqg, ev_trg_l = dequant_and_share(wg_in, NI_SH, D, wg_sh_d, wg_all, ev_dbc, None)
        agf_u, ev_dqu, ev_tru_l = dequant_and_share(wu_in, NI_SH, D, wu_sh_d, wu_all, ev_dqg, ev_trg_l)
        agf_d, ev_dqd, ev_trd_l = dequant_and_share(wd_in, D_SH, NI, wd_sh_d, wd_all, ev_dqu, ev_tru_l)

        assert TB <= 4
        ev_bu_m = []
        for b in range(TB):
            dd = emax(ev_x1f, ev_BT)
            ev_m = None
            for c in range(DCH):
                ev_m = top(lambda e, b=b, c=c: e.matmul(
                    p_dn[b][0:NS, 0:512], BT_bf[:, c * NS:(c + 1) * NS],
                    x1f_bf(c)[:, b * 512:(b + 1) * 512],
                    start=(c == 0), stop=(c == DCH - 1)), deps=dd)
                dd = []
            ev_bu_m.append(ev_m)
        ev_bu_e = []
        for b in range(TB):
            eve = sop(lambda e, b=b: e.activation(Bu_v[:, b * 512:(b + 1) * 512],
                                                  p_dn[b][0:NS, 0:512], Act.Copy),
                      deps=[ev_bu_m[-1]])
            ev_bu_e.append(eve)
        ev_bu = ev_bu_e[-1]

        ev_scan = vop(lambda e: e.tensor_tensor_scan(
            xs_v[:, :], a_col[:].broadcast_to((NS, T)), Bu_v[:, :], 0.0, Alu.mult, Alu.add),
            deps=emax(ev_bu, ev_a, ev_x1f))
        ev_pw = vop(lambda e: e.tensor_tensor_scan(
            pw_v[:, :], a_col[:].broadcast_to((NS, T)), zer_c[:].broadcast_to((NS, T)), 1.0,
            Alu.mult, Alu.add), deps=[ev_scan, ev_z0])
        ev_sts = dma("gpsimd", st_i.rearrange("a n -> n a"), xs_v[:, T - 1:T], deps=[ev_scan])
        ev_ag1 = cop(lambda e: e.collective_compute("AllGather", Alu.bypass, replica_groups=rg,
                                                    ins=[st_i[:]], outs=[st_o[:]]),
                     deps=[ev_sts, ev_warm])
        ev_ag_g = agf_g()
        ev_ag_u = agf_u()
        ev_stl = dma("gpsimd", stg[:], st_o[:], deps=[ev_ag1])
        ev_smm = top(lambda e: e.matmul(p_dn[0][0:NS, 0:1], stg[:], selc[:], start=True, stop=True),
                     deps=emax(ev_stl, ev_bu) + [smbar()])
        ev_sti = sop(lambda e: e.activation(st_in[:], p_dn[0][0:NS, 0:1], Act.Copy), deps=[ev_smm])
        ev_xs = vop(lambda e: e.scalar_tensor_tensor(
            xs_v[:, :], pw_v[:, :], st_in[:], xs_v[:, :], Alu.mult, Alu.add),
            deps=[ev_pw, ev_sti])
        ev_xsb = vop(lambda e: e.tensor_copy(xsb_v[:, :], xs_v[:, :]), deps=[ev_xs])
        ev_dxs = dma("sync", dbg_xs[:], xs_v[:, :], deps=[ev_xs]) if debug else None

        if debug:
            ev_d1 = dma("sync", dbg_bu[:], Bu_v[:, :], deps=[ev_bu, ev_scan])
            ev_d2 = dma("sync", dbg_a[:, 0:1], a_col[:], deps=[ev_a], allow_slow_non_contiguous=True)
            ev_d2b = dma("sync", dbg_a[:, 1:2], e_col[:], deps=[ev_a], allow_slow_non_contiguous=True)
            ev_d2c = dma("sync", dbg_a[:, 2:3], ldt_c[:], deps=[ev_a], allow_slow_non_contiguous=True)
            ev_d2d = dma("sync", dbg_a[:, 3:4], lam_c[:], deps=[ev_a], allow_slow_non_contiguous=True)
            ev_d3 = dma("sync", dbg_x1[:, 0:T // 1], x1f_bf(0)[:, :], deps=[ev_x1f])
            for _ev in (ev_d1, ev_d2, ev_d3):
                pass
            dbg_pre_gate = [ev_d1, ev_d3]
        else:
            dbg_pre_gate = []

        ev_zt = []
        last_dt = [None, None]
        for j in range(TT):
            dd = [ev_c1, ev_u]
            if last_dt[j % 2] is not None:
                dd.append(last_dt[j % 2])
            e_x1r = vop(lambda e, j=j: e.tensor_scalar(
                x1t[j % 2][:, :], A_u[:, j * D:(j + 1) * D],
                r1c[:, j:j + 1], b1c[:, j:j + 1], Alu.mult, Alu.add),
                deps=dd + [ev_x1f])
            e_dt = vop(lambda e, j=j: e.tensor_tensor(
                dt_t[j % 2][:, :], x1t[j % 2][:, :], D_bc[:], Alu.mult),
                deps=[e_x1r, ev_dbc])
            for h in range(SEG):
                hb = (j * SEG + h) % 2
                dd2 = emax(ev_xsb, ev_CT) + ([ev_zt[-2]] if len(ev_zt) >= 2 else [])
                e_m = top(lambda e, j=j, h=h, hb=hb: e.matmul(
                    p_mm[hb][:, 0:512], xsb_v[:, j * 128:(j + 1) * 128],
                    CT_bf[0:NS, h * 512:(h + 1) * 512], start=True, stop=True), deps=dd2)
                e_e = vop(lambda e, j=j, h=h, hb=hb: e.tensor_tensor(
                    z_bf(j)[:, h * 512:(h + 1) * 512], p_mm[hb][:, 0:512],
                    dt_t[j % 2][:, h * 512:(h + 1) * 512], Alu.add), deps=[e_m, e_dt])
                ev_zt.append(e_e)
            last_dt[j % 2] = ev_zt[-1]
        ev_z = ev_zt[-1]

        st2 = []
        for j in range(TT):
            for s_ in range(SEG):
                st2.append(vop(lambda e, j=j, s_=s_: e.bn_stats(
                    stats[:, (j * SEG + s_) * 6:(j * SEG + s_) * 6 + 6],
                    z_bf(j)[:, s_ * 512:s_ * 512 + 512]), deps=[ev_z, ev_bs1]))
        ev_bs2 = None
        for j in range(TT):
            ev_bs2 = vop(lambda e, j=j: e.bn_aggr(mvar[:, 2 * j:2 * j + 2],
                                                  stats[:, j * SEG * 6:(j + 1) * SEG * 6]),
                         deps=[st2[-1], ev_c1] + ([ev_bs2] if ev_bs2 else []))
        ev_c2 = ln_coeffs(m2c, r2c, b2c, ev_bs2)

        ev_x2t = []
        last_ztf = [None, None]
        for j in range(TT):
            dd = [ev_c2, ev_z]
            if last_ztf[j % 2] is not None:
                dd.append(last_ztf[j % 2])
            e1 = vop(lambda e, j=j: e.tensor_scalar(
                zt_f[j % 2][:, :], z_bf(j)[:, :], r2c[:, j:j + 1], b2c[:, j:j + 1],
                Alu.mult, Alu.add), deps=dd)
            e2 = vop(lambda e, j=j: e.tensor_tensor(
                A_u[:, j * D:(j + 1) * D], zt_f[j % 2][:, :], A_u[:, j * D:(j + 1) * D],
                Alu.add), deps=[e1])
            e2b = vop(lambda e, j=j: e.tensor_reduce(
                mx2[:, j:j + 1], A_u[:, j * D:(j + 1) * D], AX.X, Alu.max), deps=[e2])
            e3 = vop(lambda e, j=j: e.tensor_reduce(
                mn2[:, j:j + 1], A_u[:, j * D:(j + 1) * D], AX.X, Alu.min), deps=[e2b])
            last_ztf[j % 2] = e3
            ev_x2t.append(e3)
        ev_x2 = ev_x2t[-1]

        st3 = []
        for j in range(TT):
            for s_ in range(SEG):
                st3.append(vop(lambda e, j=j, s_=s_: e.bn_stats(
                    stats[:, (j * SEG + s_) * 6:(j * SEG + s_) * 6 + 6],
                    A_u[:, j * D + s_ * 512:j * D + s_ * 512 + 512]),
                    deps=[ev_x2t[j], ev_bs2]))
        ev_bs3 = None
        for j in range(TT):
            ev_bs3 = vop(lambda e, j=j: e.bn_aggr(mvar[:, 2 * j:2 * j + 2],
                                                  stats[:, j * SEG * 6:(j + 1) * SEG * 6]),
                         deps=[st3[-1], ev_c2] + ([ev_bs3] if ev_bs3 else []))
        ev_c3 = ln_coeffs(m3c, r3c, b3c, ev_bs3)
        ev_ag_d = agf_d()

        e1 = vop(lambda e: e.tensor_tensor(mx2[:], mx2[:], m3c[:], Alu.subtract), deps=[ev_c3, ev_x2])
        e2 = vop(lambda e: e.tensor_tensor(mn2[:], m3c[:], mn2[:], Alu.subtract), deps=[e1])
        e3 = vop(lambda e: e.tensor_tensor(mx2[:], mx2[:], mn2[:], Alu.max), deps=[e2])
        e4 = vop(lambda e: e.tensor_tensor(mx2[:], mx2[:], r3c[:], Alu.mult), deps=[e3])
        e5 = vop(lambda e: e.tensor_reduce(tc(TX_GX), mx2[:], AX.X, Alu.max), deps=[e4])
        ev_axl = gop(lambda e: e.tensor_reduce(t1r[0:1, 1:2], tc(TX_GX), AX.XYZWC, Alu.max),
                     deps=[e5])

        _bc_prev = [ev_dbc]

        def allreduce_scalar(kk, src11, dst_col, deps):
            evi = dma("gpsimd", arb[kk][0][0:1, 0:1], src11, deps=list(deps) + [ar_zero[kk]])
            eva = cop(lambda e, kk=kk: e.collective_compute(
                "AllReduce", Alu.max, replica_groups=rg,
                ins=[arb[kk][0][:]], outs=[arb[kk][1][:]]), deps=[evi, ev_warm])
            evo = dma("gpsimd", t1r[0:1, 4 + kk:5 + kk], arb[kk][1][0:1, 0:1], deps=[eva])
            evm = top(lambda e, kk=kk: e.matmul(
                p_dn[1][:, 0:1], ones_r[0:1, 0:128], t1r[0:1, 4 + kk:5 + kk],
                start=True, stop=True), deps=[evo, ev_o1] + list(_bc_prev))
            evb = sop(lambda e, dst_col=dst_col: e.activation(dst_col, p_dn[1][:, 0:1], Act.Copy),
                      deps=[evm])
            _bc_prev.clear()
            _bc_prev.append(evb)
            return evb, evi

        ev_gx, evi0 = allreduce_scalar(0, t1r[0:1, 1:2], tc(TX_GX), [ev_axl, e5])
        e6 = vop(lambda e: e.tensor_scalar(tc(TX_SX), tc(TX_GX), 1.0 / QMAX, 1e-12, Alu.mult, Alu.max),
                 deps=[ev_gx])
        ev_rq = vop(lambda e: e.reciprocal(tc(TX_RQ), tc(TX_SX)), deps=[e6])
        e7 = vop(lambda e: e.tensor_tensor(sqc[:], r3c[:], tc(TX_RQ).broadcast_to((128, TT)), Alu.mult),
                 deps=[ev_rq, ev_c3])
        e8 = vop(lambda e: e.scalar_tensor_tensor(bqc[:], m3c[:], -1.0, sqc[:], Alu.mult, Alu.mult),
                 deps=[e7])
        ev_qc = e8

        ev_xq_e = []
        ev_xq_e_dump = []
        ev_sp = []
        last_q = None
        for j in range(TT):
            dd = [ev_qc, ev_x2t[j]]
            if last_q is not None:
                dd.append(last_q)
            eq1 = vop(lambda e, j=j: e.tensor_scalar(
                xqt_v[:, :], A_u[:, j * D:(j + 1) * D], sqc[:, j:j + 1], bqc[:, j:j + 1],
                Alu.mult, Alu.add), deps=dd)
            dd2 = [eq1] + ev_xq_e_dump
            if j >= 2:
                dd2.append(ev_xq_e[j - 2][1])
            eq2 = sop(lambda e, j=j: e.activation(
                xqb_full(j)[:, :], xqt_v[:, :], Act.Copy), deps=dd2)
            if debug and j == 0:
                ev_dq0 = dma("sync", dbg_xq2[:, 0:D], xqb_full(0)[:, :], deps=[eq2])
                eq2 = emax(eq2)[0]
                ev_xq_e_dump.append(ev_dq0)
            last_q = eq2
            ev_xq_e.append((eq1, eq2))
            ev_sp.append(dma("gpsimd",
                             x2_spill.rearrange("(n p) f -> p n f", p=128)[:, j:j + 1, :],
                             A_u[:, j * D:(j + 1) * D].unsqueeze(1), deps=[eq1], semq=sdSpX))
        ev_allq = ev_xq_e[-1][0]
        ev_allsp = ev_sp[-1]
        ev_xtr = []
        for j in range(TT):
            for c in range(DCH):
                sl = ring_slot()
                dd = [ev_xq_e[j][1], ev_idb, ev_allq, ev_allsp] + ring_dep()
                evt = top(lambda e, j=j, c=c, sl=sl: e.transpose(
                    ptrb(sl), xqb_full(j)[:, c * 128:(c + 1) * 128], ident_b[:]),
                    deps=dd)
                eve = sop(lambda e, j=j, c=c, sl=sl: e.activation(
                    xqf_bf(c)[:, j * 128:(j + 1) * 128], ptrb(sl), Act.Copy),
                    deps=[evt, ev_allsp])
                ptr_ring.append(eve)
                ev_xtr.append(eve)
        ev_xqf = ev_xtr[-1]
        if debug:
            ev_dxq = dma("sync", dbg_xq[:], xqf_bf(0)[:, :], deps=[ev_xqf])
            ev_dwu = dma("sync", dbg_wu[:], wu_all[:], deps=[ev_ag_u])
            ev_dwg = dma("sync", dbg_wg[:], wg_all[:], deps=[ev_ag_g])
            ev_dwd = dma("sync", dbg_wd[:], wd_all[:], deps=[ev_ag_d])

        def wstream(all_dram, i, buf_idx, deps):
            ntr = NI_SH // 128
            r = i // ntr
            rc = i % ntr
            src = all_dram[r:r + 1, :].rearrange("a (cc rc p f) -> p a cc rc f",
                                                 cc=DCH, rc=ntr, p=128, f=128)
            return dma("sync",
                       wstr_bf[buf_idx].rearrange("p (cc f) -> p cc f", cc=DCH, f=128).unsqueeze(1).unsqueeze(3),
                       src[:, :, :, rc:rc + 1, :], deps=deps,
                       semq=(sdStA if buf_idx == 0 else sdStB))

        def big_mm(all_dram, ev_ag, evac_fn, extra_first_dep, extra_emit=None):
            ev_ws = [None, None]
            ev_evac = {}
            ev_mm_chunk_last = {}
            prev_chunk_mm = None
            ev_ws[0] = wstream(all_dram, 0, 0, deps=[ev_ag] + extra_first_dep)
            for i in range(ICH):
                if i + 1 < ICH:
                    dd = [ev_ag]
                    if prev_chunk_mm is not None:
                        dd.append(prev_chunk_mm)
                    ev_ws[(i + 1) % 2] = wstream(all_dram, i + 1, (i + 1) % 2, deps=dd)
                ev_m = None
                for b in range(TB):
                    gb = i * TB + b
                    pm = p_mm[gb % 2]
                    dd = emax(ev_ws[i % 2], ev_xqf, ev_evac.get(gb - 2))
                    for h in range(DCH):
                        ev_m = top(lambda e, i=i, b=b, h=h, pm=pm: e.matmul(
                            pm[:, 0:512], wstr_bf[i % 2][:, h * 128:(h + 1) * 128],
                            xqf_bf(h)[:, b * 512:(b + 1) * 512],
                            start=(h == 0), stop=(h == DCH - 1)), deps=dd)
                        dd = []
                    ev_evac[gb] = evac_fn(i, b, pm, ev_m)
                prev_chunk_mm = ev_m
                ev_mm_chunk_last[i] = ev_m
                if extra_emit is not None:
                    extra_emit(i)
            return ev_evac, ev_mm_chunk_last

        ev_dz = dma("sync", dbg_z[:, 0:2048], _flat2(A_z[:, 0:1024].bitcast(BF16)), deps=[ev_z]) if debug else None

        def g_guard(i):
            return [ev_xqf] + dbg_pre_gate + ([ev_dz, ev_dxs] if debug else [])

        ev_gevac = {}

        def gate_evac(i, b, pm, ev_m):
            ev = sop(lambda e, i=i, b=b, pm=pm: e.activation(
                gslot_bf(i)[:, b * 512:(b + 1) * 512], pm[:, 0:512], Act.Copy),
                deps=[ev_m] + g_guard(i))
            ev_gevac[(i, b)] = ev
            return ev

        gevac, gmm_last = big_mm(wg_all, ev_ag_g, gate_evac, [ev_xqf])
        ev_gmm_all = gmm_last[ICH - 1]
        if debug:
            ev_dgmm = dma("sync", dbg_gmm[:, 0:T], gslot_bf(0)[:, :],
                          deps=[gevac[b] for b in range(TB)])
        ev_gam = []
        for i in range(ICH):
            dd = [gevac[i * TB + b] for b in range(TB)]
            ev_gam.append(vop(lambda e, i=i: e.tensor_reduce(
                acol[:, i:i + 1], gslot_bf(i)[:, :], AX.X, Alu.max,
                apply_absolute_value=True), deps=dd))
        ev_ga = gop(lambda e: e.tensor_reduce(t1r[0:1, 2:3], acol[:, 0:ICH], AX.XYZWC, Alu.max),
                    deps=[ev_gam[-1]] + ev_gam + [ev_axl])
        if debug:
            ev_damx = dma("sync", dbg_amx[:, 0:64], amx[:], deps=[ev_ga])
            ev_damx2 = dma("sync", dbg_amx[:, 64:66], t1r[0:1, 2:4], deps=[ev_ga])
        ev_agc, evi1 = allreduce_scalar(1, t1r[0:1, 2:3], tc(TX_AG), [ev_ga] + ([ev_damx2, ev_damx] if debug else []))

        e1 = vop(lambda e: e.tensor_tensor(tc(TX_T0), tc(TX_AG), tc(TX_SX), Alu.mult),
                 deps=[ev_agc, ev_rq])
        e2 = vop(lambda e: e.tensor_scalar(tc(TX_SG), tc(TX_T0), 1.0 / QMAX, 1e-12, Alu.mult, Alu.max),
                 deps=[e1])
        e3 = vop(lambda e: e.reciprocal(tc(TX_T1), tc(TX_SG)), deps=[e2])
        e4 = vop(lambda e: e.tensor_tensor(tc(TX_FG), tc(TX_SX), tc(TX_T1), Alu.mult), deps=[e3])
        ev_sgb = vop(lambda e: e.tensor_scalar(tc(TX_SGB), tc(TX_SG), -SHIFT, None, Alu.mult, Alu.bypass),
                     deps=[e4])

        ev_uevac = {}
        ev_uspill = {}
        ev_uam = []

        def up_evac(i, b, pm, ev_m):
            dd = [ev_m]
            if i >= 2:
                dd.append(ev_uspill[i - 2])
            ev = sop(lambda e, i=i, b=b, pm=pm: e.activation(
                ustg_bf[i % 2][:, b * 512:(b + 1) * 512], pm[:, 0:512], Act.Copy), deps=dd)
            ev_uevac[(i, b)] = ev
            if b == TB - 1:
                ev_uspill[i] = dma("gpsimd", u_spill[i:i + 1, :, :].rearrange("a p f -> p a f"),
                                   ustg_bf[i % 2].unsqueeze(1),
                                   deps=[ev] + [ev_uevac[(i, bb)] for bb in range(TB)], semq=sdSp)
                dd2 = [ev]
                ev_uam.append(vop(lambda e, i=i: e.tensor_reduce(
                    acol[:, 32 + i:32 + i + 1], ustg_bf[i % 2][:, :], AX.X, Alu.max,
                    apply_absolute_value=True), deps=dd2))
            return ev

        ev_p2 = []
        ev_p2am = []
        ev_g2 = []
        _box = {}

        def emit_chain(c):
            p = c % 2
            d0 = [ev_sgb] + [gevac[c * TB + b] for b in range(TB)] + [ev_gam[c]]
            if debug and c == 0:
                d0.append(ev_dgmm)
            d0.append(ev_dqd)
            if len(ev_p2) >= 2:
                d0.append(ev_p2[-2])
            ec1 = sop(lambda e, c=c, p=p: e.activation(
                gq32_p[p][:, :], gslot_bf(c)[:, :], Act.Copy, scale=tc(TX_FG)), deps=d0)
            ec2 = sop(lambda e, p=p: e.activation(
                scf0_p[p][:, :], gq32_p[p][:, :], Act.Sigmoid, scale=tc(TX_SG)),
                deps=[ec1, ev_sgb, ev_trd_l])
            ec3 = vop(lambda e, p=p: e.tensor_scalar(
                scf0i_p[p][:, :], scf0_p[p][:, :], 65536.0, 65535.0, Alu.mult, Alu.min),
                deps=[ec2])
            ec5 = vop(lambda e, c=c, p=p: e.scalar_tensor_tensor(
                gslot_bf(c)[:, :], gq32_p[p][:, :], 1.0, scf0i_p[p][:, :], Alu.mult, Alu.mult),
                deps=[ec3])
            ev_p2.append(ec5)
            dd = [ec5] + ([ev_ga] if c == 0 else [])
            ev_p2am.append(vop(lambda e, c=c: e.tensor_reduce(
                acol[:, c:c + 1], gslot_bf(c)[:, :], AX.X, Alu.max,
                apply_absolute_value=True), deps=dd))

        def emit_ar4():
            ev_p2a = gop(lambda e: e.tensor_reduce(t1r[0:1, 2:3], acol[:, 0:ICH], AX.XYZWC, Alu.max),
                         deps=[ev_p2am[-1]] + ev_p2am + [ev_ga, evi1])
            ev_ag2, _ = allreduce_scalar(2, t1r[0:1, 2:3], tc(TX_AG2), [ev_p2a])
            e1 = vop(lambda e: e.tensor_scalar(tc(TX_T0), tc(TX_SG), 1.0 / 65536.0, None,
                                               Alu.mult, Alu.bypass), deps=[ev_ag2, ev_sgb])
            e2 = vop(lambda e: e.tensor_tensor(tc(TX_T1), tc(TX_AG2), tc(TX_T0), Alu.mult), deps=[e1])
            e3 = vop(lambda e: e.tensor_scalar(tc(TX_SG2), tc(TX_T1), 1.0 / QMAX, 1e-12,
                                               Alu.mult, Alu.max), deps=[e2])
            e4 = vop(lambda e: e.reciprocal(tc(TX_T1), tc(TX_SG2)), deps=[e3])
            _box["f2"] = vop(lambda e: e.tensor_tensor(tc(TX_F2), tc(TX_T0), tc(TX_T1), Alu.mult),
                             deps=[e4])

        def emit_g2(c):
            d0 = [_box["f2"], ev_p2[c], ev_p2am[c]]
            eg2 = sop(lambda e, c=c: e.activation(
                gslot_i16(c)[:, :], gslot_bf(c)[:, :], Act.Copy, scale=tc(TX_F2)),
                deps=d0)
            ev_g2.append(eg2)

        work = ([(emit_chain, c) for c in range(ICH)] + [(emit_ar4, None)] +
                [(emit_g2, c) for c in range(ICH)])

        def up_extra(i):
            budget = 3
            while work and budget > 0:
                fn, arg = work.pop(0)
                fn(arg) if arg is not None else fn()
                budget -= 1

        uevac, umm_last = big_mm(wu_all, ev_ag_u, up_evac, [ev_gmm_all], extra_emit=up_extra)
        while work:
            fn, arg = work.pop(0)
            fn(arg) if arg is not None else fn()
        ev_umm_all = umm_last[ICH - 1]
        ev_ua = gop(lambda e: e.tensor_reduce(t1r[0:1, 3:4], acol[:, 32:32 + ICH], AX.XYZWC, Alu.max),
                    deps=[ev_uam[-1]] + ev_uam + [ev_ga])


        ev_auc, evi3 = allreduce_scalar(3, t1r[0:1, 3:4], tc(TX_AU), [ev_ua])
        e1 = vop(lambda e: e.tensor_tensor(tc(TX_T0), tc(TX_AU), tc(TX_SX), Alu.mult), deps=[ev_auc])
        e2 = vop(lambda e: e.tensor_scalar(tc(TX_SU), tc(TX_T0), 1.0 / QMAX, 1e-12, Alu.mult, Alu.max),
                 deps=[e1])
        e3 = vop(lambda e: e.reciprocal(tc(TX_T1), tc(TX_SU)), deps=[e2])
        ev_fu = vop(lambda e: e.tensor_tensor(tc(TX_FU), tc(TX_SX), tc(TX_T1), Alu.mult), deps=[e3])

        ev_p3 = []
        ev_p3am = []
        sdsp_full = (sdSp, S.counts.get(sdSp.name, 0))
        for c in range(ICH):
            dd = [sdsp_full, ev_umm_all, ev_uam[-1]]
            if c >= 4:
                dd.append(ev_p3[c - 4])
            e_rd = dma("sync", rb4[c % 4].unsqueeze(1),
                       u_spill[c:c + 1, :, :].rearrange("a p f -> p a f"), deps=dd,
                       semq=[sdRBA, sdRBB, sdRBC, sdRBD][c % 4])
            d0 = [ev_fu, e_rd, ev_g2[c]]
            if len(ev_p3) >= 2:
                d0.append(ev_p3[-2])
            ep1 = sop(lambda e, c=c: e.activation(
                gq32_p[c % 2][:, :], rb4[c % 4][:, :], Act.Copy, scale=tc(TX_FU)),
                deps=d0)
            ep3 = vop(lambda e, c=c: e.tensor_tensor(
                gslot_bf(c)[:, :], gq32_p[c % 2][:, :], gslot_i16(c)[:, :], Alu.mult), deps=[ep1])
            ev_p3.append(ep3)
            dd = [ep3] + ([ev_ua] if c == 0 else [])
            ev_p3am.append(vop(lambda e, c=c: e.tensor_reduce(
                acol[:, 32 + c:32 + c + 1], gslot_bf(c)[:, :], AX.X, Alu.max,
                apply_absolute_value=True), deps=dd))
        ev_oa = gop(lambda e: e.tensor_reduce(t1r[0:1, 3:4], acol[:, 32:32 + ICH], AX.XYZWC, Alu.max),
                    deps=[ev_p3am[-1]] + ev_p3am + [ev_ua, evi3])
        if debug:
            ev_dam3 = dma("sync", dbg_amx[:, 32:56] if False else dbg_ar[:, 0:8], amx[:, 32:40], deps=[ev_oa])
            ev_dam4 = dma("sync", dbg_ar[:, 8:16], amx[:, 40:48], deps=[ev_oa])
            ev_dp3 = dma("sync", dbg_p3[:, 0:(ICH * T)], _flat2(A_z[:, 0:(ICH * T) // 2].bitcast(BF16)), deps=[ev_oa])
            ev_dam3 = emax(ev_dam3, ev_dp3)[0]
        ev_aoc, evi4 = allreduce_scalar(4, t1r[0:1, 3:4], tc(TX_AO), [ev_oa] + ([ev_dam3, ev_dam4] if debug else []))
        e1 = vop(lambda e: e.tensor_tensor(tc(TX_CO), tc(TX_SG2), tc(TX_SU), Alu.mult), deps=[ev_aoc])
        e2 = vop(lambda e: e.tensor_tensor(tc(TX_T0), tc(TX_AO), tc(TX_CO), Alu.mult), deps=[e1])
        e3 = vop(lambda e: e.tensor_scalar(tc(TX_SO), tc(TX_T0), 1.0 / QMAX, 1e-12, Alu.mult, Alu.max),
                 deps=[e2])
        e4 = vop(lambda e: e.reciprocal(tc(TX_T1), tc(TX_SO)), deps=[e3])
        ev_fo = vop(lambda e: e.tensor_tensor(tc(TX_FO), tc(TX_CO), tc(TX_T1), Alu.mult), deps=[e4])

        NTG = TT // 4
        bankB = [p_mm[0], p_mm[1], p_tr[0], p_tr[1]]
        ev_outd = []
        ev_wdld = {}
        ev_mm_hist = {}
        prev_oq = None
        prev_oq2 = None
        prev_evacs = None
        for tg in range(NTG):
            ev_dn_last = None
            for i in range(ICH):
                gidx = tg * ICH + i
                if gidx == 0:
                    ev_wdld[0] = dma(
                        "sync", wdstr_bf[0].rearrange("p (r f) -> p r f", r=NCORES),
                        wd_all.rearrange("r (i p f) -> p i r f", p=128, f=128)[:, 0:1, :, :].rearrange("p i r f -> p (i r) f"),
                        deps=[ev_ag_d, ev_umm_all, ev_xqf], semq=sdStA)
                if gidx + 1 < NTG * ICH:
                    i2 = (gidx + 1) % ICH
                    dd = [ev_ag_d]
                    if gidx >= 1:
                        dd.append(ev_wd_prev_mm)
                    ev_wdld[gidx + 1] = dma(
                        "sync" if (gidx + 1) % 2 == 0 else "scalar",
                        wdstr_bf[(gidx + 1) % 2].rearrange("p (r f) -> p r f", r=NCORES),
                        wd_all.rearrange("r (i p f) -> p i r f", p=128, f=128)[:, i2:i2 + 1, :, :].rearrange("p i r f -> p (i r) f"),
                        deps=dd, semq=(sdStA if (gidx + 1) % 2 == 0 else sdStC))
                d0 = [ev_fo, ev_p3[i]]
                if prev_oq2 is not None:
                    d0.append(prev_oq2)
                if gidx >= 2:
                    d0.append(ev_mm_hist[gidx - 2])
                eo1 = vop(lambda e, i=i, tg=tg, gidx=gidx: e.tensor_scalar(
                    oq32_p[gidx % 2][:, :], gslot_bf(i)[:, tg * 512:(tg + 1) * 512],
                    tc(TX_FO), None, Alu.mult, Alu.bypass), deps=d0 + [ev_p3am[i]])
                eo2 = sop(lambda e, gidx=gidx: e.activation(
                    oqbf_v[gidx % 2][:, :], oq32_p[gidx % 2][:, :], Act.Copy),
                    deps=[eo1])
                prev_oq2 = prev_oq
                prev_oq = eo2
                ev_m = None
                for jj in range(4):
                    for hh in range(SEG):
                        pbank = p_dn[jj] if hh == 0 else bankB[jj]
                        dd2 = emax(eo2, ev_wdld[gidx])
                        if i == 0 and prev_evacs is not None:
                            dd2 = emax(eo2, ev_wdld[gidx], prev_evacs)
                        ev_m = top(lambda e, jj=jj, hh=hh, i=i, gidx=gidx, pbank=pbank: e.matmul(
                            pbank[:, 0:512], oqbf_v[gidx % 2][:, jj * 128:(jj + 1) * 128],
                            wdstr_bf[gidx % 2][:, hh * 512:(hh + 1) * 512],
                            start=(i == 0), stop=(i == ICH - 1)), deps=dd2)
                ev_wd_prev_mm = ev_m
                ev_mm_hist[gidx] = ev_m
                ev_dn_last = ev_m
            last_evac = None
            prev_ea = None
            for jj in range(4):
                j = tg * 4 + jj
                dd = [ev_sp[j]]
                if len(ev_outd) >= 2:
                    dd.append(ev_outd[-2])
                e_rb = dma("scalar", x2rb[jj % 2][:, :],
                           x2_spill.rearrange("(n p) f -> p n f", p=128)[:, j:j + 1, :].rearrange("p a f -> p (a f)"),
                           deps=dd + [ev_umm_all, ev_allsp], semq=sdXR)
                evh = []
                for hh in range(SEG):
                    src = p_dn[jj] if hh == 0 else bankB[jj]
                    evh.append(sop(lambda e, jj=jj, hh=hh, src=src: e.activation(
                        outst_v[:, jj * D + hh * 512:jj * D + hh * 512 + 512],
                        src[:, 0:512], Act.Copy, scale=tc(TX_SO)), deps=[ev_dn_last, ev_fo]))
                sevs = []
                for s_ in range(SEG):
                    dds = [evh[-1]]
                    if prev_ea is not None:
                        dds.append(prev_ea)
                    sevs.append(vop(lambda e, jj=jj, s_=s_: e.bn_stats(
                        ln4s[:, s_ * 6:s_ * 6 + 6],
                        outst_v[:, jj * D + s_ * 512:jj * D + s_ * 512 + 512]), deps=dds))
                ea = vop(lambda e: e.bn_aggr(ln4m[:, 0:2], ln4s[:]),
                         deps=[sevs[-1]])
                eb = vop(lambda e: e.tensor_scalar(ln4m[:, 2:3], ln4m[:, 1:2], EPS, None,
                                                   Alu.add, Alu.bypass), deps=[ea])
                ec = sop(lambda e: e.activation(ln4m[:, 2:3], ln4m[:, 2:3], Act.Sqrt), deps=[eb])
                ed = vop(lambda e: e.reciprocal(ln4m[:, 2:3], ln4m[:, 2:3]), deps=[ec])
                ee = vop(lambda e: e.scalar_tensor_tensor(ln4m[:, 3:4], ln4m[:, 0:1], -1.0,
                                                          ln4m[:, 2:3], Alu.mult, Alu.mult),
                         deps=[ed])
                if debug and tg == 0 and jj == 0:
                    ev_do = dma("sync", dbg_o[:, 0:D], outst_v[:, 0:D], deps=[evh[-1]])
                else:
                    ev_do = None
                ef = vop(lambda e, jj=jj: e.tensor_scalar(
                    outst_v[:, jj * D:(jj + 1) * D], outst_v[:, jj * D:(jj + 1) * D],
                    ln4m[:, 2:3], ln4m[:, 3:4], Alu.mult, Alu.add),
                    deps=[ee, evh[-1]] + ([ev_do] if ev_do else []))
                eg = vop(lambda e, jj=jj: e.tensor_tensor(
                    outst_v[:, jj * D:(jj + 1) * D], outst_v[:, jj * D:(jj + 1) * D],
                    x2rb[jj % 2][:, :], Alu.add), deps=[ef, e_rb])
                ev_od = dma("gpsimd", out.rearrange("(n p) f -> p n f", p=128)[:, j:j + 1, :],
                            outst_v[:, jj * D:(jj + 1) * D].unsqueeze(1), deps=[eg], semq=sdOut)
                ev_outd.append(ev_od)
                last_evac = eg
                prev_ea = ea
            prev_evacs = last_evac

        if debug:
            dd = [ev_outd[-1], ev_fo, ev_xs, ev_bu]
            dma("sync", dbg_t[:], t1r[:], deps=dd)
            dma("sync", dbg_c[:], tiny[:], deps=dd)
            ev_dbg = dma("sync", dbg_g[:, 0:T // 2].bitcast(BF16), gslot_bf(0)[:, :].rearrange("p (a b) -> p a b", b=2) if False else gslot_bf(0)[:, :], deps=dd)
            S.wait_only("sync", ev_dbg)
        S.wait_only("sync", ev_outd[-1])

        with nc.Block() as block:
            @block.sync
            def _(e):
                for fn in S.ops["sync"]:
                    fn(e)

            @block.vector
            def _(e):
                for fn in S.ops["vector"]:
                    fn(e)

            @block.scalar
            def _(e):
                for fn in S.ops["scalar"]:
                    fn(e)

            @block.tensor
            def _(e):
                for fn in S.ops["tensor"]:
                    fn(e)

            @block.gpsimd
            def _(e):
                for fn in S.ops["gpsimd"]:
                    fn(e)

    return nc


_CACHE = {}


def kernel(L0_attn_u, L0_attn_Lambda, L0_attn_log_dt, L0_attn_B, L0_attn_C,
           L0_attn_D, L0_ffn_w_gate, L0_ffn_w_up, L0_ffn_w_down):
    u = np.ascontiguousarray(np.asarray(L0_attn_u, dtype=np.float32))
    Bsz, Sq, Dm = u.shape
    NI = int(L0_ffn_w_gate.shape[0])
    NS = int(L0_attn_Lambda.shape[0])
    T = (Bsz * Sq) // NCORES
    uf = u.reshape(Bsz * Sq, Dm)

    key = (T, Dm, NI, NS)
    if key not in _CACHE:
        _CACHE[key] = build_program(T, Dm, NI, NS)
    nc = _CACHE[key]

    lam = np.ascontiguousarray(np.asarray(L0_attn_Lambda, np.float32).reshape(NS, 1))
    ldt = np.ascontiguousarray(np.asarray(L0_attn_log_dt, np.float32).reshape(1, 1))
    Bm = np.ascontiguousarray(np.asarray(L0_attn_B, np.float32))
    Cm = np.ascontiguousarray(np.asarray(L0_attn_C, np.float32))
    Dv = np.ascontiguousarray(np.asarray(L0_attn_D, np.float32).reshape(1, Dm))
    wg = np.ascontiguousarray(np.asarray(L0_ffn_w_gate, np.float32))
    wu = np.ascontiguousarray(np.asarray(L0_ffn_w_up, np.float32))
    wd = np.ascontiguousarray(np.asarray(L0_ffn_w_down, np.float32))
    NI_SH = NI // NCORES
    D_SH = Dm // NCORES

    in_maps = []
    for c in range(NCORES):
        sel = np.zeros((NCORES, 1), np.float32)
        if c % 2 == 1:
            sel[c - 1, 0] = 1.0
        in_maps.append({
            "u": np.ascontiguousarray(uf[c * T:(c + 1) * T]),
            "lam": lam, "ldt": ldt, "Bm": Bm, "Cm": Cm, "Dv": Dv, "sel": sel,
            "wg": np.ascontiguousarray(wg[c * NI_SH:(c + 1) * NI_SH]),
            "wu": np.ascontiguousarray(wu[c * NI_SH:(c + 1) * NI_SH]),
            "wd": np.ascontiguousarray(wd[c * D_SH:(c + 1) * D_SH]),
        })
    res = run_bass_kernel_spmd(nc, in_maps, core_ids=list(range(NCORES)))
    full = np.concatenate([np.asarray(res.results[c]["out"]) for c in range(NCORES)], axis=0)
    return full.reshape(Bsz, Sq, Dm)
